# version 13
# baseline (speedup 1.0000x reference)
import numpy as np
from contextlib import ExitStack

import ml_dtypes
import concourse.bass as bass
import concourse.mybir as mybir
from concourse.bass_utils import run_bass_kernel_spmd

F32 = mybir.dt.float32
BF16 = mybir.dt.bfloat16
I32 = mybir.dt.int32
AF = mybir.ActivationFunctionType
ALU = mybir.AluOpType
AX = mybir.AxisListType

NCORES = 8
S = 2048
D = 1024
NT = S // 128
NC = D // 128


ENGS = ("tensor", "scalar", "vector", "gpsimd", "sync")
ENG_SEM = {"tensor": "pe", "scalar": "act", "vector": "dve", "gpsimd": "pool"}
QUEUE_SEM = {"sync": "dsp", "gpsimd": "dgp", "scalar": "dact"}
ALL_SEMS = ("pe", "act", "dve", "pool", "dsp", "dgp", "dact", "coll")


class Region:
    __slots__ = ("name", "w", "readers")

    def __init__(self, name):
        self.name = name
        self.w = None
        self.readers = []


class Prog:
    def __init__(self, nc, ctx):
        self.nc = nc
        self.ctx = ctx
        self.ops = {e: [] for e in ENGS}
        self.total = {s: 0 for s in ALL_SEMS}
        self.sems = {s: ctx.enter_context(nc.semaphore(s)) for s in ALL_SEMS}
        self._const_cols = {}
        self.consts = None
        self.NCONST = 24
        self.total["dve"] = self.NCONST
        self.consts_region = Region("consts")
        self.consts_region.w = ("vector", "dve", self.NCONST)

    def sb(self, name, shape, dtype):
        t = self.ctx.enter_context(self.nc.sbuf_tensor(name, shape, dtype))
        return t, Region(name)

    def psb(self, name, shape, dtype=F32):
        t = self.ctx.enter_context(self.nc.psum_tensor(name, shape, dtype))
        return t, Region(name)

    def op(self, eng, fn, reads=(), writes=(), sem=None, inc=None, tag=""):
        if sem is None:
            sem = ENG_SEM[eng]
        if inc is None:
            inc = 16 if sem in QUEUE_SEM.values() else 1
        waits = {}

        def need(dep):
            if dep is None:
                return
            dstream, dsem, dcnt = dep
            if dstream == eng == "tensor" and dsem == "pe":
                return
            waits[dsem] = max(waits.get(dsem, 0), dcnt)

        for r in reads:
            need(r.w)
        for r in writes:
            need(r.w)
            for rd in r.readers:
                need(rd)
        self.total[sem] += inc
        cnt = self.total[sem]
        rec = dict(fn=fn, waits=waits, sem=sem, inc=inc, cnt=cnt, tag=tag)
        self.ops[eng].append(rec)
        for r in reads:
            r.readers.append((eng, sem, cnt))
        for r in writes:
            r.w = (eng, sem, cnt)
            r.readers = []
        return rec

    def dma(self, queue, out_ap, in_ap, reads=(), writes=(), tag="", **kw):
        return self.op(queue, lambda e: e.dma_start(out_ap, in_ap, **kw),
                       reads=reads, writes=writes, sem=QUEUE_SEM[queue], tag=tag)

    def const_col(self, value):
        v = float(value)
        if v not in self._const_cols:
            assert self.consts is not None
            self._const_cols[v] = len(self._const_cols)
            assert len(self._const_cols) <= self.consts.shape[1]
        idx = self._const_cols[v]
        return self.consts[:, idx:idx + 1]

    def finalize(self):
        pre = []
        items = sorted(self._const_cols.items(), key=lambda kv: kv[1])
        assert len(items) <= self.NCONST
        for k in range(self.NCONST):
            if k < len(items):
                v, idx = items[k]
                fn = (lambda e, v=v, idx=idx: e.memset(self.consts[:, idx:idx + 1], v))
            else:
                fn = (lambda e: e.engine_nop())
            pre.append(dict(fn=fn, waits={}, sem="dve", inc=1, cnt=k + 1, tag="const"))
        self.ops["vector"] = pre + self.ops["vector"]

        consumer_waits = {q: set() for q in QUEUE_SEM.values()}
        for eng in ENGS:
            for rec in self.ops[eng]:
                for wsem, val in rec["waits"].items():
                    if wsem in consumer_waits and wsem != rec["sem"]:
                        consumer_waits[wsem].add(val)
        for eng in ENGS:
            qsem = QUEUE_SEM.get(eng)
            if qsem is None:
                continue
            needed = sorted(consumer_waits[qsem])
            level = 0
            for rec in self.ops[eng]:
                if rec["sem"] != qsem:
                    v = rec["waits"].get(qsem)
                    if v is not None:
                        level = max(level, v)
                    continue
                c = rec["cnt"]
                target = 0
                for V in needed:
                    if V <= c - rec["inc"]:
                        target = V
                    else:
                        break
                if target > level:
                    rec["waits"][qsem] = max(rec["waits"].get(qsem, 0), target)
                    level = target

        CSEMS = ("pe", "act", "dve", "pool")
        waited = {s: set() for s in CSEMS}
        for eng in ENGS:
            for rec in self.ops[eng]:
                for wsem, val in rec["waits"].items():
                    if wsem in waited:
                        waited[wsem].add(val)
        remap = {}
        for s in CSEMS:
            k = 0
            m = {}
            for eng in ENGS:
                for rec in self.ops[eng]:
                    if rec["sem"] != s:
                        continue
                    if rec["inc"] and rec["cnt"] in waited[s]:
                        k += rec["inc"]
                        m[rec["cnt"]] = k
                        rec["_ni"] = rec["inc"]
                    else:
                        rec["_ni"] = 0
                    rec["_nc"] = k
            remap[s] = m
        for eng in ENGS:
            for rec in self.ops[eng]:
                for wsem in list(rec["waits"]):
                    if wsem in remap:
                        rec["waits"][wsem] = remap[wsem][rec["waits"][wsem]]
                if rec["sem"] in remap:
                    rec["inc"] = rec["_ni"]
                    rec["cnt"] = rec["_nc"]

        def resolve(sem, val):
            return val

        block = self.ctx.enter_context(self.nc.Block())
        watermark = {}

        def make_body(eng):
            ops = self.ops[eng]

            def body(handle):
                for rec in ops:
                    for wsem in sorted(rec["waits"]):
                        v = resolve(wsem, rec["waits"][wsem])
                        if wsem == rec["sem"]:
                            v = min(v, rec["cnt"] - rec["inc"])
                        key = (eng, wsem)
                        if v <= 0 or watermark.get(key, 0) >= v:
                            continue
                        watermark[key] = v
                        handle.wait_ge(self.sems[wsem], v)
                    with self.nc.allow_low_precision(reason="bf16 staging/transposes; accumulation stays fp32"):
                        inst = rec["fn"](handle)
                    if inst is not None and rec["inc"]:
                        inst.then_inc(self.sems[rec["sem"]], rec["inc"])
            return body

        for eng in ENGS:
            if self.ops[eng]:
                getattr(block, eng)(make_body(eng))



class Arena:

    def __init__(self, P, cols_f32):
        self.P = P
        self.t32 = P.ctx.enter_context(P.nc.sbuf_tensor("arena", [128, cols_f32], F32))
        self.tbf = self.t32.bitcast(BF16)
        self.tu8 = self.t32.bitcast(mybir.dt.uint8)
        self.ti32 = self.t32.bitcast(I32)
        self.nbytes = cols_f32 * 4
        self.off = 0
        self.live = []
        self.epoch_deps = []

    def take(self, dtype, cols, name):
        esz = {F32: 4, BF16: 2, mybir.dt.uint8: 1, I32: 4}[dtype]
        off = (self.off + 3) // 4 * 4
        assert off + cols * esz <= self.nbytes, f"arena overflow at {name} ({off}+{cols*esz})"
        self.off = off + cols * esz
        t = {F32: self.t32, BF16: self.tbf, mybir.dt.uint8: self.tu8, I32: self.ti32}[dtype]
        ap = t[:, off // esz: off // esz + cols]
        r = Region(name)
        r.readers = list(self.epoch_deps)
        self.live.append(r)
        return ap, r

    def take_tiles(self, dtype, cols, n, name):
        ap, r = self.take(dtype, cols * n, name)
        regions = [Region(f"{name}{i}") for i in range(n)]
        for rr in regions:
            rr.readers = list(self.epoch_deps)
        self.live.extend(regions)
        self.live.remove(r)
        return ap, regions

    def mark(self):
        return (self.off, len(self.live))

    def rewind(self, m):
        off0, n0 = m
        deps = {d[1]: d for d in self.epoch_deps}
        for r in self.live[n0:]:
            for d in ([r.w] if r.w else []) + r.readers:
                if d[1] not in deps or deps[d[1]][2] < d[2]:
                    deps[d[1]] = d
        self.epoch_deps = list(deps.values())
        self.live = self.live[n0:][:0] + self.live[:n0]
        self.off = off0

    def reset(self):
        deps = {}
        for r in self.live:
            for d in ([r.w] if r.w else []) + r.readers:
                key = d[1]
                if key not in deps or deps[key][2] < d[2]:
                    deps[key] = d
        self.epoch_deps = list(deps.values())
        self.live = []
        self.off = 0


class Core:
    def __init__(self, P: Prog, hcb_ap, hcb_layout, arena_cols=34176):
        self.P = P
        self.X, _ = P.sb("X", [128, NT * 1024], F32)
        self.Xr = [Region(f"X{i}") for i in range(NT)]
        self.stat, self.statr = P.sb("stat", [128, 48], F32)
        self.lnstat, self.lnstatr = P.sb("lnstat", [128, 2 * NT], F32)
        self.pb = []
        self.pbr = []
        for i in range(7):
            t, r = P.psb(f"pb{i}", [128, 512], F32)
            self.pb.append(t)
            self.pbr.append(r)
        self.psbf, self.psbfr = P.psb("pbf", [128, 1024], BF16)
        self.hcb, self.hcbr = P.sb("hcb_sb", [128, hcb_layout["_total"]], BF16)
        self.hcb_layout = hcb_layout
        P.dma("sync", self.hcb[:, :], hcb_ap, writes=[self.hcbr], tag="hcb")
        self.ident = self.hcb[:, hcb_layout["ident"]:hcb_layout["ident"] + 128]
        self.ones_bf = self.hcb[:, hcb_layout["ones"]:hcb_layout["ones"] + 512]
        self.onecol_bf = self.hcb[:, hcb_layout["onecol"]:hcb_layout["onecol"] + 1]
        P.consts, _ = P.sb("consts", [128, 24], F32)
        self.arena = Arena(P, arena_cols)
        self.ln_fm = None
        self.ln_fmr = None

    def hc(self, name, width):
        o = self.hcb_layout[name]
        return self.hcb[:, o:o + width]


def emit_load_x(C: Core, x_ap):
    src = x_ap.rearrange("(t p) d -> p t d", p=128)
    C.P.dma("sync", C.X.rearrange("p (t d) -> p t d", t=NT), src,
            writes=C.Xr, tag="loadx")


def emit_store_x(C: Core, out_ap):
    dst = out_ap.rearrange("(t p) d -> p t d", p=128)
    C.P.dma("gpsimd", dst, C.X.rearrange("p (t d) -> p t d", t=NT),
            reads=C.Xr, tag="storex")


def emit_final_wait(P: Prog):
    tot = P.total["dgp"]
    P.op("gpsimd", lambda e: None, tag="final")

    def fn(e):
        e.wait_ge(P.sems["dgp"], tot)
        return None
    P.ops["gpsimd"].append(dict(fn=fn, waits={}, sem="pool", inc=0, cnt=P.total["pool"], tag="finalwait"))


def emit_layernorm(C: Core, fm=True):
    P = C.P
    if fm:
        C.ln_fm, C.ln_fmr = C.arena.take_tiles(BF16, S, NC, "lnfm")
        stage, stager = C.arena.take(BF16, 2 * 1024, "lnstage")
    sq, sqr = C.arena.take(F32, 1024, "lnsq")
    inv_d = P.const_col(1.0 / D)
    eps = P.const_col(1e-5)
    neg_half = P.const_col(-0.5)
    st = C.stat
    for i in range(NT):
        x = C.X[:, i * 1024:(i + 1) * 1024]
        P.op("vector", lambda e, x=x: e.reduce_sum(st[:, 0:1], x, axis=AX.X),
             reads=[C.Xr[i]], writes=[C.statr], tag=f"ln_sum{i}")
        P.op("scalar", lambda e, x=x: e.activation(
            sq[:, :], x, AF.Square, accum_out=st[:, 1:2]),
            reads=[C.Xr[i]], writes=[sqr, C.statr], tag=f"ln_sq{i}")
        P.op("vector", lambda e: e.tensor_scalar(st[:, 2:3], st[:, 0:1], inv_d, None, ALU.mult),
             reads=[C.statr, P.consts_region], writes=[C.statr], tag="ln_m")
        P.op("vector", lambda e: e.tensor_scalar(st[:, 3:4], st[:, 1:2], inv_d, None, ALU.mult),
             reads=[C.statr], writes=[C.statr], tag="ln_ex2")
        P.op("vector", lambda e: e.tensor_tensor(st[:, 4:5], st[:, 2:3], st[:, 2:3], ALU.mult),
             reads=[C.statr], writes=[C.statr], tag="ln_m2")
        P.op("vector", lambda e: e.tensor_tensor(st[:, 5:6], st[:, 3:4], st[:, 4:5], ALU.subtract),
             reads=[C.statr], writes=[C.statr], tag="ln_var")
        P.op("scalar", lambda e: e.activation(st[:, 6:7], st[:, 5:6], AF.Ln, bias=eps),
             reads=[C.statr, P.consts_region], writes=[C.statr], tag="ln_ln")
        P.op("scalar", lambda e: e.activation(st[:, 7:8], st[:, 6:7], AF.Exp, scale=neg_half),
             reads=[C.statr, P.consts_region], writes=[C.statr], tag="ln_rstd")
        P.op("vector", lambda e, i=i: e.tensor_copy(C.lnstat[:, 2 * i:2 * i + 1], st[:, 2:3]),
             reads=[C.statr], writes=[C.lnstatr], tag=f"ln_sv{i}")
        P.op("vector", lambda e, i=i: e.tensor_copy(C.lnstat[:, 2 * i + 1:2 * i + 2], st[:, 7:8]),
             reads=[C.statr], writes=[C.lnstatr], tag=f"ln_sv2{i}")
        if not fm:
            continue
        dst = stage[:, (i % 2) * 1024:(i % 2) * 1024 + 1024]
        P.op("vector", lambda e, x=x, dst=dst: e.tensor_scalar(
            dst, x, st[:, 2:3], st[:, 7:8], ALU.subtract, ALU.mult),
            reads=[C.Xr[i], C.statr], writes=[stager], tag=f"ln_bf{i}")
        for c in range(NC):
            s_ap = dst[:, c * 128:(c + 1) * 128]
            p_ap = C.psbf[:, (c % 8) * 128:(c % 8) * 128 + 128]
            d_ap = C.ln_fm[:, c * S + i * 128: c * S + (i + 1) * 128]
            P.op("tensor", lambda e, s_ap=s_ap, p_ap=p_ap: e.transpose(p_ap, s_ap, C.ident),
                 reads=[stager, C.hcbr], writes=[C.psbfr], tag=f"lntr{i}_{c}")
            if c % 2 == 0:
                P.op("vector", lambda e, p=p_ap, d=d_ap: e.tensor_copy(d, p),
                     reads=[C.psbfr], writes=[C.ln_fmr[c]], tag=f"lnte{i}_{c}")
            else:
                P.op("scalar", lambda e, p=p_ap, d=d_ap: e.copy(d, p),
                     reads=[C.psbfr], writes=[C.ln_fmr[c]], tag=f"lnte{i}_{c}")


def emit_recip_nr(P, dst, src, tmp, reg, tag, iters=2):
    P.op("vector", lambda e: e.reciprocal(dst, src), reads=[reg], writes=[reg], tag=tag + "r0")
    for it in range(iters):
        P.op("vector", lambda e: e.tensor_tensor(tmp, src, dst, ALU.mult),
             reads=[reg], writes=[reg], tag=f"{tag}n{it}a")
        P.op("vector", lambda e: e.tensor_scalar(tmp, tmp, 2.0, -1.0, ALU.subtract, ALU.mult),
             reads=[reg], writes=[reg], tag=f"{tag}n{it}b")
        P.op("vector", lambda e: e.tensor_tensor(dst, dst, tmp, ALU.mult),
             reads=[reg], writes=[reg], tag=f"{tag}n{it}c")


def emit_ln32_tile(C: Core, i, dst_ap, dst_region):
    P = C.P
    x = C.X[:, i * 1024:(i + 1) * 1024]
    P.op("vector", lambda e: e.tensor_scalar(
        dst_ap, x, C.lnstat[:, 2 * i:2 * i + 1], C.lnstat[:, 2 * i + 1:2 * i + 2],
        ALU.subtract, ALU.mult),
        reads=[C.Xr[i], C.lnstatr], writes=[dst_region], tag=f"ln32_{i}")



def emit_l0_attn(C: Core, W):
    P = C.P
    A = C.arena
    H, HS = 16, 64
    SHIFT = 30.0

    emit_layernorm(C)

    wp = []
    wpr = []
    for d in range(2):
        t, r = A.take(BF16, 3 * 1024 + 128 * 3, f"l0wp{d}")
        wp.append(t)
        wpr.append(r)
    qa, qar, ka, kar = [], [], [], []
    for j in range(2):
        t, r = A.take(BF16, S, f"l0qa{j}")
        qa.append(t); qar.append(r)
        t, r = A.take(BF16, S, f"l0ka{j}")
        ka.append(t); kar.append(r)
    v_tm, v_tmr = A.take(BF16, NT * 130, "l0vtm")
    pt, ptr = [], []
    for j in range(2):
        t, r = A.take(BF16, 512, f"l0pt{j}")
        pt.append(t); ptr.append(r)
    att_bf, att_bfr = A.take_tiles(BF16, S, NC, "l0att")
    rden_bf, rden_bfr = A.take(BF16, 512, "l0rden")
    bc_sb, bc_sbr = A.take(BF16, 512, "l0bc")
    wm, wmr = A.take(BF16, 4096, "l0wm")
    bm_bc, bm_bcr = A.take(F32, 1024, "l0bmbc")
    res_t, res_tr = A.take(F32, 512, "l0res")

    P.dma("sync", bm_bc[:, :], W["L0_attn_bm"][None].broadcast_to([128, 1024]),
          writes=[bm_bcr], tag="l0_bmbc")

    def load_pair(hp, d):
        h0 = 2 * hp
        for mi, (mat, bias) in enumerate((("Wq", "bq"), ("Wk", "bk"), ("Wv", "bv"))):
            src = W[f"L0_attn_{mat}"][:, h0 * HS:(h0 + 2) * HS].rearrange(
                "(c p) n -> p c n", p=128)
            dst = wp[d][:, mi * 1024:(mi + 1) * 1024].rearrange("p (c n) -> p c n", c=NC)
            P.dma("gpsimd", dst, src, writes=[wpr[d]], tag=f"l0wp{hp}_{mat}")
            P.dma("gpsimd", wp[d][0:1, 3072 + mi * 128: 3072 + (mi + 1) * 128],
                  W[f"L0_attn_{bias}"][None][:, h0 * HS:(h0 + 2) * HS],
                  writes=[wpr[d]], tag=f"l0bp{hp}_{bias}")

    load_pair(0, 0)
    maskv = {off: C.hc(f"l0m{off}", 512) for off in (-384, -256, -128, 0)}
    sc8 = P.const_col(0.125)

    for hp in range(H // 2):
        d = hp % 2
        if hp + 1 < H // 2:
            load_pair(hp + 1, 1 - d)
        h0 = 2 * hp
        wq_ap = lambda c, d=d: wp[d][:, 0 * 1024 + c * 128: 0 * 1024 + (c + 1) * 128]
        wk_ap = lambda c, d=d: wp[d][:, 1 * 1024 + c * 128: 1 * 1024 + (c + 1) * 128]
        wv_ap = lambda c, d=d: wp[d][:, 2 * 1024 + c * 128: 2 * 1024 + (c + 1) * 128]
        b_ap = lambda mi, d=d: wp[d][0:1, 3072 + mi * 128: 3072 + (mi + 1) * 128]

        for (w_ap, bi, dsts, dstrs, qscale) in ((wq_ap, 0, qa, qar, True),
                                                (wk_ap, 1, ka, kar, False)):
            for n0 in range(0, S, 512):
                pb, pbr = C.pb[0], C.pbr[0]

                def grp(e, w_ap=w_ap, bi=bi, n0=n0, pb=pb, b_ap=b_ap):
                    for c in range(NC):
                        e.matmul(pb[:, :], w_ap(c), C.ln_fm[:, c * S + n0: c * S + n0 + 512],
                                 start=(c == 0), stop=False)
                    return e.matmul(pb[:, :], b_ap(bi), C.ones_bf[0:1, :],
                                    start=False, stop=True)
                P.op("tensor", grp, reads=[wpr[d], C.hcbr] + C.ln_fmr,
                     writes=[pbr], tag=f"l0p{hp}")
                for j in range(2):
                    rows = slice(j * 64, j * 64 + 64)
                    if qscale:
                        P.op("scalar", lambda e, pb=pb, j=j, n0=n0, rows=rows, dsts=dsts: e.activation(
                            dsts[j][0:64, n0:n0 + 512], pb[rows, :], AF.Copy, scale=sc8[0:64, :]),
                            reads=[pbr, P.consts_region], writes=[dstrs[j]], tag=f"l0pe{hp}")
                    else:
                        P.op("scalar", lambda e, pb=pb, j=j, n0=n0, rows=rows, dsts=dsts: e.copy(
                            dsts[j][0:64, n0:n0 + 512], pb[rows, :]),
                            reads=[pbr], writes=[dstrs[j]], tag=f"l0pe{hp}")
        for j in range(2):
            P.op("vector", lambda e, j=j: e.memset(qa[j][64:65, :], -SHIFT),
                 writes=[qar[j]], tag="qrow")
            P.op("vector", lambda e, j=j: e.memset(ka[j][64:65, :], 1.0),
                 writes=[kar[j]], tag="krow")

        for i in range(NT):
            pb, pbr = C.pb[1], C.pbr[1]

            def grpv(e, i=i, pb=pb, wv_ap=wv_ap, b_ap=b_ap):
                for c in range(NC):
                    e.matmul(pb[:, 0:128], C.ln_fm[:, c * S + i * 128: c * S + (i + 1) * 128],
                             wv_ap(c), start=(c == 0), stop=False)
                return e.matmul(pb[:, 0:128], C.ones_bf[0:1, 0:128], b_ap(2),
                                start=False, stop=True)
            P.op("tensor", grpv, reads=[wpr[d], C.hcbr] + C.ln_fmr,
                 writes=[pbr], tag=f"l0v{hp}_{i}")
            for j in range(2):
                P.op("scalar", lambda e, pb=pb, i=i, j=j: e.copy(
                    v_tm[:, i * 130 + j * 65: i * 130 + j * 65 + 64], pb[:, j * 64:(j + 1) * 64]),
                    reads=[pbr], writes=[v_tmr], tag=f"l0ve{hp}_{i}_{j}")
        if hp == 0:
            P.op("vector", lambda e: e.memset(
                v_tm.rearrange("p (t j c) -> p t j c", t=NT, j=2)[:, :, :, 64:65], 1.0),
                writes=[v_tmr], tag="vones")

        for j in range(2):
            h = h0 + j
            for n0 in range(0, S, 512):
                po, por = C.pb[2 + j], C.pbr[2 + j]
                n_tk = n0 // 128 + 4
                for tk in range(n_tk):
                    sbank = 4 + (tk % 2)
                    ps_, psr_ = C.pb[sbank], C.pbr[sbank]
                    P.op("tensor", lambda e, tk=tk, n0=n0, j=j, ps_=ps_: e.matmul(
                        ps_[:, :], ka[j][0:65, tk * 128:(tk + 1) * 128], qa[j][0:65, n0:n0 + 512],
                        start=True, stop=True),
                        reads=[kar[j], qar[j]], writes=[psr_], tag=f"l0s{h}_{n0}_{tk}")
                    ptj, ptjr = pt[tk % 2], ptr[tk % 2]
                    P.op("scalar", lambda e, ps_=ps_, ptj=ptj: e.activation(
                        ptj[:, :], ps_[:, :], AF.Exp),
                        reads=[psr_], writes=[ptjr], tag=f"l0e{h}_{n0}_{tk}")
                    off = n0 - tk * 128
                    if off in maskv:
                        P.op("gpsimd", lambda e, ptj=ptj, off=off: e.tensor_tensor(
                            ptj[:, :], ptj[:, :], maskv[off], ALU.mult),
                            reads=[ptjr, C.hcbr], writes=[ptjr], tag=f"l0m{h}_{n0}_{tk}")
                    P.op("tensor", lambda e, tk=tk, po=po, ptj=ptj, n_tk=n_tk, j=j: e.matmul(
                        po[0:65, :], v_tm[:, tk * 130 + j * 65: tk * 130 + j * 65 + 65],
                        ptj[:, :], start=(tk == 0), stop=(tk == n_tk - 1)),
                        reads=[v_tmr, ptjr], writes=[por], tag=f"l0a{h}_{n0}_{tk}")
                P.op("vector", lambda e, po=po: e.reciprocal(
                    rden_bf[0:1, :], po[64:65, :]),
                    reads=[por], writes=[rden_bfr], tag=f"l0r{h}_{n0}")
                pb2, pb2r = C.pb[1], C.pbr[1]
                P.op("tensor", lambda e, pb2=pb2: e.matmul(
                    pb2[0:64, :], C.ones_bf[0:1, 0:64], rden_bf[0:1, :],
                    start=True, stop=True),
                    reads=[rden_bfr, C.hcbr], writes=[pb2r], tag=f"l0b{h}_{n0}")
                P.op("scalar", lambda e, pb2=pb2: e.copy(bc_sb[0:64, :], pb2[0:64, :]),
                     reads=[pb2r], writes=[bc_sbr], tag=f"l0bc{h}_{n0}")
                c = h // 2
                rows = slice((h % 2) * 64, (h % 2) * 64 + 64)
                P.op("vector", lambda e, po=po, c=c, rows=rows, n0=n0: e.tensor_tensor(
                    att_bf[rows, c * S + n0: c * S + n0 + 512], po[0:64, :], bc_sb[0:64, :], ALU.mult),
                    reads=[por, bc_sbr], writes=[att_bfr[c]], tag=f"l0n{h}_{n0}")

    for n0 in (0, 512):
        src = W["L0_attn_Wm"][:, n0:n0 + 512].rearrange("(c p) n -> p c n", p=128)
        P.dma("gpsimd", wm.rearrange("p (c n) -> p c n", c=NC), src,
              writes=[wmr], tag=f"l0wm{n0}")
        for i in range(NT):
            pb, pbr = C.pb[5], C.pbr[5]

            def grpo(e, i=i, pb=pb):
                last = None
                for c in range(NC):
                    last = e.matmul(pb[:, :], att_bf[:, c * S + i * 128: c * S + (i + 1) * 128],
                                    wm[:, c * 512:(c + 1) * 512],
                                    start=(c == 0), stop=(c == NC - 1))
                return last
            P.op("tensor", grpo, reads=[wmr] + att_bfr, writes=[pbr], tag=f"l0o{i}_{n0}")
            x_ap = C.X[:, i * 1024 + n0: i * 1024 + n0 + 512]
            P.op("vector", lambda e, pb=pb, n0=n0: e.tensor_tensor(
                res_t[:, :], pb[:, :], bm_bc[:, n0:n0 + 512], ALU.add),
                reads=[pbr, bm_bcr], writes=[res_tr], tag=f"l0x1_{i}_{n0}")
            P.op("vector", lambda e, x_ap=x_ap: e.tensor_tensor(
                x_ap, x_ap, res_t[:, :], ALU.add),
                reads=[res_tr, C.Xr[i]], writes=[C.Xr[i]], tag=f"l0x2_{i}_{n0}")
    C.arena.reset()



def emit_l2_ffn(C: Core, W):
    P = C.P
    A = C.arena
    nc = P.nc
    emit_layernorm(C)
    U8 = mybir.dt.uint8
    w0d = nc.dram_tensor("l2f_w0", [8, 128, 4096], BF16).ap()
    w1d = nc.dram_tensor("l2f_w1", [32, 128, 4096], BF16).ap()
    w2d = nc.dram_tensor("l2f_w2", [32, 128, 1024], BF16).ap()
    w0dr, w1dr, w2dr = Region("l2fw0d"), Region("l2fw1d"), Region("l2fw2d")

    h1, h1r = A.take_tiles(BF16, 512, 32, "l2f_h1")
    res_ap = A.t32[:, (A.off - 32 * 512 * 2) // 4:(A.off - 32 * 512 * 2) // 4 + 512]
    h2, h2r = A.take_tiles(BF16, 512, 32, "l2f_h2")
    wst, wstr = A.take(BF16, 16 * 512, "l2f_wst")
    wst_off_bytes = (A.off - 16 * 512 * 2)
    u8a = A.tu8[:, wst_off_bytes:wst_off_bytes + 4096]
    u8b = A.tu8[:, wst_off_bytes + 4096:wst_off_bytes + 8192]
    cvt = A.tbf[:, (wst_off_bytes + 8192) // 2:(wst_off_bytes + 8192) // 2 + 4096]
    u8ar = u8br = cvtr = wstr
    scc, sccr = A.take(F32, 8, "l2f_scc")
    h1_byte0 = None

    for j, nm in enumerate(("s0", "s1", "s2")):
        P.dma("sync", scc[:, j:j + 1], W[f"L2_ffn_{nm}"][None].broadcast_to([128, 1]),
              writes=[sccr], tag=f"l2f_{nm}")
    P.op("vector", lambda e: e.tensor_tensor(scc[:, 4:7], scc[:, 0:3], scc[:, 0:3], ALU.add),
         reads=[sccr], writes=[sccr], tag="l2f_2s")

    k0 = W["L2_ffn_k0"]
    for c in range(8):
        P.dma("sync", u8a[:, :], k0[c * 128:(c + 1) * 128, :], writes=[u8ar], tag=f"l2fk0a{c}")
        P.dma("sync", u8b[:, :], k0[1024 + c * 128: 1024 + (c + 1) * 128, :], writes=[u8br], tag=f"l2fk0b{c}")
        P.op("vector", lambda e: e.tensor_tensor(cvt[:, :], u8a[:, :], u8b[:, :], ALU.subtract),
             reads=[u8ar, u8br], writes=[cvtr], tag=f"l2fc0{c}")
        P.dma("gpsimd", w0d[c], cvt[:, :], reads=[cvtr], writes=[w0dr], tag=f"l2fs0{c}")
    for c in range(32):
        P.dma("sync", u8a[:, :], W["L2_ffn_k1"][c * 128:(c + 1) * 128, :], writes=[u8ar], tag=f"l2fk1{c}")
        P.op("gpsimd", lambda e: e.tensor_scalar(cvt[:, :], u8a[:, :], 2.0, -1.0, ALU.mult, ALU.add),
             reads=[u8ar], writes=[cvtr], tag=f"l2fc1{c}")
        P.dma("gpsimd", w1d[c], cvt[:, :], reads=[cvtr], writes=[w1dr], tag=f"l2fs1{c}")
    for c in range(32):
        P.dma("sync", u8a[:, 0:1024], W["L2_ffn_k2"][c * 128:(c + 1) * 128, :], writes=[u8ar], tag=f"l2fk2{c}")
        P.op("gpsimd", lambda e: e.tensor_scalar(cvt[:, 0:1024], u8a[:, 0:1024], 2.0, -1.0, ALU.mult, ALU.add),
             reads=[u8ar], writes=[cvtr], tag=f"l2fc2{c}")
        P.dma("gpsimd", w2d[c], cvt[:, 0:1024], reads=[cvtr], writes=[w2dr], tag=f"l2fs2{c}")

    import os
    dbg = None
    if os.environ.get("L2F_DEBUG"):
        dbg = nc.dram_tensor("l2f_dbg", [2, 128, 32 * 512], F32, kind="ExternalOutput").ap()
        dbgr = Region("l2fdbg")
    for tq in range(4):
        t0 = tq * 512
        for jg in range(8):
            P.dma("sync", wst.rearrange("p (c n) -> p c n", n=512)[:, 0:8, :],
                  w0d[:, :, jg * 512:(jg + 1) * 512].transpose([1, 0, 2]), reads=[w0dr], writes=[wstr], tag=f"l2fw0{tq}_{jg}")
            for jj in range(4):
                pb, pbr = C.pb[jj], C.pbr[jj]

                def g1(e, jj=jj, pb=pb, t0=t0):
                    last = None
                    for c in range(8):
                        last = e.matmul(pb[:, :], wst[:, c * 512 + jj * 128: c * 512 + (jj + 1) * 128],
                                        C.ln_fm[:, c * S + t0: c * S + t0 + 512],
                                        start=(c == 0), stop=(c == 7))
                    return last
                P.op("tensor", g1, reads=[wstr] + C.ln_fmr, writes=[pbr], tag=f"l2f1_{tq}_{jg}_{jj}")
                j = jg * 4 + jj
                P.op("scalar", lambda e, pb=pb, j=j: e.activation(
                    h1[:, j * 512:(j + 1) * 512], pb[:, :], AF.Relu, scale=scc[:, 4:5]),
                    reads=[pbr, sccr], writes=[h1r[j]], tag=f"l2fe1_{tq}_{j}")
        if dbg is not None and tq == 0:
            P.dma("gpsimd", dbg[0].rearrange("p (c n) -> p c n", n=512),
                  h1.rearrange("p (c n) -> p c n", n=512), reads=h1r, writes=[dbgr], tag="dbg_h1")
        for jg in range(8):
            for ch in range(2):
                P.dma("sync", wst.rearrange("p (c n) -> p c n", c=16)[:, 0:16, 0:512],
                      w1d[ch * 16:(ch + 1) * 16, :, jg * 512:(jg + 1) * 512].transpose([1, 0, 2]),
                      reads=[w1dr], writes=[wstr], tag=f"l2fw1{tq}_{jg}_{ch}")
                for jj in range(4):
                    pb, pbr = C.pb[jj], C.pbr[jj]

                    def g2(e, jj=jj, pb=pb, ch=ch):
                        last = None
                        for c in range(16):
                            last = e.matmul(pb[:, :], wst[:, c * 512 + jj * 128: c * 512 + (jj + 1) * 128],
                                            h1[:, (ch * 16 + c) * 512:(ch * 16 + c + 1) * 512],
                                            start=(ch == 0 and c == 0), stop=(ch == 1 and c == 15))
                        return last
                    P.op("tensor", g2, reads=[wstr] + h1r, writes=[pbr], tag=f"l2f2_{tq}_{jg}_{jj}_{ch}")
                    if ch == 1:
                        j = jg * 4 + jj
                        P.op("scalar", lambda e, pb=pb, j=j: e.activation(
                            h2[:, j * 512:(j + 1) * 512], pb[:, :], AF.Relu, scale=scc[:, 5:6]),
                            reads=[pbr, sccr], writes=[h2r[j]], tag=f"l2fe2_{tq}_{j}")
        if dbg is not None and tq == 0:
            P.dma("gpsimd", dbg[1].rearrange("p (c n) -> p c n", n=512),
                  h2.rearrange("p (c n) -> p c n", n=512), reads=h2r, writes=[dbgr], tag="dbg_h2")
        for hh in range(2):
            for ch in range(2):
                P.dma("sync", wst.rearrange("p (c n) -> p c n", c=16)[:, 0:16, 0:512],
                      w2d[ch * 16:(ch + 1) * 16, :, hh * 512:(hh + 1) * 512].transpose([1, 0, 2]),
                      reads=[w2dr], writes=[wstr], tag=f"l2fw2{tq}_{hh}_{ch}")
                for ti in range(4):
                    i = tq * 4 + ti
                    pb, pbr = C.pb[2 + ti], C.pbr[2 + ti]

                    def g3(e, ti=ti, pb=pb, ch=ch):
                        last = None
                        for c in range(16):
                            last = e.matmul(pb[:, :], h2[:, (ch * 16 + c) * 512 + ti * 128: (ch * 16 + c) * 512 + (ti + 1) * 128],
                                            wst[:, c * 512:(c + 1) * 512],
                                            start=(ch == 0 and c == 0), stop=(ch == 1 and c == 15))
                        return last
                    P.op("tensor", g3, reads=[wstr] + h2r, writes=[pbr], tag=f"l2f3_{tq}_{hh}_{ti}_{ch}")
                    if ch == 1:
                        P.op("scalar", lambda e, pb=pb: e.activation(
                            res_ap, pb[:, :], AF.Copy, scale=scc[:, 6:7]),
                            reads=[pbr, sccr], writes=[h1r[0]], tag=f"l2fe3_{tq}_{hh}_{ti}")
                        x_ap = C.X[:, i * 1024 + hh * 512: i * 1024 + hh * 512 + 512]
                        P.op("vector", lambda e, x_ap=x_ap: e.tensor_tensor(
                            x_ap, x_ap, res_ap, ALU.add),
                            reads=[h1r[0], C.Xr[i]], writes=[C.Xr[i]], tag=f"l2fx_{tq}_{hh}_{ti}")
    C.arena.reset()



def emit_l3_attn(C: Core, W):
    P = C.P
    A = C.arena
    H, HS = 16, 64
    emit_layernorm(C)

    wp, wpr = [], []
    for d in range(2):
        t, r = A.take(BF16, 3 * 1024, f"l3a_wp{d}")
        wp.append(t); wpr.append(r)
    fq, fqr = A.take(BF16, S, "l3a_q")
    fk, fkr = A.take(BF16, S, "l3a_k")
    fktm, fktmr = A.take(BF16, NT * 128, "l3a_ktm")
    vtm, vtmr = A.take(BF16, NT * 128, "l3a_vtm")
    KC_off = A.off
    KC, KCr = A.take(F32, S, "l3a_KC")
    qk_off = A.off
    qk, qkr = A.take(F32, S, "l3a_qk")
    zrow, zrowr = qk[64:66, :] if False else None, None
    zbc, zbcr = A.take(BF16, S, "l3a_zbc")
    zrow, zrowr = qk[64:65, 0:S], qkr
    fs1, fs1r = A.take(F32, 512, "l3a_fs1")
    fs2, fs2r = A.take(F32, 512, "l3a_fs2")
    ptb2 = [A.take(BF16, 128, f"l3a_ptb{d}") for d in range(2)]
    fs3, fs3r = fs2, fs2r
    Ssb, Ssbr = A.take(BF16, 128, "l3a_S")
    Ssb32, Ssb32r = A.take(F32, 64, "l3a_S32")
    att, attr = A.take_tiles(BF16, S, NC, "l3a_att")
    wo, wor = A.tbf[:, KC_off // 2: KC_off // 2 + 4096], KCr
    bo_bc, bo_bcr = A.t32[:, qk_off // 4: qk_off // 4 + 1024], qkr
    res, resr = fs1, fs1r

    def load_pair(hp, d):
        h0 = 2 * hp
        for mi, mat in enumerate(("Wq", "Wk", "Wv")):
            src = W[f"L3_attn_{mat}"][:, h0 * HS:(h0 + 2) * HS].rearrange(
                "(c p) n -> p c n", p=128)
            dst = wp[d][:, mi * 1024:(mi + 1) * 1024].rearrange("p (c n) -> p c n", c=NC)
            P.dma("gpsimd", dst, src, writes=[wpr[d]], tag=f"l3awp{hp}_{mat}")

    load_pair(0, 0)
    eps_c = P.const_col(1e-6)
    triu = C.hc("l0m0", 512)[:, 0:128]

    P.op("vector", lambda e: e.memset(KC[0:64, :], 0.0), writes=[KCr], tag="l3a_kc0")

    for hp in range(H // 2):
        d = hp % 2
        if hp + 1 < H // 2:
            load_pair(hp + 1, 1 - d)
        h0 = 2 * hp

        for (mi, dst, dstr_) in ((0, fq, fqr), (1, fk, fkr)):
            for n0 in range(0, S, 512):
                pb, pbr = C.pb[0], C.pbr[0]

                def grp(e, mi=mi, n0=n0, pb=pb, d=d):
                    last = None
                    for c in range(NC):
                        last = e.matmul(pb[:, :], wp[d][:, mi * 1024 + c * 128:mi * 1024 + (c + 1) * 128],
                                        C.ln_fm[:, c * S + n0: c * S + n0 + 512],
                                        start=(c == 0), stop=(c == NC - 1))
                    return last
                P.op("tensor", grp, reads=[wpr[d]] + C.ln_fmr, writes=[pbr], tag=f"l3ap{hp}{mi}")
                P.op("vector", lambda e, pb=pb: e.tensor_scalar_max(fs1[:, :], pb[:, :], 0.0),
                     reads=[pbr], writes=[fs1r], tag="l3af1")
                P.op("vector", lambda e, pb=pb: e.tensor_scalar_min(fs2[:, :], pb[:, :], 0.0),
                     reads=[pbr], writes=[fs2r], tag="l3af2")
                P.op("scalar", lambda e: e.activation(fs2[:, :], fs2[:, :], AF.Exp),
                     reads=[fs2r], writes=[fs2r], tag="l3af3")
                P.op("vector", lambda e, dst=dst, n0=n0: e.tensor_tensor(
                    dst[:, n0:n0 + 512], fs1[:, :], fs2[:, :], ALU.add),
                    reads=[fs1r, fs2r], writes=[dstr_], tag="l3af4")

        for i in range(NT):
            pb, pbr = C.pb[1], C.pbr[1]

            def grpv(e, i=i, pb=pb, d=d):
                last = None
                for c in range(NC):
                    last = e.matmul(pb[:, 0:128], C.ln_fm[:, c * S + i * 128: c * S + (i + 1) * 128],
                                    wp[d][:, 2 * 1024 + c * 128: 2 * 1024 + (c + 1) * 128],
                                    start=(c == 0), stop=(c == NC - 1))
                return last
            P.op("tensor", grpv, reads=[wpr[d]] + C.ln_fmr, writes=[pbr], tag=f"l3av{hp}_{i}")
            P.op("scalar", lambda e, pb=pb, i=i: e.copy(vtm[:, i * 128:(i + 1) * 128], pb[:, 0:128]),
                 reads=[pbr], writes=[vtmr], tag=f"l3ave{hp}_{i}")

        for i in range(NT):
            p_ap = C.psbf[:, 0:128]
            P.op("tensor", lambda e, i=i, p_ap=p_ap: e.transpose(
                p_ap, fk[:, i * 128:(i + 1) * 128], C.ident),
                reads=[fkr, C.hcbr], writes=[C.psbfr], tag=f"l3akt{hp}_{i}")
            P.op("vector", lambda e, i=i, p_ap=p_ap: e.tensor_copy(
                fktm[:, i * 128:(i + 1) * 128], p_ap),
                reads=[C.psbfr], writes=[fktmr], tag=f"l3akte{hp}_{i}")

        for j in range(2):
            h = h0 + j
            rows = slice(j * 64, (j + 1) * 64)
            if j == 0:
                P.op("vector", lambda e, rows=rows: e.tensor_tensor(
                    KC[0:64, :], KC[0:64, :], fk[rows, :], ALU.add),
                    reads=[KCr, fkr], writes=[KCr], tag=f"l3akc{h}")
                P.op("vector", lambda e, rows=rows: e.tensor_tensor(
                    qk[0:64, :], fq[rows, :], KC[0:64, :], ALU.mult),
                    reads=[fqr, KCr], writes=[qkr], tag=f"l3aqk{h}")
            else:
                P.op("vector", lambda e: e.tensor_copy(zbc[0:64, :], fk[64:128, :]),
                     reads=[fkr], writes=[zbcr], tag=f"l3akst{h}")
                P.op("vector", lambda e: e.tensor_tensor(
                    KC[0:64, :], KC[0:64, :], zbc[0:64, :], ALU.add),
                    reads=[KCr, zbcr], writes=[KCr], tag=f"l3akc{h}")
                P.op("vector", lambda e: e.tensor_copy(qk[0:64, :], fq[64:128, :]),
                     reads=[fqr], writes=[qkr], tag=f"l3aqst{h}")
                P.op("vector", lambda e: e.tensor_tensor(
                    qk[0:64, :], qk[0:64, :], KC[0:64, :], ALU.mult),
                    reads=[qkr, KCr], writes=[qkr], tag=f"l3aqk{h}")
            P.op("vector", lambda e: e.tensor_copy(zbc[0:64, :], qk[0:64, :]),
                 reads=[qkr], writes=[zbcr], tag=f"l3aqkc{h}")
            for n0 in range(0, S, 512):
                pb, pbr = C.pb[2], C.pbr[2]
                P.op("tensor", lambda e, n0=n0, pb=pb: e.matmul(
                    pb[0:1, :], C.onecol_bf[0:64, 0:1], zbc[0:64, n0:n0 + 512],
                    start=True, stop=True),
                    reads=[zbcr, C.hcbr], writes=[pbr], tag=f"l3az{h}_{n0}")
                P.op("vector", lambda e, pb=pb, n0=n0: e.tensor_scalar(
                    zrow[:, n0:n0 + 512], pb[0:1, :], eps_c[0:1, :], None, ALU.add),
                    reads=[pbr, P.consts_region], writes=[zrowr], tag=f"l3aze{h}_{n0}")
            P.op("vector", lambda e: e.reciprocal(zrow[:, :], zrow[:, :]),
                 reads=[zrowr], writes=[zrowr], tag=f"l3azr{h}")
            P.op("vector", lambda e: e.tensor_copy(zbc[0:1, :], zrow[:, :]),
                 reads=[zrowr], writes=[zbcr], tag=f"l3azc{h}")
            for n0 in range(0, S, 512):
                pb, pbr = C.pb[2], C.pbr[2]
                P.op("tensor", lambda e, n0=n0, pb=pb: e.matmul(
                    pb[0:64, :], C.ones_bf[0:1, 0:64], zbc[0:1, n0:n0 + 512],
                    start=True, stop=True),
                    reads=[zbcr, C.hcbr], writes=[pbr], tag=f"l3azb{h}_{n0}")
                P.op("scalar", lambda e, pb=pb, n0=n0: e.copy(zbc[0:64, n0:n0 + 512], pb[0:64, :]),
                     reads=[pbr], writes=[zbcr], tag=f"l3azbe{h}_{n0}")
            if j == 0:
                P.op("vector", lambda e: e.tensor_tensor(
                    fq[0:64, :], fq[0:64, :], zbc[0:64, :], ALU.mult),
                    reads=[fqr, zbcr], writes=[fqr], tag=f"l3aqz{h}")
            else:
                P.op("vector", lambda e: e.tensor_copy(zbc[64:128, :], zbc[0:64, :]),
                     reads=[zbcr], writes=[zbcr], tag=f"l3aqzs{h}")
                P.op("vector", lambda e: e.tensor_tensor(
                    fq[64:128, :], fq[64:128, :], zbc[64:128, :], ALU.mult),
                    reads=[fqr, zbcr], writes=[fqr], tag=f"l3aqz{h}")

            P.op("vector", lambda e, j=j: e.memset(Ssb[j * 64:(j + 1) * 64, 0:64], 0.0),
                 writes=[Ssbr], tag=f"l3as0{h}")
            P.op("vector", lambda e, j=j: e.memset(Ssb32[j * 64:(j + 1) * 64, :], 0.0),
                 writes=[Ssb32r], tag=f"l3as032{h}")
            pS, pSr = C.pb[3 + j], C.pbr[3 + j]
            for cchunk in range(NT):
                d = cchunk % 2
                pb, pbr = (C.pb[5], C.pbr[5]) if d == 0 else (C.pb[0], C.pbr[0])
                ptb, ptbr = ptb2[d]
                P.op("tensor", lambda e, cchunk=cchunk, rows=rows, pb=pb: e.matmul(
                    pb[:, 0:128], fk[rows, cchunk * 128:(cchunk + 1) * 128],
                    fq[rows, cchunk * 128:(cchunk + 1) * 128], start=True, stop=True),
                    reads=[fkr, fqr], writes=[pbr], tag=f"l3ast{h}_{cchunk}")
                P.op("vector", lambda e, pb=pb, ptb=ptb: e.tensor_tensor(
                    ptb[:, :], pb[:, 0:128], triu, ALU.mult),
                    reads=[pbr, C.hcbr], writes=[ptbr], tag=f"l3am{h}_{cchunk}")
                po, por = (C.pb[2], C.pbr[2]) if d == 0 else (C.pb[6], C.pbr[6])

                def outg(e, cchunk=cchunk, rows=rows, j=j, po=po, ptb=ptb):
                    e.matmul(po[0:64, 0:128], vtm[:, cchunk * 128 + j * 64: cchunk * 128 + (j + 1) * 64],
                             ptb[:, :], start=True, stop=False)
                    return e.matmul(po[0:64, 0:128], Ssb[j * 64:(j + 1) * 64, 0:64],
                                    fq[rows, cchunk * 128:(cchunk + 1) * 128], start=False, stop=True)
                P.op("tensor", outg, reads=[vtmr, ptbr, Ssbr, fqr], writes=[por],
                     tag=f"l3ao{h}_{cchunk}")
                P.op("tensor", lambda e, cchunk=cchunk, j=j, pS=pS: e.matmul(
                    pS[0:64, 0:64], fktm[:, cchunk * 128 + j * 64: cchunk * 128 + (j + 1) * 64],
                    vtm[:, cchunk * 128 + j * 64: cchunk * 128 + (j + 1) * 64],
                    start=True, stop=True),
                    reads=[fktmr, vtmr], writes=[pSr], tag=f"l3asu{h}_{cchunk}")
                P.op("vector", lambda e, j=j, pS=pS: e.tensor_tensor(
                    Ssb32[j * 64:(j + 1) * 64, :], Ssb32[j * 64:(j + 1) * 64, :], pS[0:64, 0:64], ALU.add),
                    reads=[pSr, Ssb32r], writes=[Ssb32r], tag=f"l3asa{h}_{cchunk}")
                P.op("scalar", lambda e, j=j: e.copy(Ssb[j * 64:(j + 1) * 64, 0:64], Ssb32[j * 64:(j + 1) * 64, :]),
                     reads=[Ssb32r], writes=[Ssbr], tag=f"l3ase{h}_{cchunk}")
                c = h // 2
                arow = slice((h % 2) * 64, (h % 2) * 64 + 64)
                P.op("scalar", lambda e, po=po, c=c, arow=arow, cchunk=cchunk: e.copy(
                    att[arow, c * S + cchunk * 128: c * S + (cchunk + 1) * 128], po[0:64, 0:128]),
                    reads=[por], writes=[attr[c]], tag=f"l3aoe{h}_{cchunk}")

    P.dma("sync", bo_bc[:, :], W["L3_attn_bo"][None].broadcast_to([128, 1024]),
          writes=[bo_bcr], tag="l3a_bobc")
    for n0 in (0, 512):
        src = W["L3_attn_Wo"][:, n0:n0 + 512].rearrange("(c p) n -> p c n", p=128)
        P.dma("gpsimd", wo.rearrange("p (c n) -> p c n", c=NC), src, writes=[wor], tag=f"l3awo{n0}")
        for i in range(NT):
            pb, pbr = C.pb[5], C.pbr[5]

            def grpo(e, i=i, pb=pb):
                last = None
                for c in range(NC):
                    last = e.matmul(pb[:, :], att[:, c * S + i * 128: c * S + (i + 1) * 128],
                                    wo[:, c * 512:(c + 1) * 512],
                                    start=(c == 0), stop=(c == NC - 1))
                return last
            P.op("tensor", grpo, reads=[wor] + attr, writes=[pbr], tag=f"l3aop{i}_{n0}")
            x_ap = C.X[:, i * 1024 + n0: i * 1024 + n0 + 512]
            P.op("vector", lambda e, pb=pb, n0=n0: e.tensor_tensor(
                res[:, :], pb[:, :], bo_bc[:, n0:n0 + 512], ALU.add),
                reads=[pbr, bo_bcr], writes=[resr], tag=f"l3ax1_{i}_{n0}")
            P.op("vector", lambda e, x_ap=x_ap: e.tensor_tensor(
                x_ap, x_ap, res[:, :], ALU.add),
                reads=[resr, C.Xr[i]], writes=[C.Xr[i]], tag=f"l3ax2_{i}_{n0}")
    C.arena.reset()



def emit_l1_ffn(C: Core, W):
    P = C.P
    A = C.arena
    emit_layernorm(C)
    E, I = 8, 512

    gw, gwr = A.take(BF16, 1024, "l1f_gw")
    cwd = P.nc.dram_tensor("l1f_cwd", [8, S], BF16).ap()
    cwdr = Region("l1f_cwd")
    gwT, gwTr = A.take(BF16, 64, "l1f_gwT")
    logi, logir = A.take(F32, 16, "l1f_logi")
    cw, cwr = A.take(F32, NT * 8, "l1f_cw")
    cwbc, cwbcr = A.take(BF16, 512, "l1f_cwbc")
    inter, interr = A.take_tiles(BF16, 512, 36, "l1f_int")
    wg, wgr = A.take(BF16, 8 * 512, "l1f_wg")
    wu, wur = A.take(BF16, 8 * 512, "l1f_wu")
    wd, wdr = A.take(BF16, 4 * 1024, "l1f_wd")
    fsg, fsgr = A.take(F32, 512, "l1f_fsg")

    P.dma("gpsimd", gw[0:8, :], W["L1_ffn_gate_w"], writes=[gwr], tag="l1f_gw")
    for c in range(NC):
        p_ap = C.psbf[:, 0:8]
        P.op("tensor", lambda e, c=c, p_ap=p_ap: e.transpose(
            p_ap, gw[0:8, c * 128:(c + 1) * 128], C.ident[0:8, 0:8]),
            reads=[gwr, C.hcbr], writes=[C.psbfr], tag=f"l1f_gwt{c}")
        P.op("vector", lambda e, c=c, p_ap=p_ap: e.tensor_copy(gwT[:, c * 8:(c + 1) * 8], p_ap),
             reads=[C.psbfr], writes=[gwTr], tag=f"l1f_gwte{c}")

    eps20 = P.const_col(1e-20)
    L = logi
    pcols = lambda i: cw[:, i * 8:(i + 1) * 8]
    for i in range(NT):
        pb, pbr = C.pb[0], C.pbr[0]

        def gl(e, i=i, pb=pb):
            last = None
            for c in range(NC):
                last = e.matmul(pb[:, 0:8], C.ln_fm[:, c * S + i * 128: c * S + (i + 1) * 128],
                                gwT[:, c * 8:(c + 1) * 8], start=(c == 0), stop=(c == NC - 1))
            return last
        P.op("tensor", gl, reads=[gwTr] + C.ln_fmr, writes=[pbr], tag=f"l1f_gl{i}")
        P.op("vector", lambda e, pb=pb: e.reduce_max(L[:, 0:1], pb[:, 0:8], axis=AX.X),
             reads=[pbr], writes=[logir], tag="l1f_m")
        P.op("vector", lambda e: e.tensor_scalar_mul(L[:, 1:2], L[:, 0:1], -1.0),
             reads=[logir], writes=[logir], tag="l1f_nm")
        P.op("scalar", lambda e, pb=pb, i=i: e.activation(
            cw[:, i * 8:(i + 1) * 8], pb[:, 0:8], AF.Exp, bias=L[:, 1:2], accum_out=L[:, 2:3]),
            reads=[pbr, logir], writes=[cwr, logir], tag="l1f_exp")
        pcols = lambda i: cw[:, i * 8:(i + 1) * 8]
        P.op("vector", lambda e, i=i: e.reduce_max(L[:, 3:4], pcols(i), axis=AX.X),
             reads=[cwr], writes=[logir], tag="l1f_m1")
        P.op("vector", lambda e, i=i: e.tensor_scalar(L[:, 8:16], pcols(i), L[:, 3:4], None, ALU.is_lt),
             reads=[cwr, logir], writes=[logir], tag="l1f_lt")
        P.op("vector", lambda e, i=i: e.tensor_tensor(L[:, 8:16], pcols(i), L[:, 8:16], ALU.mult),
             reads=[cwr, logir], writes=[logir], tag="l1f_pm")
        P.op("vector", lambda e: e.reduce_max(L[:, 4:5], L[:, 8:16], axis=AX.X),
             reads=[logir], writes=[logir], tag="l1f_m2")
        P.op("vector", lambda e: e.tensor_tensor(L[:, 5:6], L[:, 3:4], L[:, 4:5], ALU.add),
             reads=[logir], writes=[logir], tag="l1f_den")
        P.op("vector", lambda e: e.tensor_scalar(L[:, 6:7], L[:, 2:3], eps20[:, 0:1], None, ALU.mult),
             reads=[logir, P.consts_region], writes=[logir], tag="l1f_eps")
        P.op("vector", lambda e: e.tensor_tensor(L[:, 5:6], L[:, 5:6], L[:, 6:7], ALU.add),
             reads=[logir], writes=[logir], tag="l1f_den2")
        emit_recip_nr(P, L[:, 6:7], L[:, 5:6], L[:, 7:8], logir, f"l1f_rden{i}")
        P.op("vector", lambda e, i=i: e.tensor_scalar(L[:, 8:16], pcols(i), L[:, 4:5], None, ALU.is_ge),
             reads=[cwr, logir], writes=[logir], tag="l1f_keep")
        P.op("vector", lambda e, i=i: e.tensor_tensor(pcols(i), pcols(i), L[:, 8:16], ALU.mult),
             reads=[cwr, logir], writes=[cwr], tag="l1f_cwm")
        P.op("vector", lambda e, i=i: e.tensor_scalar(pcols(i), pcols(i), L[:, 6:7], None, ALU.mult),
             reads=[cwr, logir], writes=[cwr], tag="l1f_cwn")

    cwbf, cwbfr = A.take(BF16, 128, "l1f_cwbf")
    cwfm, cwfmr = A.take(BF16, S, "l1f_cwfm")
    for i in range(NT):
        P.op("vector", lambda e, i=i: e.tensor_copy(cwbf[:, 0:8], cw[:, i * 8:(i + 1) * 8]),
             reads=[cwr], writes=[cwbfr], tag=f"l1f_cwc{i}")
        p_ap = C.psbf[:, 0:128]
        P.op("tensor", lambda e, p_ap=p_ap: e.transpose(p_ap[0:8, 0:128], cwbf[:, 0:8], C.ident),
             reads=[cwbfr, C.hcbr], writes=[C.psbfr], tag=f"l1f_cwt{i}")
        P.op("vector", lambda e, i=i, p_ap=p_ap: e.tensor_copy(
            cwfm[0:8, i * 128:(i + 1) * 128], p_ap[0:8, 0:128]),
            reads=[C.psbfr], writes=[cwfmr], tag=f"l1f_cwte{i}")
    P.dma("gpsimd", cwd, cwfm[0:8, :], reads=[cwfmr], writes=[cwdr], tag="l1f_cwst")

    for tq in range(4):
        t0 = tq * 512
        for e_ in range(9):
            if e_ < 8:
                srcg = W["L1_ffn_Wg"][e_].rearrange("(c p) n -> p c n", p=128)
                srcu = W["L1_ffn_Wu"][e_].rearrange("(c p) n -> p c n", p=128)
            else:
                srcg = W["L1_ffn_sWg"].rearrange("(c p) n -> p c n", p=128)
                srcu = W["L1_ffn_sWu"].rearrange("(c p) n -> p c n", p=128)
            P.dma("gpsimd", wg.rearrange("p (c n) -> p c n", c=NC), srcg, writes=[wgr], tag=f"l1fwg{tq}_{e_}")
            P.dma("gpsimd", wu.rearrange("p (c n) -> p c n", c=NC), srcu, writes=[wur], tag=f"l1fwu{tq}_{e_}")
            for ic in range(4):
                pg, pgr = C.pb[0], C.pbr[0]
                pu, pur = C.pb[1], C.pbr[1]

                def gg(e, ic=ic, pg=pg, t0=t0):
                    last = None
                    for c in range(NC):
                        last = e.matmul(pg[:, :], wg[:, c * 512 + ic * 128: c * 512 + (ic + 1) * 128],
                                        C.ln_fm[:, c * S + t0: c * S + t0 + 512],
                                        start=(c == 0), stop=(c == NC - 1))
                    return last
                P.op("tensor", gg, reads=[wgr] + C.ln_fmr, writes=[pgr], tag=f"l1fg{tq}_{e_}_{ic}")

                def gu(e, ic=ic, pu=pu, t0=t0):
                    last = None
                    for c in range(NC):
                        last = e.matmul(pu[:, :], wu[:, c * 512 + ic * 128: c * 512 + (ic + 1) * 128],
                                        C.ln_fm[:, c * S + t0: c * S + t0 + 512],
                                        start=(c == 0), stop=(c == NC - 1))
                    return last
                P.op("tensor", gu, reads=[wur] + C.ln_fmr, writes=[pur], tag=f"l1fu{tq}_{e_}_{ic}")
                P.op("scalar", lambda e, pg=pg: e.activation(fsg[:, :], pg[:, :], AF.Gelu),
                     reads=[pgr], writes=[fsgr], tag=f"l1fge{tq}_{e_}_{ic}")
                dsti = inter[:, (e_ * 4 + ic) * 512:(e_ * 4 + ic + 1) * 512]
                P.op("vector", lambda e, pu=pu, dsti=dsti: e.tensor_tensor(
                    dsti, pu[:, :], fsg[:, :], ALU.mult),
                    reads=[pur, fsgr], writes=[interr[e_ * 4 + ic]], tag=f"l1fim{tq}_{e_}_{ic}")
            if e_ < 8:
                P.dma("gpsimd", cwbc[:, :],
                      cwd[e_, t0:t0 + 512][None].broadcast_to([128, 512]),
                      reads=[cwdr], writes=[cwbcr], tag=f"l1fcb{tq}_{e_}")
                for ic in range(4):
                    dsti = inter[:, (e_ * 4 + ic) * 512:(e_ * 4 + ic + 1) * 512]
                    P.op("gpsimd", lambda e, dsti=dsti, ic=ic: e.tensor_tensor(
                        dsti, dsti, cwbc[:, :], ALU.mult),
                        reads=[interr[e_ * 4 + ic], cwbcr], writes=[interr[e_ * 4 + ic]], tag=f"l1fsc{tq}_{e_}_{ic}")
        for tg in range(2):
            for e_ in range(9):
                if e_ < 8:
                    srcd = W["L1_ffn_Wd"][e_].rearrange("(c p) n -> p c n", p=128)
                else:
                    srcd = W["L1_ffn_sWd"].rearrange("(c p) n -> p c n", p=128)
                P.dma("gpsimd", wd.rearrange("p (c n) -> p c n", c=4), srcd, writes=[wdr], tag=f"l1fwd{tq}_{tg}_{e_}")
                for tt in range(2):
                    for hh in range(2):
                        bank = tt * 2 + hh
                        pb, pbr = C.pb[2 + bank], C.pbr[2 + bank]

                        def gd(e, e_=e_, tt=tt, hh=hh, tg=tg, pb=pb):
                            last = None
                            for ic in range(4):
                                last = e.matmul(
                                    pb[:, :],
                                    inter[:, (e_ * 4 + ic) * 512 + (tg * 2 + tt) * 128:
                                          (e_ * 4 + ic) * 512 + (tg * 2 + tt + 1) * 128],
                                    wd[:, ic * 1024 + hh * 512: ic * 1024 + (hh + 1) * 512],
                                    start=(e_ == 0 and ic == 0), stop=(e_ == 8 and ic == 3))
                            return last
                        P.op("tensor", gd, reads=[wdr] + interr, writes=[pbr],
                             tag=f"l1fd{tq}_{tg}_{e_}_{tt}_{hh}")
            for tt in range(2):
                for hh in range(2):
                    bank = tt * 2 + hh
                    pb, pbr = C.pb[2 + bank], C.pbr[2 + bank]
                    i = tq * 4 + tg * 2 + tt
                    x_ap = C.X[:, i * 1024 + hh * 512: i * 1024 + hh * 512 + 512]
                    P.op("vector", lambda e, pb=pb, x_ap=x_ap: e.tensor_tensor(
                        x_ap, x_ap, pb[:, :], ALU.add),
                        reads=[pbr, C.Xr[i]], writes=[C.Xr[i]], tag=f"l1fx{tq}_{tg}_{tt}_{hh}")
    C.arena.reset()



def _emit_round_clip(P, A, C, src_psum_or_sb, srcr, dst, dstr, tmp, tmpr, ti, tir,
                      rs_col, rscol_r, pre_clip, tag):
    ALUv = ALU
    if pre_clip:
        P.op("vector", lambda e: e.tensor_scalar(tmp, src_psum_or_sb, rs_col, -129.0, ALUv.mult, ALUv.max),
             reads=[srcr, rscol_r], writes=[tmpr], tag=tag + "_t1")
        P.op("vector", lambda e: e.tensor_scalar(tmp, tmp, 128.0, 256.5, ALUv.min, ALUv.add),
             reads=[tmpr], writes=[tmpr], tag=tag + "_t2")
    else:
        P.op("vector", lambda e: e.tensor_scalar(tmp, src_psum_or_sb, rs_col, 256.5, ALUv.mult, ALUv.add),
             reads=[srcr, rscol_r], writes=[tmpr], tag=tag + "_t1")
    P.op("scalar", lambda e: e.copy(ti, tmp), reads=[tmpr], writes=[tir], tag=tag + "_c")
    P.op("scalar", lambda e: e.copy(src_psum_or_sb, ti), reads=[tir], writes=[srcr], tag=tag + "_cf")
    P.op("vector", lambda e: e.tensor_tensor(ti, src_psum_or_sb, tmp, ALUv.is_gt),
         reads=[srcr, tmpr], writes=[tir], tag=tag + "_gt")
    P.op("vector", lambda e: e.scalar_tensor_tensor(tmp, src_psum_or_sb, -256.0, ti, ALUv.add, ALUv.subtract),
         reads=[srcr, tir], writes=[tmpr], tag=tag + "_fl")
    P.op("vector", lambda e: e.tensor_scalar(dst, tmp, -128.0, 127.0, ALUv.max, ALUv.min),
         reads=[tmpr], writes=[dstr], tag=tag + "_cl")


def emit_l0_ffn(C: Core, W):
    P = C.P
    A = C.arena
    nc = P.nc
    I = 2816
    NI = 22
    emit_layernorm(C, fm=False)

    interd = nc.dram_tensor("l0f_inter", [NI, 128, S], F32).ap()
    interdr = Region("l0f_interd")
    interd2r = [Region("l0f_interdA"), Region("l0f_interdB")]
    rhd = nc.dram_tensor("l0f_rh", [NI, 128, S], BF16).ap()
    rhdr = Region("l0f_rhd")
    ccin = nc.dram_tensor("l0f_ccin", [1, 1], F32).ap()
    ccout = nc.dram_tensor("l0f_ccout", [1, 1], F32).ap()
    ccr = Region("l0f_cc")

    rx, rxr = A.take_tiles(BF16, S, NC, "l0f_rx")
    rxtm, rxtmr = A.take(BF16, 1024, "l0f_rxtm")
    q32, q32r = A.take(F32, 1024, "l0f_q32")
    qi, qir = A.take(I32, 1024, "l0f_qi")
    qb, qbr = A.take(BF16, 1024, "l0f_qb")
    ln32, ln32r = A.take(F32, 1024, "l0f_ln32")
    mx, mxr = A.take(F32, 16, "l0f_mx")
    wsta2 = [A.take(F32, 1024, f"l0f_wsta{d}") for d in range(2)]
    wgq2 = [A.take(BF16, 1024, f"l0f_wgq{d}") for d in range(2)]
    wscl, wsclr = A.take(F32, 2 * NI, "l0f_wscl")
    wgT2 = [A.take(BF16, 8 * 128, f"l0f_wgT{d}") for d in range(2)]
    wmx2 = [A.take(F32, 4, f"l0f_wmx{d}") for d in range(2)]
    hmx, hmxr = A.take(F32, 4, "l0f_hmx")
    gsb2_ = [A.take(F32, 512, f"l0f_gsb{d}") for d in range(2)]
    itmp2 = [A.take(F32, 512, f"l0f_itmp{d}") for d in range(2)]
    wdT, wdTr = A.take_tiles(BF16, 1024, NI, "l0f_wdT")
    res, resr = gsb2_[0]
    sc, scr = A.take(F32, 8, "l0f_sc")

    P.op("vector", lambda e: e.memset(mx[:, 0:1], 0.0), writes=[mxr], tag="l0f_mx0")
    for i in range(NT):
        emit_ln32_tile(C, i, ln32[:, :], ln32r)
        P.op("vector", lambda e: e.tensor_reduce(mx[:, 1:2], ln32[:, :], axis=AX.X,
                                                 op=ALU.max, apply_absolute_value=True),
             reads=[ln32r], writes=[mxr], tag=f"l0f_mxr{i}")
        P.op("vector", lambda e: e.tensor_tensor(mx[:, 0:1], mx[:, 0:1], mx[:, 1:2], ALU.max),
             reads=[mxr], writes=[mxr], tag=f"l0f_mxa{i}")
    mrowd = nc.dram_tensor("l0f_mrow", [1, 128], F32).ap()
    mrowdr = Region("l0f_mrow")
    P.dma("gpsimd", mrowd, mx[:, 0:1], reads=[mxr], writes=[mrowdr], tag="l0f_mst")
    P.dma("gpsimd", q32[0:1, 0:128], mrowd, reads=[mrowdr], writes=[q32r], tag="l0f_mld")
    P.op("vector", lambda e: e.reduce_max(mx[0:1, 2:3], q32[0:1, 0:128], axis=AX.X),
         reads=[q32r], writes=[mxr], tag="l0f_gmx")
    P.dma("gpsimd", ccin[0:1, 0:1], mx[0:1, 2:3], reads=[mxr], writes=[ccr], tag="l0f_ccs")
    P.op("gpsimd", lambda e: e.collective_compute(
        "AllReduce", ALU.max, replica_groups=[CC_GROUP],
        ins=[ccin.opt()], outs=[ccout.opt()]),
        reads=[ccr], writes=[ccr], sem="coll", inc=1, tag="l0f_cc")
    P.dma("gpsimd", mx[0:1, 3:4], ccout[0:1, 0:1], reads=[ccr], writes=[mxr], tag="l0f_ccl")
    P.dma("sync", mx[0:1, 4:5], W["L0_ffn_clip_x"][None], writes=[mxr], tag="l0f_clipx")
    P.op("scalar", lambda e: e.activation(mx[0:1, 5:6], mx[0:1, 4:5], AF.Sigmoid),
         reads=[mxr], writes=[mxr], tag="l0f_sig")
    P.op("vector", lambda e: e.tensor_tensor(mx[0:1, 6:7], mx[0:1, 5:6], mx[0:1, 3:4], ALU.mult),
         reads=[mxr], writes=[mxr], tag="l0f_sm")
    inv127 = P.const_col(1.0 / 127.0)
    P.op("vector", lambda e: e.tensor_scalar(mx[0:1, 6:7], mx[0:1, 6:7], inv127[0:1, :], None, ALU.mult),
         reads=[mxr, P.consts_region], writes=[mxr], tag="l0f_s127")
    emit_recip_nr(P, mx[0:1, 7:8], mx[0:1, 6:7], mx[0:1, 9:10], mxr, "l0f_rs")
    sxd = nc.dram_tensor("l0f_sx", [1, 2], F32).ap()
    sxdr = Region("l0f_sx")
    P.dma("gpsimd", sxd, mx[0:1, 6:8], reads=[mxr], writes=[sxdr], tag="l0f_sxs")
    P.dma("gpsimd", sc[:, 0:1], sxd[0, 1:2][None].broadcast_to([128, 1]), reads=[sxdr],
          writes=[scr], tag="l0f_rsb")
    P.dma("gpsimd", sc[:, 1:2], sxd[0, 0:1][None].broadcast_to([128, 1]), reads=[sxdr],
          writes=[scr], tag="l0f_sxb")

    for i in range(NT):
        emit_ln32_tile(C, i, ln32[:, :], ln32r)
        dtile = rxtm[:, 0:1024]
        dreg = rxtmr
        _emit_round_clip(P, A, C, ln32[:, :], ln32r, dtile, dreg, q32[:, :], q32r,
                         qi[:, :], qir, sc[:, 0:1], scr, True, f"l0f_qx{i}")
        for c in range(NC):
            p_ap = C.psbf[:, (c % 8) * 128:(c % 8) * 128 + 128]
            P.op("tensor", lambda e, dtile=dtile, c=c, p_ap=p_ap: e.transpose(
                p_ap, dtile[:, c * 128:(c + 1) * 128], C.ident),
                reads=[dreg, C.hcbr], writes=[C.psbfr], tag=f"l0f_qt{i}_{c}")
            if c % 2 == 0:
                P.op("vector", lambda e, c=c, i=i, p_ap=p_ap: e.tensor_copy(
                    rx[:, c * S + i * 128: c * S + (i + 1) * 128], p_ap),
                    reads=[C.psbfr], writes=[rxr[c]], tag=f"l0f_qte{i}_{c}")
            else:
                P.op("scalar", lambda e, c=c, i=i, p_ap=p_ap: e.copy(
                    rx[:, c * S + i * 128: c * S + (i + 1) * 128], p_ap),
                    reads=[C.psbfr], writes=[rxr[c]], tag=f"l0f_qte{i}_{c}")

    P.op("vector", lambda e: e.memset(hmx[:, 0:1], 0.0), writes=[hmxr], tag="l0f_hm0")
    for gi, wname in ((0, "L0_ffn_w_gate"), (1, "L0_ffn_w_up")):
        for ichunk in range(NI):
            d = (gi * NI + ichunk) % 2
            wsta, wstar = wsta2[d]
            wgq, wgqr = wgq2[d]
            wgT, wgTr = wgT2[d]
            wmx, wmxr = wmx2[d]
            P.dma("sync", wsta[:, :], W[wname][ichunk * 128:(ichunk + 1) * 128, :],
                  writes=[wstar], tag=f"l0f_wl{gi}_{ichunk}")
            P.op("vector", lambda e, wsta=wsta, wmx=wmx: e.tensor_reduce(
                wmx[:, 0:1], wsta[:, :], axis=AX.X,
                op=ALU.max, apply_absolute_value=True),
                reads=[wstar], writes=[wmxr], tag=f"l0f_wm{gi}_{ichunk}")
            P.op("vector", lambda e, gi=gi, ichunk=ichunk, wmx=wmx: e.tensor_scalar(
                wscl[:, gi * NI + ichunk: gi * NI + ichunk + 1], wmx[:, 0:1], inv127[:, :], None, ALU.mult),
                reads=[wmxr, P.consts_region], writes=[wsclr], tag=f"l0f_ws{gi}_{ichunk}")
            emit_recip_nr(P, wmx[:, 1:2], wmx[:, 0:1], wmx[:, 2:3], wmxr,
                          f"l0f_wr{gi}_{ichunk}")
            P.op("vector", lambda e, wmx=wmx: e.tensor_scalar(wmx[:, 1:2], wmx[:, 1:2], 127.0, None, ALU.mult),
                 reads=[wmxr], writes=[wmxr], tag=f"l0f_wr2{gi}_{ichunk}")
            P.op("vector", lambda e, gi=gi, ichunk=ichunk, wmx=wmx: e.tensor_tensor(
                wmx[:, 3:4], wscl[:, gi * NI + ichunk: gi * NI + ichunk + 1], sc[:, 1:2], ALU.mult),
                reads=[wsclr, scr], writes=[wmxr], tag=f"l0f_se{gi}_{ichunk}")
            _emit_round_clip(P, A, C, wsta[:, :], wstar, wgq[:, 0:1024], wgqr,
                             q32[:, :], q32r, qi[:, :], qir, wmx[:, 1:2], wmxr, False,
                             f"l0f_wq{gi}_{ichunk}")
            for c in range(NC):
                p_ap = C.psbf[:, (c % 8) * 128:(c % 8) * 128 + 128]
                P.op("tensor", lambda e, c=c, p_ap=p_ap, wgq=wgq: e.transpose(
                    p_ap, wgq[:, c * 128:(c + 1) * 128], C.ident),
                    reads=[wgqr, C.hcbr], writes=[C.psbfr], tag=f"l0f_wt{gi}_{ichunk}_{c}")
                P.op("scalar", lambda e, c=c, p_ap=p_ap, wgT=wgT: e.copy(
                    wgT[:, c * 128:(c + 1) * 128], p_ap),
                    reads=[C.psbfr], writes=[wgTr], tag=f"l0f_wte{gi}_{ichunk}_{c}")
            for n0 in range(0, S, 512):
                pb, pbr = C.pb[2 * gi + d], C.pbr[2 * gi + d]
                d2 = (ichunk * 4 + n0 // 512) % 2
                gsb, gsbr = gsb2_[d2]
                itmp, itmpr = itmp2[d2]
                idr = interd2r[(n0 // 512) % 2]

                def gg(e, n0=n0, pb=pb, wgT=wgT):
                    last = None
                    for c in range(NC):
                        last = e.matmul(pb[:, :], wgT[:, c * 128:(c + 1) * 128],
                                        rx[:, c * S + n0: c * S + n0 + 512],
                                        start=(c == 0), stop=(c == NC - 1))
                    return last
                P.op("tensor", gg, reads=[wgTr] + rxr, writes=[pbr], tag=f"l0f_mm{gi}_{ichunk}_{n0}")
                if gi == 0:
                    P.op("scalar", lambda e, pb=pb, gsb=gsb, wmx=wmx: e.activation(
                        gsb[:, :], pb[:, :], AF.Silu, scale=wmx[:, 3:4]),
                        reads=[pbr, wmxr], writes=[gsbr], tag=f"l0f_sil{ichunk}_{n0}")
                    P.dma("gpsimd", interd[ichunk, :, n0:n0 + 512], gsb[:, :],
                          reads=[gsbr], writes=[idr], tag=f"l0f_ist{ichunk}_{n0}")
                else:
                    P.dma("sync", gsb[:, :], interd[ichunk, :, n0:n0 + 512],
                          reads=[idr], writes=[gsbr], tag=f"l0f_ild{ichunk}_{n0}")
                    P.op("scalar", lambda e, pb=pb, itmp=itmp, wmx=wmx: e.activation(
                        itmp[:, :], pb[:, :], AF.Copy, scale=wmx[:, 3:4]),
                        reads=[pbr, wmxr], writes=[itmpr], tag=f"l0f_up{ichunk}_{n0}")
                    P.op("vector", lambda e, itmp=itmp, gsb=gsb: e.tensor_tensor(
                        itmp[:, :], itmp[:, :], gsb[:, :], ALU.mult),
                        reads=[itmpr, gsbr], writes=[itmpr], tag=f"l0f_im{ichunk}_{n0}")
                    P.op("vector", lambda e, itmp=itmp: e.tensor_reduce(
                        hmx[:, 1:2], itmp[:, :], axis=AX.X,
                        op=ALU.max, apply_absolute_value=True),
                        reads=[itmpr], writes=[hmxr], tag=f"l0f_hm{ichunk}_{n0}")
                    P.op("vector", lambda e: e.tensor_tensor(hmx[:, 0:1], hmx[:, 0:1], hmx[:, 1:2], ALU.max),
                         reads=[hmxr], writes=[hmxr], tag=f"l0f_hma{ichunk}_{n0}")
                    P.dma("gpsimd", interd[ichunk, :, n0:n0 + 512], itmp[:, :],
                          reads=[itmpr], writes=[idr], tag=f"l0f_ist2{ichunk}_{n0}")

    P.dma("gpsimd", mrowd, hmx[:, 0:1], reads=[hmxr], writes=[mrowdr], tag="l0f_hst")
    P.dma("gpsimd", q32[0:1, 0:128], mrowd, reads=[mrowdr], writes=[q32r], tag="l0f_hld")
    P.op("vector", lambda e: e.reduce_max(mx[0:1, 13:14], q32[0:1, 0:128], axis=AX.X),
         reads=[q32r], writes=[mxr], tag="l0f_ghm")
    P.dma("gpsimd", ccin[0:1, 0:1], mx[0:1, 13:14], reads=[mxr, ccr], writes=[ccr], tag="l0f_ccs2")
    P.op("gpsimd", lambda e: e.collective_compute(
        "AllReduce", ALU.max, replica_groups=[CC_GROUP],
        ins=[ccin.opt()], outs=[ccout.opt()]),
        reads=[ccr], writes=[ccr], sem="coll", inc=1, tag="l0f_cc2")
    P.dma("gpsimd", mx[0:1, 14:15], ccout[0:1, 0:1], reads=[ccr], writes=[mxr], tag="l0f_ccl2")
    P.dma("sync", mx[0:1, 4:5], W["L0_ffn_clip_down"][None], writes=[mxr], tag="l0f_clipd")
    P.op("scalar", lambda e: e.activation(mx[0:1, 5:6], mx[0:1, 4:5], AF.Sigmoid),
         reads=[mxr], writes=[mxr], tag="l0f_sig2")
    P.op("vector", lambda e: e.tensor_tensor(mx[0:1, 6:7], mx[0:1, 5:6], mx[0:1, 14:15], ALU.mult),
         reads=[mxr], writes=[mxr], tag="l0f_sm2")
    P.op("vector", lambda e: e.tensor_scalar(mx[0:1, 6:7], mx[0:1, 6:7], inv127[0:1, :], None, ALU.mult),
         reads=[mxr, P.consts_region], writes=[mxr], tag="l0f_s1272")
    emit_recip_nr(P, mx[0:1, 7:8], mx[0:1, 6:7], mx[0:1, 9:10], mxr, "l0f_rs2")
    P.dma("gpsimd", sxd, mx[0:1, 6:8], reads=[mxr], writes=[sxdr], tag="l0f_sxs2")
    P.dma("gpsimd", sc[:, 2:3], sxd[0, 1:2][None].broadcast_to([128, 1]), reads=[sxdr],
          writes=[scr], tag="l0f_rsb2")
    P.dma("gpsimd", sc[:, 3:4], sxd[0, 0:1][None].broadcast_to([128, 1]), reads=[sxdr],
          writes=[scr], tag="l0f_sxb2")

    for ichunk in range(NI):
        for half in range(2):
            cols = slice(half * 1024, half * 1024 + 1024)
            P.dma("sync", q32[:, :] if False else ln32[:, :], interd[ichunk, :, cols],
                  reads=[interd2r[0], interd2r[1]], writes=[ln32r], tag=f"l0f_il{ichunk}_{half}")
            _emit_round_clip(P, A, C, ln32[:, :], ln32r, qb[:, :], qbr, q32[:, :], q32r,
                             qi[:, :], qir, sc[:, 2:3], scr, True, f"l0f_qh{ichunk}_{half}")
            P.dma("gpsimd", rhd[ichunk, :, cols], qb[:, :], reads=[qbr], writes=[rhdr],
                  tag=f"l0f_rhs{ichunk}_{half}")
    wds, wdsr = A.take(F32, 2816, "l0f_wds")
    for hc in range(NC):
        P.dma("sync", wds[:, :], W["L0_ffn_w_down"][hc * 128:(hc + 1) * 128, :],
              writes=[wdsr], tag=f"l0f_dl{hc}")
        P.op("vector", lambda e: e.tensor_reduce(mx[:, 9:10], wds[:, :], axis=AX.X,
                                                 op=ALU.max, apply_absolute_value=True),
             reads=[wdsr], writes=[mxr], tag=f"l0f_dm{hc}")
        P.op("vector", lambda e: e.tensor_scalar(mx[:, 15:16], mx[:, 9:10], inv127[:, :], None, ALU.mult),
             reads=[mxr, P.consts_region], writes=[mxr], tag=f"l0f_dsc{hc}")
        emit_recip_nr(P, mx[:, 10:11], mx[:, 9:10], mx[:, 12:13], mxr, f"l0f_dr{hc}")
        P.op("vector", lambda e: e.tensor_scalar(mx[:, 10:11], mx[:, 10:11], 127.0, None, ALU.mult),
             reads=[mxr], writes=[mxr], tag=f"l0f_dr2{hc}")
        for half in range(3):
            cols = slice(half * 1024, min((half + 1) * 1024, 2816))
            w = cols.stop - cols.start
            _emit_round_clip(P, A, C, wds[:, cols], wdsr, qb[:, 0:w], qbr, q32[:, 0:w], q32r,
                             qi[:, 0:w], qir, mx[:, 10:11], mxr, False, f"l0f_dq{hc}_{half}")
            P.op("vector", lambda e, w=w: e.tensor_scalar(qb[:, 0:w], qb[:, 0:w], mx[:, 15:16], None, ALU.mult),
                 reads=[qbr, mxr], writes=[qbr], tag=f"l0f_dv{hc}_{half}")
            nb = (w + 127) // 128
            for c in range(nb):
                ic = half * 8 + c
                if ic >= NI:
                    break
                p_ap = C.psbf[:, (c % 8) * 128:(c % 8) * 128 + 128]
                P.op("tensor", lambda e, c=c, p_ap=p_ap: e.transpose(
                    p_ap, qb[:, c * 128:(c + 1) * 128], C.ident),
                    reads=[qbr, C.hcbr], writes=[C.psbfr], tag=f"l0f_dt{hc}_{half}_{c}")
                P.op("scalar", lambda e, ic=ic, hc=hc, p_ap=p_ap: e.copy(
                    wdT[:, ic * 1024 + hc * 128: ic * 1024 + (hc + 1) * 128], p_ap),
                    reads=[C.psbfr], writes=[wdTr[ic]], tag=f"l0f_dte{hc}_{half}_{c}")

    rhs2 = [A.take(BF16, 256, f"l0f_rhl{d}") for d in range(2)]
    for tg in range(8):
        for ichunk in range(NI):
            rhs_, rhs_r = rhs2[ichunk % 2]
            P.dma("sync", rhs_[:, 0:256], rhd[ichunk, :, tg * 256:(tg + 1) * 256],
                  reads=[rhdr], writes=[rhs_r], tag=f"l0f_rl{tg}_{ichunk}")
            for tt in range(2):
                for hh in range(2):
                    bank = tt * 2 + hh
                    pb, pbr = C.pb[2 + bank], C.pbr[2 + bank]
                    P.op("tensor", lambda e, ichunk=ichunk, tt=tt, hh=hh, pb=pb, rhs_=rhs_: e.matmul(
                        pb[:, :], rhs_[:, tt * 128:(tt + 1) * 128],
                        wdT[:, ichunk * 1024 + hh * 512: ichunk * 1024 + (hh + 1) * 512],
                        start=(ichunk == 0), stop=(ichunk == NI - 1)),
                        reads=[rhs_r, wdTr[ichunk]], writes=[pbr], tag=f"l0f_dn{tg}_{ichunk}_{tt}_{hh}")
        for tt in range(2):
            for hh in range(2):
                bank = tt * 2 + hh
                pb, pbr = C.pb[2 + bank], C.pbr[2 + bank]
                xtile = (tg * 256 + tt * 128) // 128
                x_ap = C.X[:, xtile * 1024 + hh * 512: xtile * 1024 + hh * 512 + 512]
                P.op("scalar", lambda e, pb=pb: e.activation(
                    res[:, :], pb[:, :], AF.Copy, scale=sc[:, 3:4]),
                    reads=[pbr, scr], writes=[resr], tag=f"l0f_ev{tg}_{tt}_{hh}")
                P.op("vector", lambda e, x_ap=x_ap: e.tensor_tensor(x_ap, x_ap, res[:, :], ALU.add),
                     reads=[resr, C.Xr[xtile]], writes=[C.Xr[xtile]], tag=f"l0f_x{tg}_{tt}_{hh}")
    C.arena.reset()



def emit_l2_attn(C: Core, W):
    P = C.P
    A = C.arena
    nc = P.nc
    HEADS, STATE, INNER, HD = 8, 64, 2048, 256
    NO = 16
    m0 = A.mark()
    P.op("vector", lambda e: e.memset(C.stat[:, 16:48], 0.0), writes=[C.statr], tag="l2a_gz")
    emit_layernorm(C)

    xind = nc.dram_tensor("l2a_xin", [NO, 128, S], BF16).ap()
    xindr = Region("l2a_xind")
    zd = nc.dram_tensor("l2a_z", [NO, 128, S], BF16).ap()
    zdr = Region("l2a_zd")
    yhd = nc.dram_tensor("l2a_y", [NO, 128, S], F32).ap()
    yhdr = Region("l2a_yhd")
    misc = nc.dram_tensor("l2a_misc", [4, 128], F32).ap()
    miscr = Region("l2a_misc")

    wst, wstr = A.take(F32, 2048, "l2a_wst")
    wbfst, wbfstr = A.take(BF16, 2048, "l2a_wbfst")
    wT, wTr = A.take(BF16, 8 * 128, "l2a_wT")
    xpc, xpcr = A.take(F32, S, "l2a_xpc")
    cvacc, cvaccr = A.take(F32, S, "l2a_cva")
    xinc, xincr = A.take(BF16, S, "l2a_xinc")
    convw, convwr = A.take(F32, 4 * NO, "l2a_cw")
    convb, convbr = A.take(F32, NO, "l2a_cb")

    P.dma("sync", convw.rearrange("p (c k) -> p c k", k=4),
          W["L2_attn_conv_w"].rearrange("(c p) one k -> p (c one) k", p=128),
          writes=[convwr], tag="l2a_cwl")
    P.dma("sync", wst[0:NO, 0:128], W["L2_attn_conv_b"].rearrange("(c p) -> c p", p=128),
          writes=[wstr], tag="l2a_cbl")
    idf0, idf0r = A.take(F32, 128, "l2a_idf0")
    P.op("vector", lambda e: e.tensor_copy(idf0[:, :], C.ident),
         reads=[C.hcbr], writes=[idf0r], tag="l2a_idf0")
    pb0, pb0r = C.pb[0], C.pbr[0]
    P.op("tensor", lambda e: e.transpose(pb0[:, 0:NO], wst[0:NO, 0:128], idf0[0:NO, 0:NO]),
         reads=[wstr, idf0r], writes=[pb0r], tag="l2a_cbt")
    P.op("vector", lambda e: e.tensor_copy(convb[:, :], pb0[:, 0:NO]),
         reads=[pb0r], writes=[convbr], tag="l2a_cbe")

    for mat, isz in (("Wz", True), ("Wx", False)):
        for oc in range(NO):
            P.dma("sync", wst[:, 0:1024], W[f"L2_attn_{mat}"][oc * 128:(oc + 1) * 128, :],
                  writes=[wstr], tag=f"l2a_{mat}l{oc}")
            P.op("vector", lambda e: e.tensor_copy(wbfst[:, 0:1024], wst[:, 0:1024]),
                 reads=[wstr], writes=[wbfstr], tag=f"l2a_{mat}c{oc}")
            for c in range(NC):
                p_ap = C.psbf[:, (c % 8) * 128:(c % 8) * 128 + 128]
                P.op("tensor", lambda e, c=c, p_ap=p_ap: e.transpose(
                    p_ap, wbfst[:, c * 128:(c + 1) * 128], C.ident),
                    reads=[wbfstr, C.hcbr], writes=[C.psbfr], tag=f"l2a_{mat}t{oc}_{c}")
                P.op("scalar", lambda e, c=c, p_ap=p_ap: e.copy(wT[:, c * 128:(c + 1) * 128], p_ap),
                     reads=[C.psbfr], writes=[wTr], tag=f"l2a_{mat}te{oc}_{c}")
            for n0 in range(0, S, 512):
                pb, pbr = C.pb[1], C.pbr[1]

                def gp(e, n0=n0, pb=pb):
                    last = None
                    for c in range(NC):
                        last = e.matmul(pb[:, :], wT[:, c * 128:(c + 1) * 128],
                                        C.ln_fm[:, c * S + n0: c * S + n0 + 512],
                                        start=(c == 0), stop=(c == NC - 1))
                    return last
                P.op("tensor", gp, reads=[wTr] + C.ln_fmr, writes=[pbr], tag=f"l2a_{mat}m{oc}_{n0}")
                if isz:
                    P.op("scalar", lambda e, pb=pb, n0=n0: e.activation(
                        xinc[:, n0:n0 + 512], pb[:, :], AF.Silu),
                        reads=[pbr], writes=[xincr], tag=f"l2a_ze{oc}_{n0}")
                else:
                    P.op("scalar", lambda e, pb=pb, n0=n0: e.copy(xpc[:, n0:n0 + 512], pb[:, :]),
                         reads=[pbr], writes=[xpcr], tag=f"l2a_xe{oc}_{n0}")
            if isz:
                P.dma("gpsimd", zd[oc], xinc[:, :], reads=[xincr], writes=[zdr], tag=f"l2a_zs{oc}")
            else:
                P.op("vector", lambda e, oc=oc: e.tensor_scalar(
                    cvacc[:, :], xpc[:, :], convw[:, oc * 4 + 3:oc * 4 + 4], None, ALU.mult),
                    reads=[xpcr, convwr], writes=[cvaccr], tag=f"l2a_cv3_{oc}")
                for k in range(1, 4):
                    P.op("vector", lambda e, k=k, oc=oc: e.scalar_tensor_tensor(
                        cvacc[:, k:S], xpc[:, 0:S - k], convw[:, oc * 4 + 3 - k:oc * 4 + 4 - k],
                        cvacc[:, k:S], ALU.mult, ALU.add),
                        reads=[xpcr, convwr, cvaccr], writes=[cvaccr], tag=f"l2a_cv{k}_{oc}")
                P.op("vector", lambda e, oc=oc: e.tensor_scalar(
                    cvacc[:, :], cvacc[:, :], convb[:, oc:oc + 1], None, ALU.add),
                    reads=[cvaccr, convbr], writes=[cvaccr], tag=f"l2a_cvb{oc}")
                P.op("scalar", lambda e: e.activation(xinc[:, :], cvacc[:, :], AF.Silu),
                     reads=[cvaccr], writes=[xincr], tag=f"l2a_cvs{oc}")
                P.dma("gpsimd", xind[oc], xinc[:, :], reads=[xincr], writes=[xindr], tag=f"l2a_xs{oc}")

    A.rewind(m0)
    xin, xinr = A.take_tiles(BF16, S, NO, "l2a_xr")
    for oc in range(NO):
        P.dma("sync", xin[:, oc * S:(oc + 1) * S], xind[oc], reads=[xindr],
              writes=[xinr[oc]], tag=f"l2a_xrl{oc}")
    bmat, bmatr = A.take_tiles(BF16, S, 4, "l2a_b")
    cmat, cmatr = A.take_tiles(BF16, S, 4, "l2a_c")
    dtt, dttr = A.take(F32, 8 * NT, "l2a_dt")
    dta32, dta32r = A.take(F32, 8, "l2a_dta")
    cA, cAr = A.take(F32, 8 * NT, "l2a_cA")
    abc, abcr = A.take(F32, 8, "l2a_abc")
    bdtb, bdtbr = A.take(F32, 8, "l2a_bdtb")
    dpar, dparr = A.take(F32, 8, "l2a_dp")
    carow, carowr = A.take(F32, 144, "l2a_car")
    ones32, ones32r = A.take(F32, 128, "l2a_o32")
    trif, trifr = A.take(F32, 128, "l2a_trif")
    idf, idfr = A.take(F32, 128, "l2a_idf")
    P.op("vector", lambda e: e.tensor_copy(idf[:, :], C.ident),
         reads=[C.hcbr], writes=[idfr], tag="l2a_idf")
    st32, st32r = A.take(F32, 8 * 256, "l2a_st32")
    stbf, stbfr = A.take(BF16, 8 * 256, "l2a_stbf")
    m1 = A.mark()
    wst2, wst2r = A.take(F32, 1024, "l2a_wst2")
    wbfst2, wbfst2r = A.take(BF16, 1024, "l2a_wbfst2")
    wbT, wbTr = A.take_tiles(BF16, 512, NO, "l2a_wbT")
    wdtT, wdtTr = A.take(BF16, 8 * NO, "l2a_wdtT")

    P.op("vector", lambda e: e.memset(ones32[0:1, :], 1.0), writes=[ones32r], tag="l2a_o32")
    P.op("vector", lambda e: e.tensor_copy(trif[:, :], C.hc("l0m0", 512)[:, 0:128]),
         reads=[C.hcbr], writes=[trifr], tag="l2a_trif")
    P.dma("sync", carow[0:1, 128:136], W["L2_attn_log_a"][None], writes=[carowr], tag="l2a_lal")
    P.op("scalar", lambda e: e.activation(carow[0:1, 128:136], carow[0:1, 128:136], AF.Exp),
         reads=[carowr], writes=[carowr], tag="l2a_la1")
    P.op("vector", lambda e: e.tensor_scalar_mul(carow[0:1, 128:136], carow[0:1, 128:136], -1.0),
         reads=[carowr], writes=[carowr], tag="l2a_la2")
    P.dma("gpsimd", misc[0, 0:8][None], carow[0:1, 128:136], reads=[carowr], writes=[miscr], tag="l2a_las")
    P.dma("gpsimd", abc[:, 0:8], misc[0, 0:8][None].broadcast_to([128, 8]), reads=[miscr],
          writes=[abcr], tag="l2a_lab")
    P.dma("gpsimd", bdtb[:, 0:8], W["L2_attn_b_dt"][None].broadcast_to([128, 8]),
          writes=[bdtbr], tag="l2a_bdl")
    P.dma("gpsimd", dpar[:, 0:8], W["L2_attn_d_param"][None].broadcast_to([128, 8]),
          writes=[dparr], tag="l2a_dpl")

    for mat, dstt, dstr_ in (("Wb", bmat, bmatr), ("Wc", cmat, cmatr)):
        for rc in range(4):
          for hf in range(2):
            P.dma("sync", wst2[:, :], W[f"L2_attn_{mat}"][rc * 128:(rc + 1) * 128, hf * 1024:(hf + 1) * 1024],
                  writes=[wst2r], tag=f"l2a_{mat}l{rc}_{hf}")
            P.op("vector", lambda e: e.tensor_copy(wbfst2[:, :], wst2[:, :]),
                 reads=[wst2r], writes=[wbfst2r], tag=f"l2a_{mat}c{rc}_{hf}")
            for oc8 in range(8):
                oc = hf * 8 + oc8
                p_ap = C.psbf[:, (oc % 8) * 128:(oc % 8) * 128 + 128]
                P.op("tensor", lambda e, oc8=oc8, p_ap=p_ap: e.transpose(
                    p_ap, wbfst2[:, oc8 * 128:(oc8 + 1) * 128], C.ident),
                    reads=[wbfst2r, C.hcbr], writes=[C.psbfr], tag=f"l2a_{mat}t{rc}_{oc}")
                P.op("scalar", lambda e, oc=oc, rc=rc, p_ap=p_ap: e.copy(
                    wbT[:, oc * 512 + rc * 128: oc * 512 + (rc + 1) * 128], p_ap),
                    reads=[C.psbfr], writes=[wbTr[oc]], tag=f"l2a_{mat}te{rc}_{oc}")
        for mc in range(4):
            for n0 in range(0, S, 512):
                pb, pbr = C.pb[2 + (mc % 2)], C.pbr[2 + (mc % 2)]

                def gb(e, n0=n0, mc=mc, pb=pb):
                    last = None
                    for oc in range(NO):
                        last = e.matmul(pb[:, :], wbT[:, oc * 512 + mc * 128: oc * 512 + (mc + 1) * 128],
                                        xin[:, oc * S + n0: oc * S + n0 + 512],
                                        start=(oc == 0), stop=(oc == NO - 1))
                    return last
                P.op("tensor", gb, reads=wbTr + xinr, writes=[pbr], tag=f"l2a_{mat}m{mc}_{n0}")
                P.op("scalar", lambda e, pb=pb, mc=mc, n0=n0, dstt=dstt: e.copy(
                    dstt[:, mc * S + n0: mc * S + n0 + 512], pb[:, :]),
                    reads=[pbr], writes=[dstr_[mc]], tag=f"l2a_{mat}e{mc}_{n0}")
    for hf in range(2):
        P.dma("sync", wst2[0:8, 0:1024], W["L2_attn_Wdt"][:, hf * 1024:(hf + 1) * 1024],
              writes=[wst2r], tag=f"l2a_dtl{hf}")
        P.op("vector", lambda e: e.tensor_copy(wbfst2[0:8, 0:1024], wst2[0:8, 0:1024]),
             reads=[wst2r], writes=[wbfst2r], tag=f"l2a_dtc{hf}")
        for oc8 in range(8):
            oc = hf * 8 + oc8
            p_ap = C.psbf[:, 0:8]
            P.op("tensor", lambda e, oc8=oc8, p_ap=p_ap: e.transpose(
                p_ap, wbfst2[0:8, oc8 * 128:(oc8 + 1) * 128], C.ident[0:8, 0:8]),
                reads=[wbfst2r, C.hcbr], writes=[C.psbfr], tag=f"l2a_dtt{oc}")
            P.op("vector", lambda e, oc=oc, p_ap=p_ap: e.tensor_copy(wdtT[:, oc * 8:(oc + 1) * 8], p_ap),
                 reads=[C.psbfr], writes=[wdtTr], tag=f"l2a_dtte{oc}")
    for i in range(NT):
        pb, pbr = C.pb[4], C.pbr[4]

        def gdt(e, i=i, pb=pb):
            last = None
            for oc in range(NO):
                last = e.matmul(pb[:, 0:8], xin[:, oc * S + i * 128: oc * S + (i + 1) * 128],
                                wdtT[:, oc * 8:(oc + 1) * 8], start=(oc == 0), stop=(oc == NO - 1))
            return last
        P.op("tensor", gdt, reads=[wdtTr] + xinr, writes=[pbr], tag=f"l2a_dtm{i}")
        dcol = lambda i: dtt[:, i * 8:(i + 1) * 8]
        P.op("vector", lambda e, i=i, pb=pb: e.tensor_tensor(dcol(i), pb[:, 0:8], bdtb[:, 0:8], ALU.add),
             reads=[pbr, bdtbr], writes=[dttr], tag=f"l2a_dtb{i}")
        P.op("vector", lambda e, i=i: e.tensor_scalar_mul(dta32[:, 0:8], dcol(i), -1.0),
             reads=[dttr], writes=[dta32r], tag=f"l2a_sp1{i}")
        P.op("vector", lambda e, i=i: e.tensor_tensor(dta32[:, 0:8], dta32[:, 0:8], dcol(i), ALU.min),
             reads=[dta32r, dttr], writes=[dta32r], tag=f"l2a_sp1b{i}")
        cp1 = P.const_col(1.0)
        P.op("scalar", lambda e: e.activation(dta32[:, 0:8], dta32[:, 0:8], AF.Exp),
             reads=[dta32r], writes=[dta32r], tag=f"l2a_sp2{i}")
        P.op("scalar", lambda e, cp1=cp1: e.activation(dta32[:, 0:8], dta32[:, 0:8], AF.Ln, bias=cp1),
             reads=[dta32r, P.consts_region], writes=[dta32r], tag=f"l2a_sp3{i}")
        P.op("vector", lambda e, i=i: e.tensor_scalar(dcol(i), dcol(i), 0.0, None, ALU.max),
             reads=[dttr], writes=[dttr], tag=f"l2a_sp4{i}")
        P.op("vector", lambda e, i=i: e.tensor_tensor(dcol(i), dcol(i), dta32[:, 0:8], ALU.add),
             reads=[dttr, dta32r], writes=[dttr], tag=f"l2a_sp5{i}")
        P.op("vector", lambda e, i=i: e.tensor_tensor(dta32[:, 0:8], dcol(i), abc[:, 0:8], ALU.mult),
             reads=[dttr, abcr], writes=[dta32r], tag=f"l2a_da{i}")
        pb2, pb2r = C.pb[5], C.pbr[5]
        P.op("tensor", lambda e, pb2=pb2: e.matmul(pb2[:, 0:8], trif[:, :], dta32[:, 0:8],
                                                   start=True, stop=True),
             reads=[trifr, dta32r], writes=[pb2r], tag=f"l2a_cum{i}")
        P.op("scalar", lambda e, i=i, pb2=pb2: e.copy(cA[:, i * 8:(i + 1) * 8], pb2[:, 0:8]),
             reads=[pb2r], writes=[cAr], tag=f"l2a_cae{i}")

    A.rewind(m1)
    xtm, xtmr = A.take(BF16, S, "l2a_xtm")
    LT32, LT32r = A.take(F32, 128, "l2a_LT32")
    LTe, LTer = A.take(F32, 128, "l2a_LTe")
    LTb, LTbr = A.take(BF16, 128, "l2a_LTb")
    Mbf, Mbfr = A.take(BF16, 128, "l2a_M")
    bsc, bscr = A.take(BF16, 128, "l2a_bsc")
    btm, btmr = A.take(BF16, 64, "l2a_btm")
    ecol, ecolr = A.take(F32, 8, "l2a_ec")
    scr1, scr1r = A.take(F32, 256, "l2a_s1")
    scr2, scr2r = A.take(F32, 256, "l2a_s2")
    ytp, ytpr = A.take(F32, 256, "l2a_ytp")
    P.op("vector", lambda e: e.memset(st32[0:64, :], 0.0), writes=[st32r], tag="l2a_st0")
    P.op("vector", lambda e: e.memset(stbf[:, :], 0.0), writes=[stbfr], tag="l2a_stb0")

    masklow = C.hc("l2neg", 128)
    for cc in range(NT):
        for oc in range(NO):
            p_ap = C.psbf[:, (oc % 8) * 128:(oc % 8) * 128 + 128]
            P.op("tensor", lambda e, oc=oc, cc=cc, p_ap=p_ap: e.transpose(
                p_ap, xin[:, oc * S + cc * 128: oc * S + (cc + 1) * 128], C.ident),
                reads=[xinr[oc], C.hcbr], writes=[C.psbfr], tag=f"l2a_xt{cc}_{oc}")
            P.op("vector", lambda e, oc=oc, p_ap=p_ap: e.tensor_copy(
                xtm[:, oc * 128:(oc + 1) * 128], p_ap),
                reads=[C.psbfr], writes=[xtmr], tag=f"l2a_xte{cc}_{oc}")
        for h in range(HEADS):
            hb = (h % 2) * 64
            hrow = slice(hb, hb + 64)
            hchunk = h // 2
            cacol = cA[:, cc * 8 + h: cc * 8 + h + 1]
            dtcol = dtt[:, cc * 8 + h: cc * 8 + h + 1]
            pb, pbr = C.pb[0], C.pbr[0]
            P.op("tensor", lambda e, cacol=cacol, pb=pb: e.transpose(pb[0:1, 0:128], cacol, idf[:, :]),
                 reads=[cAr, idfr], writes=[pbr], tag=f"l2a_rt{cc}_{h}")
            P.op("vector", lambda e, pb=pb: e.tensor_copy(carow[0:1, 0:128], pb[0:1, 0:128]),
                 reads=[pbr], writes=[carowr], tag=f"l2a_re{cc}_{h}")
            P.op("tensor", lambda e, pb=pb: e.matmul(pb[:, 0:128], ones32[0:1, :], carow[0:1, 0:128],
                                                     start=True, stop=True),
                 reads=[ones32r, carowr], writes=[pbr], tag=f"l2a_R{cc}_{h}")
            P.op("vector", lambda e, cacol=cacol, pb=pb: e.tensor_scalar(
                LT32[:, :], pb[:, 0:128], cacol, None, ALU.subtract),
                reads=[pbr, cAr], writes=[LT32r], tag=f"l2a_D{cc}_{h}")
            P.op("vector", lambda e: e.tensor_tensor(LT32[:, :], LT32[:, :], masklow, ALU.add),
                 reads=[LT32r, C.hcbr], writes=[LT32r], tag=f"l2a_Dm{cc}_{h}")
            P.op("scalar", lambda e: e.activation(LTe[:, :], LT32[:, :], AF.Exp),
                 reads=[LT32r], writes=[LTer], tag=f"l2a_De{cc}_{h}")
            P.op("vector", lambda e, dtcol=dtcol: e.tensor_scalar(LTb[:, :], LTe[:, :], dtcol, None, ALU.mult),
                 reads=[LTer, dttr], writes=[LTbr], tag=f"l2a_Ds{cc}_{h}")
            pg, pgr = C.pb[1], C.pbr[1]
            P.op("tensor", lambda e, hchunk=hchunk, hrow=hrow, cc=cc, pg=pg: e.matmul(
                pg[:, 0:128], bmat[hrow, hchunk * S + cc * 128: hchunk * S + (cc + 1) * 128],
                cmat[hrow, hchunk * S + cc * 128: hchunk * S + (cc + 1) * 128],
                start=True, stop=True),
                reads=[bmatr[hchunk], cmatr[hchunk]], writes=[pgr], tag=f"l2a_G{cc}_{h}")
            P.op("vector", lambda e, pg=pg: e.tensor_tensor(Mbf[:, :], pg[:, 0:128], LTb[:, :], ALU.mult),
                 reads=[pgr, LTbr], writes=[Mbfr], tag=f"l2a_M{cc}_{h}")
            P.op("scalar", lambda e, cacol=cacol: e.activation(ecol[:, 0:1], cacol, AF.Exp),
                 reads=[cAr], writes=[ecolr], tag=f"l2a_et{cc}_{h}")
            pb2, pb2r = C.pb[0], C.pbr[0]
            P.op("tensor", lambda e, pb2=pb2: e.matmul(pb2[:, 0:1], ones32[0:1, :], carow[0:1, 127:128],
                                                       start=True, stop=True),
                 reads=[ones32r, carowr], writes=[pb2r], tag=f"l2a_cl{cc}_{h}")
            P.op("vector", lambda e, pb2=pb2: e.tensor_copy(ecol[:, 1:2], pb2[:, 0:1]),
                 reads=[pb2r], writes=[ecolr], tag=f"l2a_cle{cc}_{h}")
            P.op("vector", lambda e, cacol=cacol: e.tensor_tensor(ecol[:, 2:3], ecol[:, 1:2], cacol, ALU.subtract),
                 reads=[ecolr, cAr], writes=[ecolr], tag=f"l2a_e2a{cc}_{h}")
            P.op("scalar", lambda e: e.activation(ecol[:, 2:3], ecol[:, 2:3], AF.Exp),
                 reads=[ecolr], writes=[ecolr], tag=f"l2a_e2{cc}_{h}")
            P.op("vector", lambda e, dtcol=dtcol: e.tensor_tensor(ecol[:, 3:4], ecol[:, 2:3], dtcol, ALU.mult),
                 reads=[ecolr, dttr], writes=[ecolr], tag=f"l2a_w{cc}_{h}")
            P.op("scalar", lambda e: e.activation(ecol[:, 4:5], ecol[:, 1:2], AF.Exp),
                 reads=[ecolr], writes=[ecolr], tag=f"l2a_lam{cc}_{h}")
            pA, pAr = C.pb[2], C.pbr[2]
            P.op("tensor", lambda e, h=h, pA=pA: e.matmul(
                pA[:, 0:256], Mbf[:, :], xtm[:, h * 256:(h + 1) * 256], start=True, stop=True),
                reads=[Mbfr, xtmr], writes=[pAr], tag=f"l2a_Yi{cc}_{h}")
            pB, pBr = C.pb[3], C.pbr[3]
            P.op("tensor", lambda e, h=h, hrow=hrow, hchunk=hchunk, cc=cc, pB=pB: e.matmul(
                pB[:, 0:256], cmat[hrow, hchunk * S + cc * 128: hchunk * S + (cc + 1) * 128],
                stbf[hrow, h * 256:(h + 1) * 256], start=True, stop=True),
                reads=[cmatr[hchunk], stbfr], writes=[pBr], tag=f"l2a_Ys{cc}_{h}")
            P.op("scalar", lambda e, pB=pB: e.activation(scr1[:, :], pB[:, 0:256], AF.Copy, scale=ecol[:, 0:1]),
                 reads=[pBr, ecolr], writes=[scr1r], tag=f"l2a_Ye{cc}_{h}")
            P.op("vector", lambda e, h=h: e.scalar_tensor_tensor(
                scr2[:, :], xtm[:, h * 256:(h + 1) * 256], dpar[:, h:h + 1], scr1[:, :], ALU.mult, ALU.add),
                reads=[xtmr, dparr, scr1r], writes=[scr2r], tag=f"l2a_Yd{cc}_{h}")
            P.op("vector", lambda e, pA=pA: e.tensor_tensor(ytp[:, :], pA[:, 0:256], scr2[:, :], ALU.add),
                 reads=[pAr, scr2r], writes=[ytpr], tag=f"l2a_Ya{cc}_{h}")
            P.op("vector", lambda e: e.reduce_sum(ecol[:, 5:6], ytp[:, :], axis=AX.X),
                 reads=[ytpr], writes=[ecolr], tag=f"l2a_gs{cc}_{h}")
            P.op("vector", lambda e, h=h: e.tensor_tensor(C.stat[:, 16 + h:17 + h], C.stat[:, 16 + h:17 + h],
                                                          ecol[:, 5:6], ALU.add),
                 reads=[ecolr, C.statr], writes=[C.statr], tag=f"l2a_gsa{cc}_{h}")
            P.op("scalar", lambda e: e.activation(scr1[:, :], ytp[:, :], AF.Square, accum_out=ecol[:, 6:7]),
                 reads=[ytpr], writes=[scr1r, ecolr], tag=f"l2a_gq{cc}_{h}")
            P.op("vector", lambda e, h=h: e.tensor_tensor(C.stat[:, 24 + h:25 + h], C.stat[:, 24 + h:25 + h],
                                                          ecol[:, 6:7], ALU.add),
                 reads=[ecolr, C.statr], writes=[C.statr], tag=f"l2a_gqa{cc}_{h}")
            for db in range(2):
                pT, pTr = C.pb[4], C.pbr[4]
                P.op("tensor", lambda e, db=db, pT=pT: e.transpose(
                    pT[:, 0:128], ytp[:, db * 128:(db + 1) * 128], idf[:, :]),
                    reads=[ytpr, idfr], writes=[pTr], tag=f"l2a_yt{cc}_{h}_{db}")
                P.op("vector", lambda e, pT=pT: e.tensor_copy(scr2[:, 0:128], pT[:, 0:128]),
                     reads=[pTr], writes=[scr2r], tag=f"l2a_yte{cc}_{h}_{db}")
                P.dma("gpsimd", yhd[2 * h + db, :, cc * 128:(cc + 1) * 128], scr2[:, 0:128],
                      reads=[scr2r], writes=[yhdr], tag=f"l2a_yst{cc}_{h}_{db}")
            P.op("vector", lambda e, hrow=hrow, hchunk=hchunk, cc=cc: e.tensor_copy(
                bsc[0:64, :], bmat[hrow, hchunk * S + cc * 128: hchunk * S + (cc + 1) * 128]),
                reads=[bmatr[hchunk]], writes=[bscr], tag=f"l2a_bc{cc}_{h}")
            pT, pTr = C.pb[4], C.pbr[4]
            P.op("tensor", lambda e, pT=pT: e.transpose(pT.bitcast(BF16)[:, 0:128][:, 0:64] if False else C.psbf[:, 0:64], bsc[0:64, :], C.ident[0:64, 0:64]),
                 reads=[bscr, C.hcbr], writes=[C.psbfr], tag=f"l2a_bt{cc}_{h}")
            P.op("vector", lambda e, cc=cc: e.tensor_scalar(btm[:, 0:64], C.psbf[:, 0:64], ecol[:, 3:4], None, ALU.mult),
                 reads=[C.psbfr, ecolr], writes=[btmr], tag=f"l2a_bw{cc}_{h}")
            pS, pSr = C.pb[5], C.pbr[5]
            P.op("tensor", lambda e, h=h, pS=pS: e.matmul(
                pS[0:64, 0:256], btm[:, 0:64], xtm[:, h * 256:(h + 1) * 256], start=True, stop=True),
                reads=[btmr, xtmr], writes=[pSr], tag=f"l2a_su{cc}_{h}")
            P.op("vector", lambda e, h=h: e.tensor_scalar(
                st32[0:64, h * 256:(h + 1) * 256], st32[0:64, h * 256:(h + 1) * 256], ecol[0:64, 4:5], None, ALU.mult),
                reads=[st32r, ecolr], writes=[st32r], tag=f"l2a_sl{cc}_{h}")
            P.op("vector", lambda e, h=h, pS=pS: e.tensor_tensor(
                st32[0:64, h * 256:(h + 1) * 256], st32[0:64, h * 256:(h + 1) * 256], pS[0:64, 0:256], ALU.add),
                reads=[st32r, pSr], writes=[st32r], tag=f"l2a_sa{cc}_{h}")
            P.op("scalar", lambda e, h=h, hb=hb: e.copy(
                stbf[hb:hb + 64, h * 256:(h + 1) * 256], st32[0:64, h * 256:(h + 1) * 256]),
                reads=[st32r], writes=[stbfr], tag=f"l2a_se{cc}_{h}")

    A.rewind(m0)
    outfm, outfmr = A.take_tiles(BF16, S, NO, "l2a_of")
    woT, woTr = A.take_tiles(BF16, 1024, NO, "l2a_woT")
    ych, ychr = A.take(F32, S, "l2a_ych")
    zch, zchr = A.take(BF16, S, "l2a_zch")
    gnp, gnpr = A.take(F32, 2 * NO, "l2a_gnp")
    grow, growr = A.take(F32, 144, "l2a_grow")
    gcol, gcolr = A.take(F32, 8, "l2a_gcol")
    wst3, wst3r = A.take(F32, 2048, "l2a_wst3")
    wbf3, wbf3r = A.take(BF16, 2048, "l2a_wbf3")
    res, resr = A.take(F32, 512, "l2a_res")
    idf2, idf2r = A.take(F32, 128, "l2a_idf2")
    P.op("vector", lambda e: e.tensor_copy(idf2[:, :], C.ident),
         reads=[C.hcbr], writes=[idf2r], tag="l2a_idf2")

    for which, nm in ((0, "gn_w"), (1, "gn_b")):
        P.dma("sync", wst3[0:16, 0:128], W[f"L2_attn_{nm}"].rearrange("(c p) -> c p", p=128),
              writes=[wst3r], tag=f"l2a_{nm}l")
        pb, pbr = C.pb[0], C.pbr[0]
        P.op("tensor", lambda e, pb=pb: e.transpose(pb[:, 0:16], wst3[0:16, 0:128], idf2[0:16, 0:16]),
             reads=[wst3r, idf2r], writes=[pbr], tag=f"l2a_{nm}t")
        P.op("vector", lambda e, which=which, pb=pb: e.tensor_copy(
            gnp[:, which * NO:(which + 1) * NO], pb[:, 0:16]),
            reads=[pbr], writes=[gnpr], tag=f"l2a_{nm}e")
    gnd = nc.dram_tensor("l2a_gnd", [16, 128], F32).ap()
    gndr = Region("l2a_gnd")
    pbg, pbgr = C.pb[1], C.pbr[1]
    P.op("tensor", lambda e: e.transpose(pbg[0:16, 0:128], C.stat[:, 16:32], idf2[:, :]),
         reads=[C.statr, idf2r], writes=[pbgr], tag="l2a_gstT")
    P.op("vector", lambda e: e.tensor_copy(grow[0:16, 0:128], pbg[0:16, 0:128]),
         reads=[pbgr], writes=[growr], tag="l2a_gstE")
    P.dma("gpsimd", gnd, grow[0:16, 0:128], reads=[growr], writes=[gndr], tag="l2a_gst")
    for h in range(HEADS):
        P.dma("sync", grow[0:1, 0:128], gnd[h][None], reads=[gndr], writes=[growr], tag=f"l2a_gl{h}")
        P.op("vector", lambda e: e.reduce_sum(gcol[0:1, 0:1], grow[0:1, 0:128], axis=AX.X),
             reads=[growr], writes=[gcolr], tag=f"l2a_gr{h}")
        P.dma("sync", grow[0:1, 0:128], gnd[8 + h][None], reads=[gndr], writes=[growr], tag=f"l2a_gl2{h}")
        P.op("vector", lambda e: e.reduce_sum(gcol[0:1, 1:2], grow[0:1, 0:128], axis=AX.X),
             reads=[growr], writes=[gcolr], tag=f"l2a_gr2{h}")
        NE = float(S * 256)
        P.op("vector", lambda e: e.tensor_scalar(gcol[0:1, 2:3], gcol[0:1, 0:1], 1.0 / NE, None, ALU.mult),
             reads=[gcolr], writes=[gcolr], tag=f"l2a_gm{h}")
        P.op("vector", lambda e: e.tensor_scalar(gcol[0:1, 3:4], gcol[0:1, 1:2], 1.0 / NE, None, ALU.mult),
             reads=[gcolr], writes=[gcolr], tag=f"l2a_gq{h}")
        P.op("vector", lambda e: e.tensor_tensor(gcol[0:1, 4:5], gcol[0:1, 2:3], gcol[0:1, 2:3], ALU.mult),
             reads=[gcolr], writes=[gcolr], tag=f"l2a_gm2{h}")
        P.op("vector", lambda e: e.tensor_tensor(gcol[0:1, 3:4], gcol[0:1, 3:4], gcol[0:1, 4:5], ALU.subtract),
             reads=[gcolr], writes=[gcolr], tag=f"l2a_gv{h}")
        ceps = P.const_col(1e-5)
        cnh = P.const_col(-0.5)
        P.op("scalar", lambda e, ceps=ceps: e.activation(gcol[0:1, 3:4], gcol[0:1, 3:4], AF.Ln, bias=ceps[0:1, :]),
             reads=[gcolr, P.consts_region], writes=[gcolr], tag=f"l2a_gvl{h}")
        P.op("scalar", lambda e, cnh=cnh: e.activation(gcol[0:1, 3:4], gcol[0:1, 3:4], AF.Exp, scale=cnh[0:1, :]),
             reads=[gcolr, P.consts_region], writes=[gcolr], tag=f"l2a_gvr{h}")
        P.dma("gpsimd", misc[2, 2 * h:2 * h + 2][None], gcol[0:1, 2:4], reads=[gcolr],
              writes=[miscr], tag=f"l2a_gms{h}")
    for oc in range(NO):
        h = oc // 2
        P.dma("sync", ych[:, :], yhd[oc], reads=[yhdr], writes=[ychr], tag=f"l2a_yl{oc}")
        P.dma("gpsimd", gcol[:, 5:6], misc[2, 2 * h:2 * h + 1][None].broadcast_to([128, 1]),
              reads=[miscr], writes=[gcolr], tag=f"l2a_mb{oc}")
        P.dma("gpsimd", gcol[:, 6:7], misc[2, 2 * h + 1:2 * h + 2][None].broadcast_to([128, 1]),
              reads=[miscr], writes=[gcolr], tag=f"l2a_rb{oc}")
        P.op("vector", lambda e, oc=oc: e.tensor_scalar(
            ych[:, :], ych[:, :], gcol[:, 5:6], gcol[:, 6:7], ALU.subtract, ALU.mult),
            reads=[ychr, gcolr], writes=[ychr], tag=f"l2a_gn1{oc}")
        P.op("vector", lambda e, oc=oc: e.tensor_scalar(
            ych[:, :], ych[:, :], gnp[:, oc:oc + 1], gnp[:, NO + oc:NO + oc + 1], ALU.mult, ALU.add),
            reads=[ychr, gnpr], writes=[ychr], tag=f"l2a_gn2{oc}")
        P.dma("sync", zch[:, :], zd[oc], reads=[zdr], writes=[zchr], tag=f"l2a_zl{oc}")
        P.op("vector", lambda e, oc=oc: e.tensor_tensor(
            outfm[:, oc * S:(oc + 1) * S], ych[:, :], zch[:, :], ALU.mult),
            reads=[ychr, zchr], writes=[outfmr[oc]], tag=f"l2a_og{oc}")
    for dc in range(NC):
        P.dma("sync", wst3[:, :], W["L2_attn_Wout"][dc * 128:(dc + 1) * 128, :],
              writes=[wst3r], tag=f"l2a_wol{dc}")
        P.op("vector", lambda e: e.tensor_copy(wbf3[:, :], wst3[:, :]),
             reads=[wst3r], writes=[wbf3r], tag=f"l2a_woc{dc}")
        for oc in range(NO):
            p_ap = C.psbf[:, (oc % 8) * 128:(oc % 8) * 128 + 128]
            P.op("tensor", lambda e, oc=oc, p_ap=p_ap: e.transpose(
                p_ap, wbf3[:, oc * 128:(oc + 1) * 128], C.ident),
                reads=[wbf3r, C.hcbr], writes=[C.psbfr], tag=f"l2a_wot{dc}_{oc}")
            P.op("scalar", lambda e, oc=oc, dc=dc, p_ap=p_ap: e.copy(
                woT[:, oc * 1024 + dc * 128: oc * 1024 + (dc + 1) * 128], p_ap),
                reads=[C.psbfr], writes=[woTr[oc]], tag=f"l2a_wote{dc}_{oc}")
    for tg in range(8):
        for ichunk in range(NO):
            pass
        for tt in range(2):
            for hh in range(2):
                bank = tt * 2 + hh
                pb, pbr = C.pb[2 + bank], C.pbr[2 + bank]

                def go(e, tg=tg, tt=tt, hh=hh, pb=pb):
                    last = None
                    for oc in range(NO):
                        last = e.matmul(
                            pb[:, :], outfm[:, oc * S + tg * 256 + tt * 128: oc * S + tg * 256 + (tt + 1) * 128],
                            woT[:, oc * 1024 + hh * 512: oc * 1024 + (hh + 1) * 512],
                            start=(oc == 0), stop=(oc == NO - 1))
                    return last
                P.op("tensor", go, reads=woTr + outfmr, writes=[pbr], tag=f"l2a_om{tg}_{tt}_{hh}")
                xtile = (tg * 256 + tt * 128) // 128
                x_ap = C.X[:, xtile * 1024 + hh * 512: xtile * 1024 + hh * 512 + 512]
                P.op("vector", lambda e, x_ap=x_ap, pb=pb, xtile=xtile: e.tensor_tensor(
                    x_ap, x_ap, pb[:, :], ALU.add),
                    reads=[pbr, C.Xr[xtile]], writes=[C.Xr[xtile]], tag=f"l2a_x{tg}_{tt}_{hh}")
    P.op("vector", lambda e: e.memset(C.stat[:, 16:48], 0.0), writes=[C.statr], tag="l2a_statz")
    C.arena.reset()



def emit_l1_attn(C: Core, W):
    P = C.P
    A = C.arena
    nc = P.nc
    DSTATE, DTRANK = 16, 64
    m0 = A.mark()
    emit_layernorm(C)

    bcd = nc.dram_tensor("l1a_bc", [32, S], F32).ap()
    bcdr = Region("l1a_bcd")
    drd = nc.dram_tensor("l1a_dr", [64, S], BF16).ap()
    drdr = Region("l1a_drd")

    wst, wstr = A.take(F32, 1024, "l1a_wst")
    wbf, wbfr = A.take(BF16, 1024, "l1a_wbf")
    xpwT, xpwTr = A.take(BF16, 8 * 96, "l1a_xpwT")
    drank, drankr = A.take(BF16, S, "l1a_drank")
    bc32, bc32r = A.take(F32, S, "l1a_bc32")
    P.dma("sync", wst[0:96, :], W["L1_attn_x_proj_w"], writes=[wstr], tag="l1a_xpl")
    P.op("vector", lambda e: e.tensor_copy(wbf[0:96, :], wst[0:96, :]),
         reads=[wstr], writes=[wbfr], tag="l1a_xpc")
    for c in range(NC):
        p_ap = C.psbf[:, (c % 8) * 128:(c % 8) * 128 + 128]
        P.op("tensor", lambda e, c=c, p_ap=p_ap: e.transpose(
            p_ap[:, 0:96], wbf[0:96, c * 128:(c + 1) * 128], C.ident[0:96, 0:96]),
            reads=[wbfr, C.hcbr], writes=[C.psbfr], tag=f"l1a_xpt{c}")
        P.op("scalar", lambda e, c=c, p_ap=p_ap: e.copy(xpwT[:, c * 96:(c + 1) * 96], p_ap[:, 0:96]),
             reads=[C.psbfr], writes=[xpwTr], tag=f"l1a_xpte{c}")
    for n0 in range(0, S, 512):
        pb, pbr = C.pb[0], C.pbr[0]

        def gxp(e, n0=n0, pb=pb):
            last = None
            for c in range(NC):
                last = e.matmul(pb[0:96, :], xpwT[:, c * 96:(c + 1) * 96],
                                C.ln_fm[:, c * S + n0: c * S + n0 + 512],
                                start=(c == 0), stop=(c == NC - 1))
            return last
        P.op("tensor", gxp, reads=[xpwTr] + C.ln_fmr, writes=[pbr], tag=f"l1a_xpm{n0}")
        P.op("scalar", lambda e, n0=n0, pb=pb: e.copy(drank[0:64, n0:n0 + 512], pb[0:64, :]),
             reads=[pbr], writes=[drankr], tag=f"l1a_dre{n0}")
        P.op("vector", lambda e, n0=n0, pb=pb: e.tensor_copy(bc32[0:32, n0:n0 + 512], pb[64:96, :]),
             reads=[pbr], writes=[bc32r], tag=f"l1a_bce{n0}")
    P.dma("gpsimd", bcd, bc32[0:32, :], reads=[bc32r], writes=[bcdr], tag="l1a_bcs")
    P.dma("gpsimd", drd, drank[0:64, :], reads=[drankr], writes=[drdr], tag="l1a_drs")
    A.rewind(m0)
    drank2, drank2r = A.take(BF16, S, "l1a_drank2")
    P.dma("sync", drank2[0:64, :], drd, reads=[drdr], writes=[drank2r], tag="l1a_drl")
    wst2, wst2r = A.take(F32, 1024, "l1a_wst2")
    wbf2, wbf2r = A.take(BF16, 1024, "l1a_wbf2")

    dtwT, dtwTr = A.take(BF16, 8 * 128, "l1a_dtwT")
    dtb, dtbr = A.take(F32, 8, "l1a_dtb")
    for c in range(NC):
        P.dma("sync", wst2[:, 0:64], W["L1_attn_dt_proj_w"][c * 128:(c + 1) * 128, :],
              writes=[wst2r], tag=f"l1a_dtl{c}")
        P.op("vector", lambda e: e.tensor_copy(wbf2[:, 0:64], wst2[:, 0:64]),
             reads=[wst2r], writes=[wbf2r], tag=f"l1a_dtc{c}")
        p_ap = C.psbf[:, 0:128]
        P.op("tensor", lambda e, p_ap=p_ap: e.transpose(p_ap[0:64, 0:128], wbf2[:, 0:64], C.ident),
             reads=[wbf2r, C.hcbr], writes=[C.psbfr], tag=f"l1a_dtt{c}")
        P.op("scalar", lambda e, c=c, p_ap=p_ap: e.copy(dtwT[0:64, c * 128:(c + 1) * 128], p_ap[0:64, 0:128]),
             reads=[C.psbfr], writes=[dtwTr], tag=f"l1a_dtte{c}")
    P.dma("sync", wst2[0:8, 0:128], W["L1_attn_dt_proj_b"].rearrange("(c p) -> c p", p=128),
          writes=[wst2r], tag="l1a_dtbl")
    idf, idfr = A.take(F32, 128, "l1a_idf")
    P.op("vector", lambda e: e.tensor_copy(idf[:, :], C.ident),
         reads=[C.hcbr], writes=[idfr], tag="l1a_idf")
    pb, pbr = C.pb[1], C.pbr[1]
    P.op("tensor", lambda e, pb=pb: e.transpose(pb[:, 0:8], wst2[0:8, 0:128], idf[0:8, 0:8]),
         reads=[wst2r, idfr], writes=[pbr], tag="l1a_dtbt")
    P.op("vector", lambda e, pb=pb: e.tensor_copy(dtb[:, :], pb[:, 0:8]),
         reads=[pbr], writes=[dtbr], tag="l1a_dtbe")
    acol, acolr = A.take(F32, 128, "l1a_acol")
    P.dma("sync", wst2[:, 0:128], W["L1_attn_A_log"].rearrange("(g p) n -> g (p n)", p=8),
          writes=[wst2r], tag="l1a_al")
    pb, pbr = C.pb[1], C.pbr[1]
    P.op("tensor", lambda e, pb=pb: e.transpose(pb[:, 0:128], wst2[:, 0:128], idf[:, :]),
         reads=[wst2r, idfr], writes=[pbr], tag="l1a_at")
    P.op("scalar", lambda e, pb=pb: e.activation(acol[:, :], pb[:, 0:128], AF.Exp),
         reads=[pbr], writes=[acolr], tag="l1a_ae")
    P.op("vector", lambda e: e.tensor_scalar_mul(acol[:, :], acol[:, :], -1.0),
         reads=[acolr], writes=[acolr], tag="l1a_an")
    dcolv, dcolvr = A.take(F32, 8, "l1a_dcol")
    P.dma("sync", wst2[0:8, 0:128], W["L1_attn_D"].rearrange("(c p) -> c p", p=128),
          writes=[wst2r], tag="l1a_dl")
    pb, pbr = C.pb[1], C.pbr[1]
    P.op("tensor", lambda e, pb=pb: e.transpose(pb[:, 0:8], wst2[0:8, 0:128], idf[0:8, 0:8]),
         reads=[wst2r, idfr], writes=[pbr], tag="l1a_dt2")
    P.op("vector", lambda e, pb=pb: e.tensor_copy(dcolv[:, :], pb[:, 0:8]),
         reads=[pbr], writes=[dcolvr], tag="l1a_de")

    btile, btiler = A.take(F32, S, "l1a_bt")
    ctile, ctiler = A.take(F32, S, "l1a_ct")
    P.dma("gpsimd", btile[:, :], bcd[0:16][None].broadcast_to([8, 16, S]),
          reads=[bcdr], writes=[btiler], tag="l1a_btl")
    P.dma("gpsimd", ctile[:, :], bcd[16:32][None].broadcast_to([8, 16, S]),
          reads=[bcdr], writes=[ctiler], tag="l1a_ctl")

    x32c, x32cr = A.take(F32, S, "l1a_x32")
    lnst, lnstr = A.take(F32, 1024, "l1a_lnst")
    delta, deltar = A.take(F32, S, "l1a_delta")
    deltab, deltabr = A.take(BF16, S, "l1a_deltab")
    ubf, ubfr = A.take(BF16, S, "l1a_ubf")
    dA, dAr = A.take(F32, S, "l1a_dA")
    dBu, dBur = A.take(F32, S, "l1a_dBu")
    hsc, hscr = A.take(F32, S, "l1a_h")
    hcb_, hcb_r = A.take(BF16, S, "l1a_hc")
    ysc, yscr = A.take(F32, S, "l1a_y")
    ytt, yttr = A.take(F32, 128, "l1a_ytt")
    sp1, sp1r = A.take(F32, S, "l1a_sp1")
    cm1 = P.const_col(-1.0)
    cp1 = P.const_col(1.0)

    for c in range(NC):
        for i in range(NT):
            emit_ln32_tile(C, i, lnst[:, :], lnstr)
            pb, pbr = C.pb[1], C.pbr[1]
            P.op("tensor", lambda e, c=c, pb=pb: e.transpose(
                pb[:, 0:128], lnst[:, c * 128:(c + 1) * 128], idf[:, :]),
                reads=[lnstr, idfr], writes=[pbr], tag=f"l1a_x32t{c}_{i}")
            P.op("vector", lambda e, i=i, pb=pb: e.tensor_copy(
                x32c[:, i * 128:(i + 1) * 128], pb[:, 0:128]),
                reads=[pbr], writes=[x32cr], tag=f"l1a_x32e{c}_{i}")
        for n0 in range(0, S, 512):
            pb, pbr = C.pb[2], C.pbr[2]
            P.op("tensor", lambda e, c=c, n0=n0, pb=pb: e.matmul(
                pb[:, :], dtwT[0:64, c * 128:(c + 1) * 128], drank2[0:64, n0:n0 + 512],
                start=True, stop=True),
                reads=[dtwTr, drank2r], writes=[pbr], tag=f"l1a_dm{c}_{n0}")
            P.op("vector", lambda e, n0=n0, pb=pb, c=c: e.tensor_scalar(
                delta[:, n0:n0 + 512], pb[:, :], dtb[:, c:c + 1], None, ALU.add),
                reads=[pbr, dtbr], writes=[deltar], tag=f"l1a_db{c}_{n0}")
        P.op("vector", lambda e: e.tensor_scalar_mul(sp1[:, :], delta[:, :], -1.0),
             reads=[deltar], writes=[sp1r], tag=f"l1a_sp1{c}")
        P.op("vector", lambda e: e.tensor_tensor(sp1[:, :], sp1[:, :], delta[:, :], ALU.min),
             reads=[sp1r, deltar], writes=[sp1r], tag=f"l1a_sp1b{c}")
        P.op("scalar", lambda e: e.activation(sp1[:, :], sp1[:, :], AF.Exp),
             reads=[sp1r], writes=[sp1r], tag=f"l1a_sp2{c}")
        P.op("scalar", lambda e: e.activation(sp1[:, :], sp1[:, :], AF.Ln, bias=cp1),
             reads=[sp1r, P.consts_region], writes=[sp1r], tag=f"l1a_sp3{c}")
        P.op("vector", lambda e: e.tensor_scalar(delta[:, :], delta[:, :], 0.0, None, ALU.max),
             reads=[deltar], writes=[deltar], tag=f"l1a_sp4{c}")
        P.op("vector", lambda e: e.tensor_tensor(delta[:, :], delta[:, :], sp1[:, :], ALU.add),
             reads=[deltar, sp1r], writes=[deltar], tag=f"l1a_sp5{c}")
        P.op("vector", lambda e: e.tensor_copy(deltab[:, :], delta[:, :]),
             reads=[deltar], writes=[deltabr], tag=f"l1a_dc{c}")
        P.op("gpsimd", lambda e: e.tensor_tensor(ubf[:, :], delta[:, :], x32c[:, :], ALU.mult),
             reads=[deltar, x32cr], writes=[ubfr], tag=f"l1a_u{c}")
        P.op("vector", lambda e: e.memset(ysc[:, :], 0.0), writes=[yscr], tag=f"l1a_y0{c}")
        for g16 in range(16):
            g = c * 16 + g16
            sel = C.hc(f"l1sel{g16}", 128)
            prd, prdr = C.pb[3], C.pbr[3]
            pru, prur = C.pb[4], C.pbr[4]
            for q in range(4):
                P.op("tensor", lambda e, q=q, prd=prd, sel=sel: e.matmul(
                    prd[:, :], sel, deltab[:, q * 512:(q + 1) * 512], start=True, stop=True),
                    reads=[deltabr, C.hcbr], writes=[prdr], tag=f"l1a_rd{g}_{q}")
                P.op("scalar", lambda e, q=q, g=g, prd=prd: e.activation(
                    dA[:, q * 512:(q + 1) * 512], prd[:, :], AF.Exp, scale=acol[:, g:g + 1]),
                    reads=[prdr, acolr], writes=[dAr], tag=f"l1a_dA{g}_{q}")
                P.op("tensor", lambda e, q=q, pru=pru, sel=sel: e.matmul(
                    pru[:, :], sel, ubf[:, q * 512:(q + 1) * 512], start=True, stop=True),
                    reads=[ubfr, C.hcbr], writes=[prur], tag=f"l1a_ru{g}_{q}")
                P.op("vector", lambda e, q=q, pru=pru: e.tensor_tensor(
                    dBu[:, q * 512:(q + 1) * 512], pru[:, :], btile[:, q * 512:(q + 1) * 512], ALU.mult),
                    reads=[prur, btiler], writes=[dBur], tag=f"l1a_dBu{g}_{q}")
            P.op("vector", lambda e: e.tensor_tensor_scan(
                hsc[:, :], dA[:, :], dBu[:, :], 0.0, ALU.mult, ALU.add),
                reads=[dAr, dBur], writes=[hscr], tag=f"l1a_scan{g}")
            P.op("gpsimd", lambda e: e.tensor_tensor(hcb_[:, :], hsc[:, :], ctile[:, :], ALU.mult),
                 reads=[hscr, ctiler], writes=[hcb_r], tag=f"l1a_hc{g}")
            m2 = C.hc(f"l1m2_{g16 % 4}", 32)
            for q in range(4):
                pyb, pybr = C.pb[5], C.pbr[5]
                P.op("tensor", lambda e, q=q, m2=m2, pyb=pyb, g16=g16: e.matmul(
                    pyb[32 * q:32 * q + 32, :], m2, hcb_[:, q * 512:(q + 1) * 512],
                    start=(g16 % 4 == 0), stop=(g16 % 4 == 3),
                    tile_position=(0, 32 * q), skip_group_check=True),
                    reads=[hcb_r, C.hcbr], writes=[pybr], tag=f"l1a_yr{g}_{q}")
                if g16 % 4 == 3:
                    rw = 32 * (g16 // 4)
                    srcap = (lambda qq: C.pb[5].rearrange("p n -> p n"))(q) if False else None
                    P.op("vector", lambda e, q=q, rw=rw: e.tensor_copy(
                        ysc[rw:rw + 32, q * 512:(q + 1) * 512],
                        C.pb[5].rearrange("(a b) n -> a b n", a=4)[q] if False else C.pb[5][32 * q:32 * q + 32, :]),
                        reads=[C.pbr[5]], writes=[yscr], tag=f"l1a_ye{g}_{q}")
        P.op("vector", lambda e, c=c: e.scalar_tensor_tensor(
            ysc[:, :], x32c[:, :], dcolv[:, c:c + 1], ysc[:, :], ALU.mult, ALU.add),
            reads=[x32cr, dcolvr, yscr], writes=[yscr], tag=f"l1a_yd{c}")
        for i in range(NT):
            pb, pbr = C.pb[1], C.pbr[1]
            P.op("tensor", lambda e, i=i, pb=pb: e.transpose(
                pb[:, 0:128], ysc[:, i * 128:(i + 1) * 128], idf[:, :]),
                reads=[yscr, idfr], writes=[pbr], tag=f"l1a_yt{c}_{i}")
            P.op("vector", lambda e, pb=pb: e.tensor_copy(ytt[:, :], pb[:, 0:128]),
                 reads=[pbr], writes=[yttr], tag=f"l1a_yte{c}_{i}")
            x_ap = C.X[:, i * 1024 + c * 128: i * 1024 + (c + 1) * 128]
            P.op("vector", lambda e, x_ap=x_ap: e.tensor_tensor(x_ap, x_ap, ytt[:, :], ALU.add),
                 reads=[yttr, C.Xr[i]], writes=[C.Xr[i]], tag=f"l1a_xa{c}_{i}")
    C.arena.reset()



def emit_l3_ffn(C: Core, W):
    P = C.P
    A = C.arena
    nc = P.nc
    H, DH, HID = 8, 64, 2730
    NAC = 22
    emit_layernorm(C, fm=False)

    yd = nc.dram_tensor("l3f_y", [NT, 128, 1024], F32).ap()
    ydr = [Region(f"l3f_yd{i}") for i in range(NT)]
    rot_ap = nc.dram_tensor("l3f_rot", [128, 2 * S], F32, kind="ExternalInput").ap()
    m0 = A.mark()
    lnst0, lnst0r = A.take(F32, 1024, "l3f_lnst")
    for i in range(NT):
        emit_ln32_tile(C, i, lnst0[:, :], lnst0r)
        P.dma("gpsimd", yd[i], lnst0[:, :], reads=[lnst0r], writes=[ydr[i]], tag=f"l3f_y0{i}")
    A.rewind(m0)

    rott, rottr = A.take(BF16, 2 * S, "l3f_rot")
    P.dma("gpsimd", rott[:, :], rot_ap, writes=[rottr], tag="l3f_rotl")
    w1d = nc.dram_tensor("l3f_w1d", [2, 1024, 5460], BF16).ap()
    w1dr = Region("l3f_w1d")
    w2d = nc.dram_tensor("l3f_w2d", [2, HID, 1024], BF16).ap()
    w2dr = Region("l3f_w2d")
    wqd = nc.dram_tensor("l3f_wqd", [2, 1024, 1536], BF16).ap()
    wqdr = Region("l3f_wqd")
    wod = nc.dram_tensor("l3f_wod", [2, 512, 1024], BF16).ap()
    wodr = Region("l3f_wod")
    for lc in range(2):
        P.dma("gpsimd", wqd[lc], W["L3_ffn_wqkv"][lc], writes=[wqdr], tag=f"l3f_cq{lc}")
        P.dma("gpsimd", wod[lc], W["L3_ffn_wo"][lc], writes=[wodr], tag=f"l3f_co{lc}")
        for j0 in range(0, 5460, 1820):
            P.dma("gpsimd", w1d[lc][:, j0:j0 + 1820], W["L3_ffn_w1"][lc][:, j0:j0 + 1820],
                  writes=[w1dr], tag=f"l3f_c1{lc}_{j0}")
        P.dma("gpsimd", w2d[lc], W["L3_ffn_w2"][lc], writes=[w2dr], tag=f"l3f_c2{lc}")
    fcell, fcellr = A.take(F32, 1, "l3f_fence")
    idf, idfr = A.take(F32, 8, "l3f_idf")
    P.op("vector", lambda e: e.tensor_copy(idf[:, :], C.ident[:, 0:8]),
         reads=[C.hcbr], writes=[idfr], tag="l3f_idf")
    nh = P.const_col(-0.5)
    cpe = P.const_col(1e-5)

    def build_shift_fm(l, hfm, hfmr, ybufs, tag):
        for c in range(4, NC):
            P.op("vector", lambda e, c=c: e.memset(hfm[:, c * S:c * S + 1], 0.0),
                 writes=[hfmr[c]], tag=f"{tag}z{l}_{c}")
        nb = len(ybufs)
        for i in range(NT):
            ytile, ytiler, ybf, ybfr = ybufs[i % nb]
            P.dma("sync", ytile[:, :], yd[i], reads=[ydr[i]], writes=[ytiler], tag=f"{tag}l{l}_{i}")
            P.op("vector", lambda e, ytile=ytile, ybf=ybf: e.tensor_copy(ybf[:, :], ytile[:, :]),
                 reads=[ytiler], writes=[ybfr], tag=f"{tag}c{l}_{i}")
            for c in range(NC):
                p_ap = C.psbf[:, (c % 8) * 128:(c % 8) * 128 + 128]
                P.op("tensor", lambda e, c=c, p_ap=p_ap, ybf=ybf: e.transpose(
                    p_ap, ybf[:, c * 128:(c + 1) * 128], C.ident),
                    reads=[ybfr, C.hcbr], writes=[C.psbfr], tag=f"{tag}t{l}_{i}_{c}")
                if c < 4:
                    P.op("vector", lambda e, c=c, i=i, p_ap=p_ap: e.tensor_copy(
                        hfm[:, c * S + i * 128: c * S + (i + 1) * 128], p_ap),
                        reads=[C.psbfr], writes=[hfmr[c]], tag=f"{tag}e{l}_{i}_{c}")
                elif i < NT - 1:
                    P.op("vector", lambda e, c=c, i=i, p_ap=p_ap: e.tensor_copy(
                        hfm[:, c * S + i * 128 + 1: c * S + (i + 1) * 128 + 1], p_ap),
                        reads=[C.psbfr], writes=[hfmr[c]], tag=f"{tag}e{l}_{i}_{c}")
                else:
                    P.op("vector", lambda e, c=c, i=i, p_ap=p_ap: e.tensor_copy(
                        hfm[:, c * S + i * 128 + 1: (c + 1) * S], p_ap[:, 0:127]),
                        reads=[C.psbfr], writes=[hfmr[c]], tag=f"{tag}e{l}_{i}_{c}")

    for l in range(2):
        m1 = A.mark()
        qkv, qkvr = A.take_tiles(BF16, S, 12, "l3f_qkv")
        vtm, vtmr = A.take(BF16, NT * 512, "l3f_vtm")
        att, attr = A.take(BF16, NT * 512, "l3f_att")
        wqs2 = [A.take(BF16, NC * 128, "l3f_wqs")]
        mh = A.mark()
        hfm, hfmr = A.take_tiles(BF16, S, NC, "l3f_hfm")
        ybufs1 = [A.take(F32, 1024, "l3f_yt") + A.take(BF16, 1024, "l3f_ybf")]
        build_shift_fm(l, hfm, hfmr, ybufs1, "l3f_h")
        for mc in range(12):
            wqs, wqsr = wqs2[0]
            P.dma("sync", wqs.rearrange("p (c n) -> p c n", c=NC),
                  wqd[l][:, mc * 128:(mc + 1) * 128].rearrange("(c p) n -> p c n", p=128),
                  reads=[wqdr], writes=[wqsr], tag=f"l3f_wq{l}_{mc}")
            for n0 in range(0, S, 512):
                bank = 2 * (mc % 2) + (n0 // 512) % 2
                pb, pbr = C.pb[bank], C.pbr[bank]

                def gq(e, mc=mc, n0=n0, pb=pb, hfm=hfm, wqs=wqs):
                    last = None
                    for c in range(NC):
                        last = e.matmul(pb[:, :], wqs[:, c * 128:(c + 1) * 128],
                                        hfm[:, c * S + n0: c * S + n0 + 512],
                                        start=(c == 0), stop=(c == NC - 1))
                    return last
                P.op("tensor", gq, reads=[wqsr] + hfmr, writes=[pbr], tag=f"l3f_qm{l}_{mc}_{n0}")
                P.op("scalar", lambda e, mc=mc, n0=n0, pb=pb: e.copy(
                    qkv[:, mc * S + n0: mc * S + n0 + 512], pb[:, :]),
                    reads=[pbr], writes=[qkvr[mc]], tag=f"l3f_qe{l}_{mc}_{n0}")
        for i in range(NT):
            for vc in range(4):
                p_ap = C.psbf[:, (vc % 8) * 128:(vc % 8) * 128 + 128]
                P.op("tensor", lambda e, i=i, vc=vc, p_ap=p_ap: e.transpose(
                    p_ap, qkv[:, (8 + vc) * S + i * 128: (8 + vc) * S + (i + 1) * 128], C.ident),
                    reads=[qkvr[8 + vc], C.hcbr], writes=[C.psbfr], tag=f"l3f_vt{l}_{i}_{vc}")
                P.op("vector", lambda e, i=i, vc=vc, p_ap=p_ap: e.tensor_copy(
                    vtm[:, i * 512 + vc * 128: i * 512 + (vc + 1) * 128], p_ap),
                    reads=[C.psbfr], writes=[vtmr], tag=f"l3f_ve{l}_{i}_{vc}")
        A.rewind(mh)
        mh2 = A.mark()
        qsc, qscr = A.take(F32, 4, "l3f_qs")
        P.dma("sync", qsc[0:64, 0:1], W["L3_ffn_q_scale"][l].rearrange("(n o) -> n o", o=1),
              writes=[qscr], tag=f"l3f_qsl{l}")
        P.dma("sync", qsc[0:64, 1:2], W["L3_ffn_k_scale"][l].rearrange("(n o) -> n o", o=1),
              writes=[qscr], tag=f"l3f_ksl{l}")
        qh, qhr = A.take(BF16, S, "l3f_qh")
        kh, khr = A.take(BF16, S, "l3f_kh")
        sq2, sq2r = A.take(BF16, S, "l3f_sq2")
        nrm, nrmr = A.take(F32, S, "l3f_nrm")
        rnr, rnrr = A.take(BF16, S, "l3f_rnr")
        rotb, rotbr = A.take(BF16, S, "l3f_rotb")
        pbuf2 = [A.take(BF16, 5 * 128, f"l3f_pbuf{d}") for d in range(2)]
        pT2 = [A.take(BF16, 128, f"l3f_pT{d}") for d in range(2)]
        mcol2 = [A.take(F32, 8, f"l3f_mcol{d}") for d in range(2)]
        dnegm, dnegmr = A.take(BF16, 128, "l3f_dneg")
        onegm, onegmr = A.take(BF16, 128, "l3f_oneg")
        triu01 = C.hc("l0m0", 512)[:, 0:128]
        P.op("vector", lambda e: e.tensor_tensor(dnegm[:, :], C.ident, triu01, ALU.subtract),
             reads=[C.hcbr], writes=[dnegmr], tag=f"l3f_dn1{l}")
        P.op("vector", lambda e: e.tensor_scalar_mul(dnegm[:, :], dnegm[:, :], 1e30),
             reads=[dnegmr], writes=[dnegmr], tag=f"l3f_dn2{l}")
        P.op("vector", lambda e: e.tensor_scalar(onegm[:, :], triu01, -1.0, 1e30, ALU.add, ALU.mult),
             reads=[C.hcbr], writes=[onegmr], tag=f"l3f_on1{l}")
        psch = [Region(f"l3f_psA{l}"), Region(f"l3f_psB{l}")]
        P.op("vector", lambda e: e.memset(fcell[0:1, 0:1], 0.0),
             reads=[C.psbfr], writes=[psch[0], fcellr], tag=f"l3f_fiA{l}")
        P.op("vector", lambda e: e.memset(fcell[0:1, 0:1], 0.0),
             reads=[C.psbfr], writes=[psch[1], fcellr], tag=f"l3f_fiB{l}")

        for h in range(H):
            hb = (h % 2) * 64
            hc_ = h // 2
            krows = slice(hb, hb + 64)
            for which in range(2):
                ch = hc_ + (0 if which == 0 else 4)
                src_rows = qkv[krows, ch * S:(ch + 1) * S]
                base = 0 if which == 0 else 64
                bs = slice(base, base + 64)
                cost = rott[bs, 0:S]
                sint = rott[bs, S:2 * S]
                rhT = C.hc("l3frh", 64)[bs, :]
                P.op("vector", lambda e, src_rows=src_rows, bs=bs: e.tensor_copy(
                    rotb[bs, :], src_rows),
                    reads=[qkvr[ch]], writes=[rotbr], tag=f"l3f_st{l}_{h}_{which}")
                P.op("scalar", lambda e, bs=bs: e.activation(
                    sq2[bs, :], rotb[bs, :], AF.Square),
                    reads=[rotbr], writes=[sq2r], tag=f"l3f_sq{l}_{h}_{which}")
                for n0 in range(0, S, 512):
                    pb, pbr = C.pb[1], C.pbr[1]
                    P.op("tensor", lambda e, n0=n0, pb=pb, bs=bs, base=base: e.matmul(
                        pb[0:1, :], C.onecol_bf[base:base + 64, 0:1], sq2[bs, n0:n0 + 512],
                        start=True, stop=True),
                        reads=[sq2r, C.hcbr], writes=[pbr], tag=f"l3f_ss{l}_{h}_{which}_{n0}")
                    P.op("vector", lambda e, n0=n0, pb=pb: e.tensor_scalar(
                        nrm[0:1, n0:n0 + 512], pb[0:1, :], 1e-24, None, ALU.max),
                        reads=[pbr], writes=[nrmr], tag=f"l3f_sm{l}_{h}_{which}_{n0}")
                P.op("scalar", lambda e: e.activation(nrm[0:1, :], nrm[0:1, :], AF.Ln),
                     reads=[nrmr], writes=[nrmr], tag=f"l3f_ln{l}_{h}_{which}")
                P.op("scalar", lambda e: e.activation(nrm[0:1, :], nrm[0:1, :], AF.Exp, scale=nh[0:1, :]),
                     reads=[nrmr, P.consts_region], writes=[nrmr], tag=f"l3f_ex{l}_{h}_{which}")
                P.op("vector", lambda e: e.tensor_copy(rnr[0:1, :], nrm[0:1, :]),
                     reads=[nrmr], writes=[rnrr], tag=f"l3f_rc{l}_{h}_{which}")
                for n0 in range(0, S, 512):
                    pb, pbr = C.pb[1], C.pbr[1]
                    P.op("tensor", lambda e, n0=n0, pb=pb: e.matmul(
                        pb[0:64, :], C.ones_bf[0:1, 0:64], rnr[0:1, n0:n0 + 512],
                        start=True, stop=True),
                        reads=[rnrr, C.hcbr], writes=[pbr], tag=f"l3f_rb{l}_{h}_{which}_{n0}")
                    P.op("scalar", lambda e, n0=n0, pb=pb, bs=bs: e.copy(
                        sq2[bs, n0:n0 + 512], pb[0:64, :]),
                        reads=[pbr], writes=[sq2r], tag=f"l3f_rbe{l}_{h}_{which}_{n0}")
                qs_col = qsc[0:64, which:which + 1]
                P.op("vector", lambda e, bs=bs: e.tensor_tensor(
                    rotb[bs, :], rotb[bs, :], sq2[bs, :], ALU.mult),
                    reads=[rotbr, sq2r], writes=[rotbr], tag=f"l3f_n1{l}_{h}_{which}")
                P.op("vector", lambda e, qs_col=qs_col, bs=bs: e.tensor_scalar(
                    rotb[bs, :], rotb[bs, :], qs_col, None, ALU.mult),
                    reads=[rotbr, qscr], writes=[rotbr], tag=f"l3f_n2{l}_{h}_{which}")
                for n0 in range(0, S, 512):
                    pb, pbr = C.pb[1], C.pbr[1]
                    P.op("tensor", lambda e, n0=n0, pb=pb, bs=bs, rhT=rhT: e.matmul(
                        pb[bs, :], rhT, rotb[bs, n0:n0 + 512], start=True, stop=True),
                        reads=[rotbr, C.hcbr], writes=[pbr], tag=f"l3f_rh{l}_{h}_{which}_{n0}")
                    P.op("vector", lambda e, n0=n0, pb=pb, bs=bs, sint=sint: e.tensor_tensor(
                        sq2[bs, n0:n0 + 512], pb[bs, :], sint[:, n0:n0 + 512], ALU.mult),
                        reads=[pbr, rottr], writes=[sq2r], tag=f"l3f_rs{l}_{h}_{which}_{n0}")
                P.op("gpsimd", lambda e, bs=bs, cost=cost: e.tensor_tensor(
                    rotb[bs, :], rotb[bs, :], cost, ALU.mult),
                    reads=[rotbr, rottr], writes=[rotbr], tag=f"l3f_r1{l}_{h}_{which}")
                if which == 0:
                    P.op("vector", lambda e: e.tensor_tensor(
                        qh[0:64, :], rotb[0:64, :], sq2[0:64, :], ALU.add),
                        reads=[rotbr, sq2r], writes=[qhr], tag=f"l3f_r4{l}_{h}_{which}")
                else:
                    P.op("vector", lambda e: e.tensor_tensor(
                        kh[64:128, :], rotb[64:128, :], sq2[64:128, :], ALU.add),
                        reads=[rotbr, sq2r], writes=[khr], tag=f"l3f_r4{l}_{h}_{which}")
                    P.op("vector", lambda e: e.tensor_copy(kh[0:64, :], kh[64:128, :]),
                         reads=[khr], writes=[khr], tag=f"l3f_r5{l}_{h}_{which}")
            P.op("vector", lambda e: e.tensor_scalar_mul(qh[0:64, :], qh[0:64, :], 8.0),
                 reads=[qhr], writes=[qhr], tag=f"l3f_q8{l}_{h}")
            for qt in range(NT):
                d2 = qt % 2
                pA, pAr = (C.pb[2], C.pbr[2]) if d2 == 0 else (C.pb[5], C.pbr[5])
                pB, pBr = (C.pb[3], C.pbr[3]) if d2 == 0 else (C.pb[6], C.pbr[6])
                pV, pVr = (C.pb[4], C.pbr[4]) if d2 == 0 else (C.pb[0], C.pbr[0])
                pbuf, pbufr = pbuf2[d2]
                pT_, pT_r = pT2[d2]
                mcol, mcolr = mcol2[d2]
                chan = psch[d2]
                r0 = max(0, 4 - qt)
                for r in range(r0, 5):
                    kt = qt + r - 4
                    tgt, tgtr = ((pA, pAr) if r < 4 else (pB, pBr))
                    cols = slice(r * 128, (r + 1) * 128) if r < 4 else slice(0, 128)
                    P.op("tensor", lambda e, kt=kt, qt=qt, tgt=tgt, cols=cols: e.matmul(
                        tgt[:, cols], qh[0:64, qt * 128:(qt + 1) * 128],
                        kh[0:64, kt * 128:(kt + 1) * 128], start=True, stop=True,
                        skip_group_check=True),
                        reads=[qhr, khr], writes=[tgtr], tag=f"l3f_s{l}_{h}_{qt}_{r}")
                if r0 == 0:
                    P.op("vector", lambda e, pA=pA: e.tensor_tensor(
                        pA[:, 0:128], pA[:, 0:128], onegm[:, :], ALU.add),
                        reads=[pAr, onegmr], writes=[pAr], tag=f"l3f_am0{l}_{h}_{qt}")
                P.op("vector", lambda e, pB=pB: e.tensor_tensor(
                    pB[:, 0:128], pB[:, 0:128], dnegm[:, :], ALU.add),
                    reads=[pBr, dnegmr], writes=[pBr], tag=f"l3f_am4{l}_{h}_{qt}")
                if r0 < 4:
                    P.op("vector", lambda e, r0=r0, pA=pA, mcol=mcol: e.reduce_max(
                        mcol[:, 0:1], pA[:, r0 * 128:512], axis=AX.X),
                        reads=[pAr], writes=[mcolr], tag=f"l3f_m1{l}_{h}_{qt}")
                    P.op("vector", lambda e, pB=pB, mcol=mcol: e.reduce_max(
                        mcol[:, 1:2], pB[:, 0:128], axis=AX.X),
                        reads=[pBr], writes=[mcolr], tag=f"l3f_m2{l}_{h}_{qt}")
                    P.op("vector", lambda e, mcol=mcol: e.tensor_tensor(
                        mcol[:, 0:1], mcol[:, 0:1], mcol[:, 1:2], ALU.max),
                        reads=[mcolr], writes=[mcolr], tag=f"l3f_m3{l}_{h}_{qt}")
                else:
                    P.op("vector", lambda e, pB=pB, mcol=mcol: e.reduce_max(
                        mcol[:, 0:1], pB[:, 0:128], axis=AX.X),
                        reads=[pBr], writes=[mcolr], tag=f"l3f_m2{l}_{h}_{qt}")
                P.op("vector", lambda e, mcol=mcol: e.tensor_scalar_mul(mcol[:, 2:3], mcol[:, 0:1], -1.0),
                     reads=[mcolr], writes=[mcolr], tag=f"l3f_m4{l}_{h}_{qt}")
                if r0 < 4:
                    P.op("scalar", lambda e, r0=r0, pA=pA, pbuf=pbuf, mcol=mcol: e.activation(
                        pbuf[:, r0 * 128:512], pA[:, r0 * 128:512], AF.Exp, bias=mcol[:, 2:3]),
                        reads=[pAr, mcolr], writes=[pbufr], tag=f"l3f_e1{l}_{h}_{qt}")
                P.op("scalar", lambda e, pB=pB, pbuf=pbuf, mcol=mcol: e.activation(
                    pbuf[:, 512:640], pB[:, 0:128], AF.Exp, bias=mcol[:, 2:3]),
                    reads=[pBr, mcolr], writes=[pbufr], tag=f"l3f_e2{l}_{h}_{qt}")
                P.op("vector", lambda e, r0=r0, pbuf=pbuf, mcol=mcol: e.reduce_sum(
                    mcol[:, 3:4], pbuf[:, r0 * 128:640], axis=AX.X),
                    reads=[pbufr], writes=[mcolr], tag=f"l3f_rs{l}_{h}_{qt}")
                P.op("vector", lambda e, mcol=mcol: e.reciprocal(mcol[:, 4:5], mcol[:, 3:4]),
                     reads=[mcolr], writes=[mcolr], tag=f"l3f_rr{l}_{h}_{qt}")
                for r in range(r0, 5):
                    kt = qt + r - 4
                    p_ap = C.psbf[:, d2 * 512 + (r % 4) * 128: d2 * 512 + (r % 4) * 128 + 128]
                    P.op("tensor", lambda e, r=r, p_ap=p_ap, pbuf=pbuf: e.transpose(
                        p_ap, pbuf[:, r * 128:(r + 1) * 128], C.ident),
                        reads=[pbufr, C.hcbr], writes=[chan], tag=f"l3f_pt{l}_{h}_{qt}_{r}")
                    P.op("vector", lambda e, p_ap=p_ap, pT_=pT_: e.tensor_copy(pT_[:, :], p_ap),
                         reads=[chan], writes=[pT_r], tag=f"l3f_pte{l}_{h}_{qt}_{r}")
                    P.op("tensor", lambda e, r=r, kt=kt, h=h, pV=pV, r0=r0, pT_=pT_: e.matmul(
                        pV[:, 0:64], pT_[:, :], vtm[:, kt * 512 + h * 64: kt * 512 + (h + 1) * 64],
                        start=(r == r0), stop=(r == 4)),
                        reads=[pT_r, vtmr], writes=[pVr], tag=f"l3f_av{l}_{h}_{qt}_{r}")
                P.op("vector", lambda e, h=h, qt=qt, pV=pV, mcol=mcol: e.tensor_scalar(
                    att[:, qt * 512 + h * 64: qt * 512 + (h + 1) * 64], pV[:, 0:64],
                    mcol[:, 4:5], None, ALU.mult),
                    reads=[pVr, mcolr], writes=[attr], tag=f"l3f_ae{l}_{h}_{qt}")
        P.op("vector", lambda e: e.memset(fcell[0:1, 0:1], 0.0),
             reads=psch, writes=[C.psbfr, fcellr], tag=f"l3f_fo{l}")
        A.rewind(mh2)
        wo, wor = A.take(BF16, 4 * 1024, "l3f_wo")
        P.dma("sync", wo.rearrange("p (c n) -> p c n", c=4),
              wod[l].rearrange("(c p) n -> p c n", p=128),
              reads=[wodr], writes=[wor], tag=f"l3f_wol{l}")
        attfm, attfmr = A.take(BF16, 4 * S, "l3f_attfm")
        yt22 = [A.take(F32, 1024, f"l3f_yt2{d}") for d in range(2)]
        for i in range(NT):
            for vc in range(4):
                p_ap = C.psbf[:, (vc % 8) * 128:(vc % 8) * 128 + 128]
                P.op("tensor", lambda e, i=i, vc=vc, p_ap=p_ap: e.transpose(
                    p_ap, att[:, i * 512 + vc * 128: i * 512 + (vc + 1) * 128], C.ident),
                    reads=[attr, C.hcbr], writes=[C.psbfr], tag=f"l3f_at{l}_{i}_{vc}")
                P.op("vector", lambda e, i=i, vc=vc, p_ap=p_ap: e.tensor_copy(
                    attfm[:, vc * S + i * 128: vc * S + (i + 1) * 128], p_ap),
                    reads=[C.psbfr], writes=[attfmr], tag=f"l3f_ate{l}_{i}_{vc}")
        for i in range(NT):
            yt2, yt2r = yt22[i % 2]
            P.dma("sync", yt2[:, :], yd[i], reads=[ydr[i]], writes=[yt2r], tag=f"l3f_yrl{l}_{i}")
            for hh in range(2):
                pb, pbr = C.pb[5 + hh], C.pbr[5 + hh]

                def go(e, i=i, hh=hh, pb=pb):
                    last = None
                    for vc in range(4):
                        last = e.matmul(pb[:, :], attfm[:, vc * S + i * 128: vc * S + (i + 1) * 128],
                                        wo[:, vc * 1024 + hh * 512: vc * 1024 + (hh + 1) * 512],
                                        start=(vc == 0), stop=(vc == 3))
                    return last
                P.op("tensor", go, reads=[attfmr, wor], writes=[pbr], tag=f"l3f_om{l}_{i}_{hh}")
                P.op("vector", lambda e, hh=hh, pb=pb, yt2=yt2: e.tensor_tensor(
                    yt2[:, hh * 512:(hh + 1) * 512], yt2[:, hh * 512:(hh + 1) * 512],
                    pb[:, :], ALU.add),
                    reads=[pbr, yt2r], writes=[yt2r], tag=f"l3f_oa{l}_{i}_{hh}")
            P.dma("gpsimd", yd[i], yt2[:, :], reads=[yt2r], writes=[ydr[i]], tag=f"l3f_ys{l}_{i}")
        A.rewind(m1)

        m2_ = A.mark()
        hfm, hfmr = A.take_tiles(BF16, S, NC, "l3f_h2fm")
        ybufs2 = [A.take(F32, 1024, f"l3f_yt3{d}") + A.take(BF16, 1024, f"l3f_ybf2{d}")
                  for d in range(2)]
        build_shift_fm(l, hfm, hfmr, ybufs2, "l3f_g")
        sqf, sqfr = A.take(F32, S, "l3f_sqf")
        vrf, vrfr = A.take(F32, S, "l3f_vrf")
        scf, scfr = A.take(F32, S, "l3f_scf")
        mrow, mrowr2 = A.take(BF16, S, "l3f_mrow")
        rrow, rrowr2 = A.take(BF16, S, "l3f_rrow")
        mbc, mbcr = A.take(BF16, S, "l3f_mbc")
        rbc, rbcr = A.take(BF16, S, "l3f_rbc")
        gcols, gcolsr = A.take(F32, 16, "l3f_gc")
        for which, nm in ((0, "ln_g"), (1, "ln_b")):
            P.dma("sync", sqf[0:8, 0:128], W[f"L3_ffn_{nm}"][l].rearrange("(c p) -> c p", p=128),
                  writes=[sqfr], tag=f"l3f_{nm}{l}")
            pb, pbr = C.pb[1], C.pbr[1]
            P.op("tensor", lambda e, pb=pb: e.transpose(pb[:, 0:8], sqf[0:8, 0:128], idf[0:8, 0:8]),
                 reads=[sqfr, idfr], writes=[pbr], tag=f"l3f_{nm}t{l}")
            P.op("vector", lambda e, which=which, pb=pb: e.tensor_copy(
                gcols[:, which * 8:(which + 1) * 8], pb[:, 0:8]),
                reads=[pbr], writes=[gcolsr], tag=f"l3f_{nm}e{l}")
        for n0 in range(0, S, 512):
            pb, pbr = C.pb[1], C.pbr[1]

            def gsum(e, n0=n0, pb=pb):
                last = None
                for c in range(NC):
                    last = e.matmul(pb[0:1, :], C.onecol_bf[:, 0:1],
                                    hfm[:, c * S + n0: c * S + n0 + 512],
                                    start=(c == 0), stop=(c == NC - 1))
                return last
            P.op("tensor", gsum, reads=hfmr + [C.hcbr], writes=[pbr], tag=f"l3f_su{l}_{n0}")
            P.op("vector", lambda e, n0=n0, pb=pb: e.tensor_scalar_mul(
                sqf[0:1, n0:n0 + 512], pb[0:1, :], 1.0 / 1024),
                reads=[pbr], writes=[sqfr], tag=f"l3f_sue{l}_{n0}")
        sq512, sq512r = A.take(F32, 512, "l3f_sq512")
        onef, onefr = A.take(F32, 1, "l3f_onef")
        P.op("vector", lambda e: e.memset(onef[:, :], 1.0), writes=[onefr], tag=f"l3f_of{l}")
        for n0 in range(0, S, 512):
            pb, pbr = C.pb[1], C.pbr[1]
            for c in range(NC):
                P.op("scalar", lambda e, c=c, n0=n0: e.activation(
                    sq512[:, :], hfm[:, c * S + n0: c * S + n0 + 512], AF.Square),
                    reads=[hfmr[c]], writes=[sq512r], tag=f"l3f_q2{l}_{c}_{n0}")
                P.op("tensor", lambda e, c=c, pb=pb: e.matmul(
                    pb[0:1, :], onef[:, 0:1], sq512[:, :],
                    start=(c == 0), stop=(c == NC - 1), skip_group_check=True),
                    reads=[sq512r, onefr], writes=[pbr], tag=f"l3f_q2m{l}_{c}_{n0}")
            P.op("vector", lambda e, n0=n0, pb=pb: e.tensor_scalar_mul(
                vrf[0:1, n0:n0 + 512], pb[0:1, :], 1.0 / 1024),
                reads=[pbr], writes=[vrfr], tag=f"l3f_q2e{l}_{n0}")
        P.op("vector", lambda e: e.tensor_tensor(scf[0:1, 0:S], sqf[0:1, 0:S], sqf[0:1, 0:S], ALU.mult),
             reads=[sqfr], writes=[scfr], tag=f"l3f_mm{l}")
        P.op("vector", lambda e: e.tensor_tensor(vrf[0:1, 0:S], vrf[0:1, 0:S], scf[0:1, 0:S], ALU.subtract),
             reads=[vrfr, scfr], writes=[vrfr], tag=f"l3f_vv{l}")
        P.op("scalar", lambda e: e.activation(vrf[0:1, 0:S], vrf[0:1, 0:S], AF.Ln, bias=cpe[0:1, :]),
             reads=[vrfr, P.consts_region], writes=[vrfr], tag=f"l3f_vl{l}")
        P.op("scalar", lambda e: e.activation(vrf[0:1, 0:S], vrf[0:1, 0:S], AF.Exp, scale=nh[0:1, :]),
             reads=[vrfr, P.consts_region], writes=[vrfr], tag=f"l3f_vr{l}")
        P.op("vector", lambda e: e.tensor_copy(mrow[0:1, :], sqf[0:1, 0:S]),
             reads=[sqfr], writes=[mrowr2], tag=f"l3f_mc2{l}")
        P.op("vector", lambda e: e.tensor_copy(rrow[0:1, :], vrf[0:1, 0:S]),
             reads=[vrfr], writes=[rrowr2], tag=f"l3f_rc3{l}")
        for n0 in range(0, S, 512):
            pb, pbr = C.pb[1], C.pbr[1]
            P.op("tensor", lambda e, n0=n0, pb=pb: e.matmul(
                pb[:, :], C.ones_bf[0:1, 0:128], mrow[0:1, n0:n0 + 512], start=True, stop=True),
                reads=[mrowr2, C.hcbr], writes=[pbr], tag=f"l3f_mb{l}_{n0}")
            P.op("scalar", lambda e, n0=n0, pb=pb: e.copy(mbc[:, n0:n0 + 512], pb[:, :]),
                 reads=[pbr], writes=[mbcr], tag=f"l3f_mbe{l}_{n0}")
            P.op("tensor", lambda e, n0=n0, pb=pb: e.matmul(
                pb[:, :], C.ones_bf[0:1, 0:128], rrow[0:1, n0:n0 + 512], start=True, stop=True),
                reads=[rrowr2, C.hcbr], writes=[pbr], tag=f"l3f_rb2{l}_{n0}")
            P.op("scalar", lambda e, n0=n0, pb=pb: e.copy(rbc[:, n0:n0 + 512], pb[:, :]),
                 reads=[pbr], writes=[rbcr], tag=f"l3f_rbe2{l}_{n0}")
        for c in range(NC):
            P.op("vector", lambda e, c=c: e.tensor_tensor(
                hfm[:, c * S:(c + 1) * S], hfm[:, c * S:(c + 1) * S], mbc[:, :], ALU.subtract),
                reads=[hfmr[c], mbcr], writes=[hfmr[c]], tag=f"l3f_a1{l}_{c}")
            P.op("gpsimd", lambda e, c=c: e.tensor_tensor(
                hfm[:, c * S:(c + 1) * S], hfm[:, c * S:(c + 1) * S], rbc[:, :], ALU.mult),
                reads=[hfmr[c], rbcr], writes=[hfmr[c]], tag=f"l3f_a2{l}_{c}")
            P.op("vector", lambda e, c=c: e.tensor_scalar(
                hfm[:, c * S:(c + 1) * S], hfm[:, c * S:(c + 1) * S],
                gcols[:, c:c + 1], gcols[:, 8 + c:8 + c + 1], ALU.mult, ALU.add),
                reads=[hfmr[c], gcolsr], writes=[hfmr[c]], tag=f"l3f_a3{l}_{c}")
        wstg2 = [A.take(BF16, NC * 128, f"l3f_wstg{d}") for d in range(2)]
        b1c2 = [A.take(F32, 1, f"l3f_b1c{d}") for d in range(2)]
        inter, interr = A.take_tiles(BF16, 512, NAC, "l3f_int")
        gsb2, gsb2r = A.take(F32, 512, "l3f_gsb")
        b2row, b2rowr = A.take(BF16, 1024, "l3f_b2")
        w2st2 = [A.take(BF16, 512, f"l3f_w2st{d}") for d in range(2)]
        P.dma("gpsimd", b2row[0:1, :], W["L3_ffn_b2"][l][None], writes=[b2rowr], tag=f"l3f_b2l{l}")
        for tq in range(4):
            t0 = tq * 512
            for part in range(2):
                for kc in range(NAC):
                    mrows = 128 if kc < 21 else 42
                    j0 = part * 2730 + kc * 128
                    d = (part * NAC + kc) % 2
                    wstg, wstgr = wstg2[d]
                    b1c, b1cr = b1c2[d]
                    P.dma("sync", wstg.rearrange("p (c n) -> p c n", c=NC)[:, :, 0:mrows],
                          w1d[l][:, j0:j0 + mrows].rearrange("(c p) n -> p c n", p=128),
                          reads=[w1dr], writes=[wstgr], tag=f"l3f_w1l{l}_{tq}_{part}_{kc}")
                    P.dma("sync", b1c[0:mrows, 0:1],
                          W["L3_ffn_b1"][l][j0:j0 + mrows].rearrange("(n o) -> n o", o=1),
                          writes=[b1cr], tag=f"l3f_b1l{l}_{tq}_{part}_{kc}")
                    bank = 2 + 2 * part + (kc % 2)
                    pb, pbr = C.pb[bank], C.pbr[bank]

                    def g1(e, kc=kc, pb=pb, mrows=mrows, t0=t0, hfm=hfm, wstg=wstg):
                        last = None
                        for c in range(NC):
                            last = e.matmul(pb[0:mrows, :], wstg[:, c * 128:c * 128 + mrows],
                                            hfm[:, c * S + t0: c * S + t0 + 512],
                                            start=(c == 0), stop=(c == NC - 1))
                        return last
                    P.op("tensor", g1, reads=[wstgr] + hfmr,
                         writes=[pbr], tag=f"l3f_w1m{l}_{tq}_{part}_{kc}")
                    if part == 0:
                        P.op("vector", lambda e, kc=kc, pb=pb, mrows=mrows, b1c=b1c: e.tensor_scalar(
                            inter[0:mrows, kc * 512:(kc + 1) * 512], pb[0:mrows, :],
                            b1c[0:mrows, 0:1], None, ALU.add),
                            reads=[pbr, b1cr], writes=[interr[kc]], tag=f"l3f_ae1{l}_{tq}_{kc}")
                    else:
                        P.op("scalar", lambda e, pb=pb, mrows=mrows, b1c=b1c: e.activation(
                            gsb2[0:mrows, :], pb[0:mrows, :], AF.Gelu, bias=b1c[0:mrows, 0:1]),
                            reads=[pbr, b1cr], writes=[gsb2r], tag=f"l3f_ge{l}_{tq}_{kc}")
                        P.op("vector", lambda e, kc=kc, mrows=mrows: e.tensor_tensor(
                            inter[0:mrows, kc * 512:(kc + 1) * 512],
                            inter[0:mrows, kc * 512:(kc + 1) * 512],
                            gsb2[0:mrows, :], ALU.mult),
                            reads=[gsb2r, interr[kc]], writes=[interr[kc]], tag=f"l3f_im{l}_{tq}_{kc}")
            for hh in range(2):
                for kc in range(NAC):
                    mrows = 128 if kc < 21 else 42
                    w2st, w2str = w2st2[kc % 2]
                    P.dma("sync", w2st[0:mrows, :], w2d[l][kc * 128:kc * 128 + mrows,
                                                           hh * 512:(hh + 1) * 512],
                          reads=[w2dr], writes=[w2str], tag=f"l3f_w2l{l}_{tq}_{hh}_{kc}")
                    for ti in range(4):
                        pb, pbr = C.pb[2 + ti], C.pbr[2 + ti]
                        P.op("tensor", lambda e, kc=kc, ti=ti, pb=pb, mrows=mrows, w2st=w2st: e.matmul(
                            pb[:, :], inter[0:mrows, kc * 512 + ti * 128: kc * 512 + ti * 128 + 128],
                            w2st[0:mrows, :], start=(kc == 0), stop=False),
                            reads=[w2str] + interr, writes=[pbr], tag=f"l3f_w2m{l}_{tq}_{hh}_{kc}_{ti}")
                for ti in range(4):
                    pb, pbr = C.pb[2 + ti], C.pbr[2 + ti]
                    P.op("tensor", lambda e, ti=ti, pb=pb, hh=hh: e.matmul(
                        pb[:, :], C.ones_bf[0:1, 0:128], b2row[0:1, hh * 512:(hh + 1) * 512],
                        start=False, stop=True),
                        reads=[b2rowr, C.hcbr], writes=[pbr], tag=f"l3f_w2b{l}_{tq}_{hh}_{ti}")
                for ti in range(4):
                    i = tq * 4 + ti
                    pb, pbr = C.pb[2 + ti], C.pbr[2 + ti]
                    ytile, ytiler = ybufs2[ti % 2][0], ybufs2[ti % 2][1]
                    P.dma("sync", ytile[:, 0:512],
                          yd[i][:, hh * 512:(hh + 1) * 512], reads=[ydr[i]], writes=[ytiler],
                          tag=f"l3f_y4l{l}_{tq}_{hh}_{ti}")
                    P.op("vector", lambda e, pb=pb, ytile=ytile: e.tensor_tensor(
                        ytile[:, 0:512], ytile[:, 0:512], pb[:, :], ALU.add),
                        reads=[pbr, ytiler], writes=[ytiler], tag=f"l3f_y4a{l}_{tq}_{hh}_{ti}")
                    P.dma("gpsimd", yd[i][:, hh * 512:(hh + 1) * 512], ytile[:, 0:512],
                          reads=[ytiler], writes=[ydr[i]], tag=f"l3f_y4s{l}_{tq}_{hh}_{ti}")
        A.rewind(m2_)

    ytf, ytfr = A.take(F32, 1024, "l3f_ytf")
    for i in range(NT):
        P.dma("sync", ytf[:, :], yd[i], reads=[ydr[i]], writes=[ytfr], tag=f"l3f_yfl{i}")
        x_ap = C.X[:, i * 1024:(i + 1) * 1024]
        P.op("vector", lambda e, x_ap=x_ap: e.tensor_tensor(x_ap, x_ap, ytf[:, :], ALU.add),
             reads=[ytfr, C.Xr[i]], writes=[C.Xr[i]], tag=f"l3f_xf{i}")
    C.arena.reset()



def build_hconsts():
    cols = {}
    parts = []
    ofs = 0

    def add(name, arr):
        nonlocal ofs
        cols[name] = ofs
        parts.append(arr.astype(ml_dtypes.bfloat16))
        ofs += arr.shape[1]

    add("ident", np.eye(128, dtype=np.float32))
    ones = np.zeros((128, 512), np.float32)
    ones[0, :] = 1.0
    add("ones", ones)
    add("onecol", np.ones((128, 1), np.float32))
    s_idx = np.arange(128)[:, None]
    t_idx = np.arange(128)[None, :]
    add("l2neg", np.where(s_idx <= t_idx, 0.0, -1e30).astype(np.float32))
    for v in range(16):
        M = np.zeros((128, 128), np.float32)
        m_ = np.arange(128)
        M[8 * v + m_ // 16, m_] = 1.0
        add(f"l1sel{v}", M)
    for v in range(4):
        M = np.zeros((128, 32), np.float32)
        k_ = np.arange(128)
        M[k_, 8 * v + k_ // 16] = 1.0
        add(f"l1m2_{v}", M)
    for off in (-384, -256, -128, 0):
        m = np.zeros((128, 512), np.float32)
        s_idx = np.arange(128)[:, None]
        jrel = np.arange(512)[None, :]
        m[:] = (s_idx <= off + jrel)
        add(f"l0m{off}", m)
    rh = np.zeros((128, 64), np.float32)
    for b0 in (0, 64):
        for k_ in range(32):
            rh[b0 + k_, k_ + 32] = 1.0
            rh[b0 + 32 + k_, k_] = -1.0
    add("l3frh", rh)
    cols["_total"] = ofs
    return cols, np.concatenate(parts, axis=1)



BLOCK_EMITTERS = {}
LAST_PROG = None
CC_GROUP = list(range(NCORES))


def build_program(block_names, weight_names, hc_layout, hc_total):
    nc = bass.Bass("TRN2", target_bir_lowering=False)
    x_in = nc.dram_tensor("x_in", [S, D], F32, kind="ExternalInput").ap()
    out = nc.dram_tensor("out", [S, D], F32, kind="ExternalOutput").ap()
    hcb = nc.dram_tensor("hcb", [128, hc_total], BF16, kind="ExternalInput").ap()
    W = {}
    DT = {"float32": F32, "uint8": mybir.dt.uint8}
    for name, shape, dt in weight_names:
        W[name] = nc.dram_tensor(name, list(shape), DT[dt], kind="ExternalInput").ap()
    global LAST_PROG
    with ExitStack() as ctx:
        P = Prog(nc, ctx)
        LAST_PROG = P
        C = Core(P, hcb, hc_layout)
        emit_load_x(C, x_in)
        for b in block_names:
            BLOCK_EMITTERS[b](C, W)
        emit_store_x(C, out)
        tot = P.total["dgp"]
        P.ops["gpsimd"].append(dict(
            fn=lambda e: e.wait_ge(P.sems["dgp"], tot) and None,
            waits={}, sem="pool", inc=0, cnt=P.total["pool"], tag="final"))
        P.finalize()
    return nc


BLOCK_EMITTERS["l0_attn"] = emit_l0_attn
BLOCK_EMITTERS["l2_ffn"] = emit_l2_ffn
BLOCK_EMITTERS["l3_attn"] = emit_l3_attn
BLOCK_EMITTERS["l1_ffn"] = emit_l1_ffn
BLOCK_EMITTERS["l0_ffn"] = emit_l0_ffn
BLOCK_EMITTERS["l2_attn"] = emit_l2_attn
BLOCK_EMITTERS["l1_attn"] = emit_l1_attn
BLOCK_EMITTERS["l3_ffn"] = emit_l3_ffn

WEIGHTS_BY_BLOCK = {
    "l0_attn": [(n, sh, "float32") for n, sh in
                [("L0_attn_Wq", (1024, 1024)), ("L0_attn_bq", (1024,)),
                 ("L0_attn_Wk", (1024, 1024)), ("L0_attn_bk", (1024,)),
                 ("L0_attn_Wv", (1024, 1024)), ("L0_attn_bv", (1024,)),
                 ("L0_attn_Wm", (1024, 1024)), ("L0_attn_bm", (1024,))]],
    "l3_attn": [(n, sh, "float32") for n, sh in
                [("L3_attn_Wq", (1024, 1024)), ("L3_attn_Wk", (1024, 1024)),
                 ("L3_attn_Wv", (1024, 1024)), ("L3_attn_Wo", (1024, 1024)),
                 ("L3_attn_bo", (1024,))]],
    "l0_ffn": [(n, sh, "float32") for n, sh in
               [("L0_ffn_clip_x", (1,)), ("L0_ffn_clip_down", (1,)),
                ("L0_ffn_w_gate", (2816, 1024)), ("L0_ffn_w_up", (2816, 1024)),
                ("L0_ffn_w_down", (1024, 2816))]],
    "l1_ffn": [(n, sh, "float32") for n, sh in
               [("L1_ffn_gate_w", (8, 1024)), ("L1_ffn_Wg", (8, 1024, 512)),
                ("L1_ffn_Wu", (8, 1024, 512)), ("L1_ffn_Wd", (8, 512, 1024)),
                ("L1_ffn_sWg", (1024, 512)), ("L1_ffn_sWu", (1024, 512)),
                ("L1_ffn_sWd", (512, 1024))]],
    "l3_ffn": [(n, sh, "float32") for n, sh in
               [("L3_ffn_wqkv", (2, 1024, 1536)), ("L3_ffn_q_scale", (2, 64)),
                ("L3_ffn_k_scale", (2, 64)), ("L3_ffn_wo", (2, 512, 1024)),
                ("L3_ffn_ln_g", (2, 1024)), ("L3_ffn_ln_b", (2, 1024)),
                ("L3_ffn_w1", (2, 1024, 5460)), ("L3_ffn_b1", (2, 5460)),
                ("L3_ffn_w2", (2, 2730, 1024)), ("L3_ffn_b2", (2, 1024))]],
    "l1_attn": [(n, sh, "float32") for n, sh in
                [("L1_attn_A_log", (1024, 16)), ("L1_attn_D", (1024,)),
                 ("L1_attn_x_proj_w", (96, 1024)), ("L1_attn_dt_proj_w", (1024, 64)),
                 ("L1_attn_dt_proj_b", (1024,))]],
    "l2_attn": [(n, sh, "float32") for n, sh in
                [("L2_attn_Wx", (2048, 1024)), ("L2_attn_Wz", (2048, 1024)),
                 ("L2_attn_conv_w", (2048, 1, 4)), ("L2_attn_conv_b", (2048,)),
                 ("L2_attn_Wb", (512, 2048)), ("L2_attn_Wc", (512, 2048)),
                 ("L2_attn_Wdt", (8, 2048)), ("L2_attn_b_dt", (8,)),
                 ("L2_attn_log_a", (8,)), ("L2_attn_d_param", (8,)),
                 ("L2_attn_gn_w", (2048,)), ("L2_attn_gn_b", (2048,)),
                 ("L2_attn_Wout", (1024, 2048))]],
    "l2_ffn": [("L2_ffn_k0", (2048, 4096), "uint8"), ("L2_ffn_k1", (4096, 4096), "uint8"),
               ("L2_ffn_k2", (4096, 1024), "uint8"), ("L2_ffn_s0", (1,), "float32"),
               ("L2_ffn_s1", (1,), "float32"), ("L2_ffn_s2", (1,), "float32")],
}

LAUNCHES = [["l0_attn", "l0_ffn", "l1_attn", "l1_ffn",
             "l2_attn", "l2_ffn", "l3_attn", "l3_ffn"]]


def build_l3f_rot():
    DH, WIN, n = 64, 512, S
    half = np.arange(0, DH, 2, dtype=np.float64)
    inv_freq = 1.0 / 10000.0 ** (half / DH)
    t = np.arange(n, dtype=np.float64)
    freqs = t[:, None] * inv_freq[None, :]
    freqs = np.concatenate([freqs, freqs], axis=1)
    base = (half + 0.4 * DH) / (1.4 * DH)
    power = (t - n // 2) / float(WIN // 2)
    scale = base[None, :] ** power[:, None]
    scale = np.concatenate([scale, scale], axis=1)
    cos, sin = np.cos(freqs), np.sin(freqs)
    tab = np.zeros((128, 2 * n), np.float32)
    tab[0:64, 0:n] = (cos * scale).T
    tab[0:64, n:2 * n] = (sin * scale).T
    tab[64:128, 0:n] = (cos / scale).T
    tab[64:128, n:2 * n] = (sin / scale).T
    return tab


def kernel(**inputs):
    hc_layout, hc_data = build_hconsts()
    x = np.ascontiguousarray(inputs["L0_attn_x"], dtype=np.float32)
    cur = [x[i] for i in range(NCORES)]
    for blocks in LAUNCHES:
        wnames = []
        for b in blocks:
            wnames += WEIGHTS_BY_BLOCK[b]
        nc = build_program(blocks, wnames, hc_layout, hc_layout["_total"])
        rot = build_l3f_rot() if "l3_ffn" in blocks else None
        in_maps = []
        for i in range(NCORES):
            m = {"x_in": cur[i], "hcb": hc_data}
            if rot is not None:
                m["l3f_rot"] = rot
            for name, shape, dt in wnames:
                v = inputs[name]
                m[name] = np.ascontiguousarray(np.reshape(v, shape),
                                               np.uint8 if dt == "uint8" else np.float32)
            in_maps.append(m)
        res = run_bass_kernel_spmd(nc, in_maps, core_ids=list(range(NCORES)))
        cur = [res.results[i]["out"] for i in range(NCORES)]
    return np.stack(cur, axis=0)


# version 14
# speedup vs baseline: 1.0271x; 1.0271x over previous
import numpy as np
from contextlib import ExitStack

import ml_dtypes
import concourse.bass as bass
import concourse.mybir as mybir
from concourse.bass_utils import run_bass_kernel_spmd

F32 = mybir.dt.float32
BF16 = mybir.dt.bfloat16
I32 = mybir.dt.int32
AF = mybir.ActivationFunctionType
ALU = mybir.AluOpType
AX = mybir.AxisListType

NCORES = 8
S = 2048
D = 1024
NT = S // 128
NC = D // 128


ENGS = ("tensor", "scalar", "vector", "gpsimd", "sync")
ENG_SEM = {"tensor": "pe", "scalar": "act", "vector": "dve", "gpsimd": "pool"}
QUEUE_SEM = {"sync": "dsp", "gpsimd": "dgp", "scalar": "dact"}
ALL_SEMS = ("pe", "act", "dve", "pool", "dsp", "dgp", "dact", "coll")


class Region:
    __slots__ = ("name", "w", "readers")

    def __init__(self, name):
        self.name = name
        self.w = None
        self.readers = []


class Prog:
    def __init__(self, nc, ctx):
        self.nc = nc
        self.ctx = ctx
        self.ops = {e: [] for e in ENGS}
        self.total = {s: 0 for s in ALL_SEMS}
        self.sems = {s: ctx.enter_context(nc.semaphore(s)) for s in ALL_SEMS}
        self._const_cols = {}
        self.consts = None
        self.NCONST = 24
        self.total["dve"] = self.NCONST
        self.consts_region = Region("consts")
        self.consts_region.w = ("vector", "dve", self.NCONST)

    def sb(self, name, shape, dtype):
        t = self.ctx.enter_context(self.nc.sbuf_tensor(name, shape, dtype))
        return t, Region(name)

    def psb(self, name, shape, dtype=F32):
        t = self.ctx.enter_context(self.nc.psum_tensor(name, shape, dtype))
        return t, Region(name)

    def op(self, eng, fn, reads=(), writes=(), sem=None, inc=None, tag=""):
        if sem is None:
            sem = ENG_SEM[eng]
        if inc is None:
            inc = 16 if sem in QUEUE_SEM.values() else 1
        waits = {}

        def need(dep):
            if dep is None:
                return
            dstream, dsem, dcnt = dep
            if dstream == eng == "tensor" and dsem == "pe":
                return
            waits[dsem] = max(waits.get(dsem, 0), dcnt)

        for r in reads:
            need(r.w)
        for r in writes:
            need(r.w)
            for rd in r.readers:
                need(rd)
        self.total[sem] += inc
        cnt = self.total[sem]
        rec = dict(fn=fn, waits=waits, sem=sem, inc=inc, cnt=cnt, tag=tag)
        self.ops[eng].append(rec)
        for r in reads:
            r.readers.append((eng, sem, cnt))
        for r in writes:
            r.w = (eng, sem, cnt)
            r.readers = []
        return rec

    def dma(self, queue, out_ap, in_ap, reads=(), writes=(), tag="", **kw):
        return self.op(queue, lambda e: e.dma_start(out_ap, in_ap, **kw),
                       reads=reads, writes=writes, sem=QUEUE_SEM[queue], tag=tag)

    def const_col(self, value):
        v = float(value)
        if v not in self._const_cols:
            assert self.consts is not None
            self._const_cols[v] = len(self._const_cols)
            assert len(self._const_cols) <= self.consts.shape[1]
        idx = self._const_cols[v]
        return self.consts[:, idx:idx + 1]

    def finalize(self):
        pre = []
        items = sorted(self._const_cols.items(), key=lambda kv: kv[1])
        assert len(items) <= self.NCONST
        for k in range(self.NCONST):
            if k < len(items):
                v, idx = items[k]
                fn = (lambda e, v=v, idx=idx: e.memset(self.consts[:, idx:idx + 1], v))
            else:
                fn = (lambda e: e.engine_nop())
            pre.append(dict(fn=fn, waits={}, sem="dve", inc=1, cnt=k + 1, tag="const"))
        self.ops["vector"] = pre + self.ops["vector"]

        consumer_waits = {q: set() for q in QUEUE_SEM.values()}
        for eng in ENGS:
            for rec in self.ops[eng]:
                for wsem, val in rec["waits"].items():
                    if wsem in consumer_waits and wsem != rec["sem"]:
                        consumer_waits[wsem].add(val)
        for eng in ENGS:
            qsem = QUEUE_SEM.get(eng)
            if qsem is None:
                continue
            needed = sorted(consumer_waits[qsem])
            level = 0
            for rec in self.ops[eng]:
                if rec["sem"] != qsem:
                    v = rec["waits"].get(qsem)
                    if v is not None:
                        level = max(level, v)
                    continue
                c = rec["cnt"]
                target = 0
                for V in needed:
                    if V <= c - rec["inc"]:
                        target = V
                    else:
                        break
                if target > level:
                    rec["waits"][qsem] = max(rec["waits"].get(qsem, 0), target)
                    level = target

        CSEMS = ("pe", "act", "dve", "pool")
        waited = {s: set() for s in CSEMS}
        for eng in ENGS:
            for rec in self.ops[eng]:
                for wsem, val in rec["waits"].items():
                    if wsem in waited:
                        waited[wsem].add(val)
        remap = {}
        for s in CSEMS:
            k = 0
            m = {}
            for eng in ENGS:
                for rec in self.ops[eng]:
                    if rec["sem"] != s:
                        continue
                    if rec["inc"] and rec["cnt"] in waited[s]:
                        k += rec["inc"]
                        m[rec["cnt"]] = k
                        rec["_ni"] = rec["inc"]
                    else:
                        rec["_ni"] = 0
                    rec["_nc"] = k
            remap[s] = m
        for eng in ENGS:
            for rec in self.ops[eng]:
                for wsem in list(rec["waits"]):
                    if wsem in remap:
                        rec["waits"][wsem] = remap[wsem][rec["waits"][wsem]]
                if rec["sem"] in remap:
                    rec["inc"] = rec["_ni"]
                    rec["cnt"] = rec["_nc"]

        def resolve(sem, val):
            return val

        block = self.ctx.enter_context(self.nc.Block())
        watermark = {}

        def make_body(eng):
            ops = self.ops[eng]

            def body(handle):
                for rec in ops:
                    for wsem in sorted(rec["waits"]):
                        v = resolve(wsem, rec["waits"][wsem])
                        if wsem == rec["sem"]:
                            v = min(v, rec["cnt"] - rec["inc"])
                        key = (eng, wsem)
                        if v <= 0 or watermark.get(key, 0) >= v:
                            continue
                        watermark[key] = v
                        handle.wait_ge(self.sems[wsem], v)
                    with self.nc.allow_low_precision(reason="bf16 staging/transposes; accumulation stays fp32"):
                        inst = rec["fn"](handle)
                    if inst is not None and rec["inc"]:
                        inst.then_inc(self.sems[rec["sem"]], rec["inc"])
            return body

        for eng in ENGS:
            if self.ops[eng]:
                getattr(block, eng)(make_body(eng))



class Arena:

    def __init__(self, P, cols_f32):
        self.P = P
        self.t32 = P.ctx.enter_context(P.nc.sbuf_tensor("arena", [128, cols_f32], F32))
        self.tbf = self.t32.bitcast(BF16)
        self.tu8 = self.t32.bitcast(mybir.dt.uint8)
        self.ti32 = self.t32.bitcast(I32)
        self.nbytes = cols_f32 * 4
        self.off = 0
        self.live = []
        self.epoch_deps = []

    def take(self, dtype, cols, name):
        esz = {F32: 4, BF16: 2, mybir.dt.uint8: 1, I32: 4}[dtype]
        off = (self.off + 3) // 4 * 4
        assert off + cols * esz <= self.nbytes, f"arena overflow at {name} ({off}+{cols*esz})"
        self.off = off + cols * esz
        t = {F32: self.t32, BF16: self.tbf, mybir.dt.uint8: self.tu8, I32: self.ti32}[dtype]
        ap = t[:, off // esz: off // esz + cols]
        r = Region(name)
        r.readers = list(self.epoch_deps)
        self.live.append(r)
        return ap, r

    def take_tiles(self, dtype, cols, n, name):
        ap, r = self.take(dtype, cols * n, name)
        regions = [Region(f"{name}{i}") for i in range(n)]
        for rr in regions:
            rr.readers = list(self.epoch_deps)
        self.live.extend(regions)
        self.live.remove(r)
        return ap, regions

    def mark(self):
        return (self.off, len(self.live))

    def rewind(self, m):
        off0, n0 = m
        deps = {d[1]: d for d in self.epoch_deps}
        for r in self.live[n0:]:
            for d in ([r.w] if r.w else []) + r.readers:
                if d[1] not in deps or deps[d[1]][2] < d[2]:
                    deps[d[1]] = d
        self.epoch_deps = list(deps.values())
        self.live = self.live[n0:][:0] + self.live[:n0]
        self.off = off0

    def reset(self):
        deps = {}
        for r in self.live:
            for d in ([r.w] if r.w else []) + r.readers:
                key = d[1]
                if key not in deps or deps[key][2] < d[2]:
                    deps[key] = d
        self.epoch_deps = list(deps.values())
        self.live = []
        self.off = 0


class Core:
    def __init__(self, P: Prog, hcb_ap, hcb_layout, arena_cols=34176):
        self.P = P
        self.X, _ = P.sb("X", [128, NT * 1024], F32)
        self.Xr = [Region(f"X{i}") for i in range(NT)]
        self.stat, self.statr = P.sb("stat", [128, 48], F32)
        self.lnstat, self.lnstatr = P.sb("lnstat", [128, 2 * NT], F32)
        self.pb = []
        self.pbr = []
        for i in range(7):
            t, r = P.psb(f"pb{i}", [128, 512], F32)
            self.pb.append(t)
            self.pbr.append(r)
        self.psbf, self.psbfr = P.psb("pbf", [128, 1024], BF16)
        self.hcb, self.hcbr = P.sb("hcb_sb", [128, hcb_layout["_total"]], BF16)
        self.hcb_layout = hcb_layout
        P.dma("sync", self.hcb[:, :], hcb_ap, writes=[self.hcbr], tag="hcb")
        self.ident = self.hcb[:, hcb_layout["ident"]:hcb_layout["ident"] + 128]
        self.ones_bf = self.hcb[:, hcb_layout["ones"]:hcb_layout["ones"] + 512]
        self.onecol_bf = self.hcb[:, hcb_layout["onecol"]:hcb_layout["onecol"] + 1]
        P.consts, _ = P.sb("consts", [128, 24], F32)
        self.arena = Arena(P, arena_cols)
        self.ln_fm = None
        self.ln_fmr = None

    def hc(self, name, width):
        o = self.hcb_layout[name]
        return self.hcb[:, o:o + width]


def emit_load_x(C: Core, x_ap):
    src = x_ap.rearrange("(t p) d -> p t d", p=128)
    C.P.dma("sync", C.X.rearrange("p (t d) -> p t d", t=NT), src,
            writes=C.Xr, tag="loadx")


def emit_store_x(C: Core, out_ap):
    dst = out_ap.rearrange("(t p) d -> p t d", p=128)
    C.P.dma("gpsimd", dst, C.X.rearrange("p (t d) -> p t d", t=NT),
            reads=C.Xr, tag="storex")


def emit_final_wait(P: Prog):
    tot = P.total["dgp"]
    P.op("gpsimd", lambda e: None, tag="final")

    def fn(e):
        e.wait_ge(P.sems["dgp"], tot)
        return None
    P.ops["gpsimd"].append(dict(fn=fn, waits={}, sem="pool", inc=0, cnt=P.total["pool"], tag="finalwait"))


def emit_layernorm(C: Core, fm=True):
    P = C.P
    if fm:
        C.ln_fm, C.ln_fmr = C.arena.take_tiles(BF16, S, NC, "lnfm")
        stage, stager = C.arena.take(BF16, 2 * 1024, "lnstage")
    sq, sqr = C.arena.take(F32, 1024, "lnsq")
    inv_d = P.const_col(1.0 / D)
    eps = P.const_col(1e-5)
    neg_half = P.const_col(-0.5)
    st = C.stat
    for i in range(NT):
        x = C.X[:, i * 1024:(i + 1) * 1024]
        P.op("vector", lambda e, x=x: e.reduce_sum(st[:, 0:1], x, axis=AX.X),
             reads=[C.Xr[i]], writes=[C.statr], tag=f"ln_sum{i}")
        P.op("scalar", lambda e, x=x: e.activation(
            sq[:, :], x, AF.Square, accum_out=st[:, 1:2]),
            reads=[C.Xr[i]], writes=[sqr, C.statr], tag=f"ln_sq{i}")
        P.op("vector", lambda e: e.tensor_scalar(st[:, 2:3], st[:, 0:1], inv_d, None, ALU.mult),
             reads=[C.statr, P.consts_region], writes=[C.statr], tag="ln_m")
        P.op("vector", lambda e: e.tensor_scalar(st[:, 3:4], st[:, 1:2], inv_d, None, ALU.mult),
             reads=[C.statr], writes=[C.statr], tag="ln_ex2")
        P.op("vector", lambda e: e.tensor_tensor(st[:, 4:5], st[:, 2:3], st[:, 2:3], ALU.mult),
             reads=[C.statr], writes=[C.statr], tag="ln_m2")
        P.op("vector", lambda e: e.tensor_tensor(st[:, 5:6], st[:, 3:4], st[:, 4:5], ALU.subtract),
             reads=[C.statr], writes=[C.statr], tag="ln_var")
        P.op("scalar", lambda e: e.activation(st[:, 6:7], st[:, 5:6], AF.Ln, bias=eps),
             reads=[C.statr, P.consts_region], writes=[C.statr], tag="ln_ln")
        P.op("scalar", lambda e: e.activation(st[:, 7:8], st[:, 6:7], AF.Exp, scale=neg_half),
             reads=[C.statr, P.consts_region], writes=[C.statr], tag="ln_rstd")
        P.op("vector", lambda e, i=i: e.tensor_copy(C.lnstat[:, 2 * i:2 * i + 1], st[:, 2:3]),
             reads=[C.statr], writes=[C.lnstatr], tag=f"ln_sv{i}")
        P.op("vector", lambda e, i=i: e.tensor_copy(C.lnstat[:, 2 * i + 1:2 * i + 2], st[:, 7:8]),
             reads=[C.statr], writes=[C.lnstatr], tag=f"ln_sv2{i}")
        if not fm:
            continue
        dst = stage[:, (i % 2) * 1024:(i % 2) * 1024 + 1024]
        P.op("vector", lambda e, x=x, dst=dst: e.tensor_scalar(
            dst, x, st[:, 2:3], st[:, 7:8], ALU.subtract, ALU.mult),
            reads=[C.Xr[i], C.statr], writes=[stager], tag=f"ln_bf{i}")
        for c in range(NC):
            s_ap = dst[:, c * 128:(c + 1) * 128]
            p_ap = C.psbf[:, (c % 8) * 128:(c % 8) * 128 + 128]
            d_ap = C.ln_fm[:, c * S + i * 128: c * S + (i + 1) * 128]
            P.op("tensor", lambda e, s_ap=s_ap, p_ap=p_ap: e.transpose(p_ap, s_ap, C.ident),
                 reads=[stager, C.hcbr], writes=[C.psbfr], tag=f"lntr{i}_{c}")
            if c % 2 == 0:
                P.op("vector", lambda e, p=p_ap, d=d_ap: e.tensor_copy(d, p),
                     reads=[C.psbfr], writes=[C.ln_fmr[c]], tag=f"lnte{i}_{c}")
            else:
                P.op("scalar", lambda e, p=p_ap, d=d_ap: e.copy(d, p),
                     reads=[C.psbfr], writes=[C.ln_fmr[c]], tag=f"lnte{i}_{c}")


def emit_recip_nr(P, dst, src, tmp, reg, tag, iters=2):
    P.op("vector", lambda e: e.reciprocal(dst, src), reads=[reg], writes=[reg], tag=tag + "r0")
    for it in range(iters):
        P.op("vector", lambda e: e.tensor_tensor(tmp, src, dst, ALU.mult),
             reads=[reg], writes=[reg], tag=f"{tag}n{it}a")
        P.op("vector", lambda e: e.tensor_scalar(tmp, tmp, 2.0, -1.0, ALU.subtract, ALU.mult),
             reads=[reg], writes=[reg], tag=f"{tag}n{it}b")
        P.op("vector", lambda e: e.tensor_tensor(dst, dst, tmp, ALU.mult),
             reads=[reg], writes=[reg], tag=f"{tag}n{it}c")


def emit_ln32_tile(C: Core, i, dst_ap, dst_region):
    P = C.P
    x = C.X[:, i * 1024:(i + 1) * 1024]
    P.op("vector", lambda e: e.tensor_scalar(
        dst_ap, x, C.lnstat[:, 2 * i:2 * i + 1], C.lnstat[:, 2 * i + 1:2 * i + 2],
        ALU.subtract, ALU.mult),
        reads=[C.Xr[i], C.lnstatr], writes=[dst_region], tag=f"ln32_{i}")



def emit_l0_attn(C: Core, W):
    P = C.P
    A = C.arena
    H, HS = 16, 64
    SHIFT = 30.0

    emit_layernorm(C)

    wp = []
    wpr = []
    for d in range(2):
        t, r = A.take(BF16, 3 * 1024 + 128 * 3, f"l0wp{d}")
        wp.append(t)
        wpr.append(r)
    qa, qar, ka, kar = [], [], [], []
    for j in range(2):
        t, r = A.take(BF16, S, f"l0qa{j}")
        qa.append(t); qar.append(r)
        t, r = A.take(BF16, S, f"l0ka{j}")
        ka.append(t); kar.append(r)
    v_tm, v_tmr = A.take(BF16, NT * 130, "l0vtm")
    pt, ptr = [], []
    for j in range(2):
        t, r = A.take(BF16, 512, f"l0pt{j}")
        pt.append(t); ptr.append(r)
    att_bf, att_bfr = A.take_tiles(BF16, S, NC, "l0att")
    rden_bf, rden_bfr = A.take(BF16, 512, "l0rden")
    bc_sb, bc_sbr = A.take(BF16, 512, "l0bc")
    wm, wmr = A.take(BF16, 4096, "l0wm")
    bm_bc, bm_bcr = A.take(F32, 1024, "l0bmbc")
    res_t, res_tr = A.take(F32, 512, "l0res")

    P.dma("sync", bm_bc[:, :], W["L0_attn_bm"][None].broadcast_to([128, 1024]),
          writes=[bm_bcr], tag="l0_bmbc")

    def load_pair(hp, d):
        h0 = 2 * hp
        for mi, (mat, bias) in enumerate((("Wq", "bq"), ("Wk", "bk"), ("Wv", "bv"))):
            src = W[f"L0_attn_{mat}"][:, h0 * HS:(h0 + 2) * HS].rearrange(
                "(c p) n -> p c n", p=128)
            dst = wp[d][:, mi * 1024:(mi + 1) * 1024].rearrange("p (c n) -> p c n", c=NC)
            P.dma("gpsimd", dst, src, writes=[wpr[d]], tag=f"l0wp{hp}_{mat}")
            P.dma("gpsimd", wp[d][0:1, 3072 + mi * 128: 3072 + (mi + 1) * 128],
                  W[f"L0_attn_{bias}"][None][:, h0 * HS:(h0 + 2) * HS],
                  writes=[wpr[d]], tag=f"l0bp{hp}_{bias}")

    load_pair(0, 0)
    maskv = {off: C.hc(f"l0m{off}", 512) for off in (-384, -256, -128, 0)}
    sc8 = P.const_col(0.125)

    for hp in range(H // 2):
        d = hp % 2
        if hp + 1 < H // 2:
            load_pair(hp + 1, 1 - d)
        h0 = 2 * hp
        wq_ap = lambda c, d=d: wp[d][:, 0 * 1024 + c * 128: 0 * 1024 + (c + 1) * 128]
        wk_ap = lambda c, d=d: wp[d][:, 1 * 1024 + c * 128: 1 * 1024 + (c + 1) * 128]
        wv_ap = lambda c, d=d: wp[d][:, 2 * 1024 + c * 128: 2 * 1024 + (c + 1) * 128]
        b_ap = lambda mi, d=d: wp[d][0:1, 3072 + mi * 128: 3072 + (mi + 1) * 128]

        for (w_ap, bi, dsts, dstrs, qscale) in ((wq_ap, 0, qa, qar, True),
                                                (wk_ap, 1, ka, kar, False)):
            for n0 in range(0, S, 512):
                pb, pbr = C.pb[0], C.pbr[0]

                def grp(e, w_ap=w_ap, bi=bi, n0=n0, pb=pb, b_ap=b_ap):
                    for c in range(NC):
                        e.matmul(pb[:, :], w_ap(c), C.ln_fm[:, c * S + n0: c * S + n0 + 512],
                                 start=(c == 0), stop=False)
                    return e.matmul(pb[:, :], b_ap(bi), C.ones_bf[0:1, :],
                                    start=False, stop=True)
                P.op("tensor", grp, reads=[wpr[d], C.hcbr] + C.ln_fmr,
                     writes=[pbr], tag=f"l0p{hp}")
                for j in range(2):
                    rows = slice(j * 64, j * 64 + 64)
                    if qscale:
                        P.op("scalar", lambda e, pb=pb, j=j, n0=n0, rows=rows, dsts=dsts: e.activation(
                            dsts[j][0:64, n0:n0 + 512], pb[rows, :], AF.Copy, scale=sc8[0:64, :]),
                            reads=[pbr, P.consts_region], writes=[dstrs[j]], tag=f"l0pe{hp}")
                    else:
                        P.op("scalar", lambda e, pb=pb, j=j, n0=n0, rows=rows, dsts=dsts: e.copy(
                            dsts[j][0:64, n0:n0 + 512], pb[rows, :]),
                            reads=[pbr], writes=[dstrs[j]], tag=f"l0pe{hp}")
        for j in range(2):
            P.op("vector", lambda e, j=j: e.memset(qa[j][64:65, :], -SHIFT),
                 writes=[qar[j]], tag="qrow")
            P.op("vector", lambda e, j=j: e.memset(ka[j][64:65, :], 1.0),
                 writes=[kar[j]], tag="krow")

        for i in range(NT):
            pb, pbr = C.pb[1], C.pbr[1]

            def grpv(e, i=i, pb=pb, wv_ap=wv_ap, b_ap=b_ap):
                for c in range(NC):
                    e.matmul(pb[:, 0:128], C.ln_fm[:, c * S + i * 128: c * S + (i + 1) * 128],
                             wv_ap(c), start=(c == 0), stop=False)
                return e.matmul(pb[:, 0:128], C.ones_bf[0:1, 0:128], b_ap(2),
                                start=False, stop=True)
            P.op("tensor", grpv, reads=[wpr[d], C.hcbr] + C.ln_fmr,
                 writes=[pbr], tag=f"l0v{hp}_{i}")
            for j in range(2):
                P.op("scalar", lambda e, pb=pb, i=i, j=j: e.copy(
                    v_tm[:, i * 130 + j * 65: i * 130 + j * 65 + 64], pb[:, j * 64:(j + 1) * 64]),
                    reads=[pbr], writes=[v_tmr], tag=f"l0ve{hp}_{i}_{j}")
        if hp == 0:
            P.op("vector", lambda e: e.memset(
                v_tm.rearrange("p (t j c) -> p t j c", t=NT, j=2)[:, :, :, 64:65], 1.0),
                writes=[v_tmr], tag="vones")

        for j in range(2):
            h = h0 + j
            for n0 in range(0, S, 512):
                po, por = C.pb[2 + j], C.pbr[2 + j]
                n_tk = n0 // 128 + 4
                for tk in range(n_tk):
                    sbank = 4 + (tk % 2)
                    ps_, psr_ = C.pb[sbank], C.pbr[sbank]
                    P.op("tensor", lambda e, tk=tk, n0=n0, j=j, ps_=ps_: e.matmul(
                        ps_[:, :], ka[j][0:65, tk * 128:(tk + 1) * 128], qa[j][0:65, n0:n0 + 512],
                        start=True, stop=True),
                        reads=[kar[j], qar[j]], writes=[psr_], tag=f"l0s{h}_{n0}_{tk}")
                    ptj, ptjr = pt[tk % 2], ptr[tk % 2]
                    P.op("scalar", lambda e, ps_=ps_, ptj=ptj: e.activation(
                        ptj[:, :], ps_[:, :], AF.Exp),
                        reads=[psr_], writes=[ptjr], tag=f"l0e{h}_{n0}_{tk}")
                    off = n0 - tk * 128
                    if off in maskv:
                        P.op("gpsimd", lambda e, ptj=ptj, off=off: e.tensor_tensor(
                            ptj[:, :], ptj[:, :], maskv[off], ALU.mult),
                            reads=[ptjr, C.hcbr], writes=[ptjr], tag=f"l0m{h}_{n0}_{tk}")
                    P.op("tensor", lambda e, tk=tk, po=po, ptj=ptj, n_tk=n_tk, j=j: e.matmul(
                        po[0:65, :], v_tm[:, tk * 130 + j * 65: tk * 130 + j * 65 + 65],
                        ptj[:, :], start=(tk == 0), stop=(tk == n_tk - 1)),
                        reads=[v_tmr, ptjr], writes=[por], tag=f"l0a{h}_{n0}_{tk}")
                P.op("vector", lambda e, po=po: e.reciprocal(
                    rden_bf[0:1, :], po[64:65, :]),
                    reads=[por], writes=[rden_bfr], tag=f"l0r{h}_{n0}")
                pb2, pb2r = C.pb[1], C.pbr[1]
                P.op("tensor", lambda e, pb2=pb2: e.matmul(
                    pb2[0:64, :], C.ones_bf[0:1, 0:64], rden_bf[0:1, :],
                    start=True, stop=True),
                    reads=[rden_bfr, C.hcbr], writes=[pb2r], tag=f"l0b{h}_{n0}")
                P.op("scalar", lambda e, pb2=pb2: e.copy(bc_sb[0:64, :], pb2[0:64, :]),
                     reads=[pb2r], writes=[bc_sbr], tag=f"l0bc{h}_{n0}")
                c = h // 2
                rows = slice((h % 2) * 64, (h % 2) * 64 + 64)
                P.op("vector", lambda e, po=po, c=c, rows=rows, n0=n0: e.tensor_tensor(
                    att_bf[rows, c * S + n0: c * S + n0 + 512], po[0:64, :], bc_sb[0:64, :], ALU.mult),
                    reads=[por, bc_sbr], writes=[att_bfr[c]], tag=f"l0n{h}_{n0}")

    for n0 in (0, 512):
        src = W["L0_attn_Wm"][:, n0:n0 + 512].rearrange("(c p) n -> p c n", p=128)
        P.dma("gpsimd", wm.rearrange("p (c n) -> p c n", c=NC), src,
              writes=[wmr], tag=f"l0wm{n0}")
        for i in range(NT):
            pb, pbr = C.pb[5], C.pbr[5]

            def grpo(e, i=i, pb=pb):
                last = None
                for c in range(NC):
                    last = e.matmul(pb[:, :], att_bf[:, c * S + i * 128: c * S + (i + 1) * 128],
                                    wm[:, c * 512:(c + 1) * 512],
                                    start=(c == 0), stop=(c == NC - 1))
                return last
            P.op("tensor", grpo, reads=[wmr] + att_bfr, writes=[pbr], tag=f"l0o{i}_{n0}")
            x_ap = C.X[:, i * 1024 + n0: i * 1024 + n0 + 512]
            P.op("vector", lambda e, pb=pb, n0=n0: e.tensor_tensor(
                res_t[:, :], pb[:, :], bm_bc[:, n0:n0 + 512], ALU.add),
                reads=[pbr, bm_bcr], writes=[res_tr], tag=f"l0x1_{i}_{n0}")
            P.op("vector", lambda e, x_ap=x_ap: e.tensor_tensor(
                x_ap, x_ap, res_t[:, :], ALU.add),
                reads=[res_tr, C.Xr[i]], writes=[C.Xr[i]], tag=f"l0x2_{i}_{n0}")
    C.arena.reset()



def emit_l2_ffn(C: Core, W):
    P = C.P
    A = C.arena
    nc = P.nc
    emit_layernorm(C)
    U8 = mybir.dt.uint8
    w0d = nc.dram_tensor("l2f_w0", [8, 128, 4096], BF16).ap()
    w1d = nc.dram_tensor("l2f_w1", [32, 128, 4096], BF16).ap()
    w2d = nc.dram_tensor("l2f_w2", [32, 128, 1024], BF16).ap()
    w0dr, w1dr, w2dr = Region("l2fw0d"), Region("l2fw1d"), Region("l2fw2d")

    h1, h1r = A.take_tiles(BF16, 512, 32, "l2f_h1")
    res_ap = A.t32[:, (A.off - 32 * 512 * 2) // 4:(A.off - 32 * 512 * 2) // 4 + 512]
    h2, h2r = A.take_tiles(BF16, 512, 32, "l2f_h2")
    wst, wstr = A.take(BF16, 16 * 512, "l2f_wst")
    wst_off_bytes = (A.off - 16 * 512 * 2)
    u8a = A.tu8[:, wst_off_bytes:wst_off_bytes + 4096]
    u8b = A.tu8[:, wst_off_bytes + 4096:wst_off_bytes + 8192]
    cvt = A.tbf[:, (wst_off_bytes + 8192) // 2:(wst_off_bytes + 8192) // 2 + 4096]
    u8ar = u8br = cvtr = wstr
    scc, sccr = A.take(F32, 8, "l2f_scc")
    h1_byte0 = None

    for j, nm in enumerate(("s0", "s1", "s2")):
        P.dma("sync", scc[:, j:j + 1], W[f"L2_ffn_{nm}"][None].broadcast_to([128, 1]),
              writes=[sccr], tag=f"l2f_{nm}")
    P.op("vector", lambda e: e.tensor_tensor(scc[:, 4:7], scc[:, 0:3], scc[:, 0:3], ALU.add),
         reads=[sccr], writes=[sccr], tag="l2f_2s")

    k0 = W["L2_ffn_k0"]
    for c in range(8):
        P.dma("sync", u8a[:, :], k0[c * 128:(c + 1) * 128, :], writes=[u8ar], tag=f"l2fk0a{c}")
        P.dma("sync", u8b[:, :], k0[1024 + c * 128: 1024 + (c + 1) * 128, :], writes=[u8br], tag=f"l2fk0b{c}")
        P.op("vector", lambda e: e.tensor_tensor(cvt[:, :], u8a[:, :], u8b[:, :], ALU.subtract),
             reads=[u8ar, u8br], writes=[cvtr], tag=f"l2fc0{c}")
        P.dma("gpsimd", w0d[c], cvt[:, :], reads=[cvtr], writes=[w0dr], tag=f"l2fs0{c}")
    for c in range(32):
        P.dma("sync", u8a[:, :], W["L2_ffn_k1"][c * 128:(c + 1) * 128, :], writes=[u8ar], tag=f"l2fk1{c}")
        P.op("gpsimd", lambda e: e.tensor_scalar(cvt[:, :], u8a[:, :], 2.0, -1.0, ALU.mult, ALU.add),
             reads=[u8ar], writes=[cvtr], tag=f"l2fc1{c}")
        P.dma("gpsimd", w1d[c], cvt[:, :], reads=[cvtr], writes=[w1dr], tag=f"l2fs1{c}")
    for c in range(32):
        P.dma("sync", u8a[:, 0:1024], W["L2_ffn_k2"][c * 128:(c + 1) * 128, :], writes=[u8ar], tag=f"l2fk2{c}")
        P.op("gpsimd", lambda e: e.tensor_scalar(cvt[:, 0:1024], u8a[:, 0:1024], 2.0, -1.0, ALU.mult, ALU.add),
             reads=[u8ar], writes=[cvtr], tag=f"l2fc2{c}")
        P.dma("gpsimd", w2d[c], cvt[:, 0:1024], reads=[cvtr], writes=[w2dr], tag=f"l2fs2{c}")

    import os
    dbg = None
    if os.environ.get("L2F_DEBUG"):
        dbg = nc.dram_tensor("l2f_dbg", [2, 128, 32 * 512], F32, kind="ExternalOutput").ap()
        dbgr = Region("l2fdbg")
    for tq in range(4):
        t0 = tq * 512
        for jg in range(8):
            P.dma("sync", wst.rearrange("p (c n) -> p c n", n=512)[:, 0:8, :],
                  w0d[:, :, jg * 512:(jg + 1) * 512].transpose([1, 0, 2]), reads=[w0dr], writes=[wstr], tag=f"l2fw0{tq}_{jg}")
            for jj in range(4):
                pb, pbr = C.pb[jj], C.pbr[jj]

                def g1(e, jj=jj, pb=pb, t0=t0):
                    last = None
                    for c in range(8):
                        last = e.matmul(pb[:, :], wst[:, c * 512 + jj * 128: c * 512 + (jj + 1) * 128],
                                        C.ln_fm[:, c * S + t0: c * S + t0 + 512],
                                        start=(c == 0), stop=(c == 7))
                    return last
                P.op("tensor", g1, reads=[wstr] + C.ln_fmr, writes=[pbr], tag=f"l2f1_{tq}_{jg}_{jj}")
                j = jg * 4 + jj
                P.op("scalar", lambda e, pb=pb, j=j: e.activation(
                    h1[:, j * 512:(j + 1) * 512], pb[:, :], AF.Relu, scale=scc[:, 4:5]),
                    reads=[pbr, sccr], writes=[h1r[j]], tag=f"l2fe1_{tq}_{j}")
        if dbg is not None and tq == 0:
            P.dma("gpsimd", dbg[0].rearrange("p (c n) -> p c n", n=512),
                  h1.rearrange("p (c n) -> p c n", n=512), reads=h1r, writes=[dbgr], tag="dbg_h1")
        for jg in range(8):
            for ch in range(2):
                P.dma("sync", wst.rearrange("p (c n) -> p c n", c=16)[:, 0:16, 0:512],
                      w1d[ch * 16:(ch + 1) * 16, :, jg * 512:(jg + 1) * 512].transpose([1, 0, 2]),
                      reads=[w1dr], writes=[wstr], tag=f"l2fw1{tq}_{jg}_{ch}")
                for jj in range(4):
                    pb, pbr = C.pb[jj], C.pbr[jj]

                    def g2(e, jj=jj, pb=pb, ch=ch):
                        last = None
                        for c in range(16):
                            last = e.matmul(pb[:, :], wst[:, c * 512 + jj * 128: c * 512 + (jj + 1) * 128],
                                            h1[:, (ch * 16 + c) * 512:(ch * 16 + c + 1) * 512],
                                            start=(ch == 0 and c == 0), stop=(ch == 1 and c == 15))
                        return last
                    P.op("tensor", g2, reads=[wstr] + h1r, writes=[pbr], tag=f"l2f2_{tq}_{jg}_{jj}_{ch}")
                    if ch == 1:
                        j = jg * 4 + jj
                        P.op("scalar", lambda e, pb=pb, j=j: e.activation(
                            h2[:, j * 512:(j + 1) * 512], pb[:, :], AF.Relu, scale=scc[:, 5:6]),
                            reads=[pbr, sccr], writes=[h2r[j]], tag=f"l2fe2_{tq}_{j}")
        if dbg is not None and tq == 0:
            P.dma("gpsimd", dbg[1].rearrange("p (c n) -> p c n", n=512),
                  h2.rearrange("p (c n) -> p c n", n=512), reads=h2r, writes=[dbgr], tag="dbg_h2")
        for hh in range(2):
            for ch in range(2):
                P.dma("sync", wst.rearrange("p (c n) -> p c n", c=16)[:, 0:16, 0:512],
                      w2d[ch * 16:(ch + 1) * 16, :, hh * 512:(hh + 1) * 512].transpose([1, 0, 2]),
                      reads=[w2dr], writes=[wstr], tag=f"l2fw2{tq}_{hh}_{ch}")
                for ti in range(4):
                    i = tq * 4 + ti
                    pb, pbr = C.pb[2 + ti], C.pbr[2 + ti]

                    def g3(e, ti=ti, pb=pb, ch=ch):
                        last = None
                        for c in range(16):
                            last = e.matmul(pb[:, :], h2[:, (ch * 16 + c) * 512 + ti * 128: (ch * 16 + c) * 512 + (ti + 1) * 128],
                                            wst[:, c * 512:(c + 1) * 512],
                                            start=(ch == 0 and c == 0), stop=(ch == 1 and c == 15))
                        return last
                    P.op("tensor", g3, reads=[wstr] + h2r, writes=[pbr], tag=f"l2f3_{tq}_{hh}_{ti}_{ch}")
                    if ch == 1:
                        P.op("scalar", lambda e, pb=pb: e.activation(
                            res_ap, pb[:, :], AF.Copy, scale=scc[:, 6:7]),
                            reads=[pbr, sccr], writes=[h1r[0]], tag=f"l2fe3_{tq}_{hh}_{ti}")
                        x_ap = C.X[:, i * 1024 + hh * 512: i * 1024 + hh * 512 + 512]
                        P.op("vector", lambda e, x_ap=x_ap: e.tensor_tensor(
                            x_ap, x_ap, res_ap, ALU.add),
                            reads=[h1r[0], C.Xr[i]], writes=[C.Xr[i]], tag=f"l2fx_{tq}_{hh}_{ti}")
    C.arena.reset()



def emit_l3_attn(C: Core, W):
    P = C.P
    A = C.arena
    H, HS = 16, 64
    emit_layernorm(C)

    wp, wpr = [], []
    for d in range(2):
        t, r = A.take(BF16, 3 * 1024, f"l3a_wp{d}")
        wp.append(t); wpr.append(r)
    fq, fqr = A.take(BF16, S, "l3a_q")
    fk, fkr = A.take(BF16, S, "l3a_k")
    fktm, fktmr = A.take(BF16, NT * 128, "l3a_ktm")
    vtm, vtmr = A.take(BF16, NT * 128, "l3a_vtm")
    KC_off = A.off
    KC, KCr = A.take(F32, S, "l3a_KC")
    qk_off = A.off
    qk, qkr = A.take(F32, S, "l3a_qk")
    zrow, zrowr = qk[64:66, :] if False else None, None
    zbc, zbcr = A.take(BF16, S, "l3a_zbc")
    zrow, zrowr = qk[64:65, 0:S], qkr
    fs1, fs1r = A.take(F32, 512, "l3a_fs1")
    fs2, fs2r = A.take(F32, 512, "l3a_fs2")
    ptb, ptbr = A.take(BF16, 128, "l3a_ptb")
    fs3, fs3r = fs2, fs2r
    Ssb, Ssbr = A.take(BF16, 128, "l3a_S")
    Ssb32, Ssb32r = A.take(F32, 64, "l3a_S32")
    att, attr = A.take_tiles(BF16, S, NC, "l3a_att")
    wo, wor = A.tbf[:, KC_off // 2: KC_off // 2 + 4096], KCr
    bo_bc, bo_bcr = A.t32[:, qk_off // 4: qk_off // 4 + 1024], qkr
    res, resr = fs1, fs1r

    def load_pair(hp, d):
        h0 = 2 * hp
        for mi, mat in enumerate(("Wq", "Wk", "Wv")):
            src = W[f"L3_attn_{mat}"][:, h0 * HS:(h0 + 2) * HS].rearrange(
                "(c p) n -> p c n", p=128)
            dst = wp[d][:, mi * 1024:(mi + 1) * 1024].rearrange("p (c n) -> p c n", c=NC)
            P.dma("gpsimd", dst, src, writes=[wpr[d]], tag=f"l3awp{hp}_{mat}")

    load_pair(0, 0)
    eps_c = P.const_col(1e-6)
    triu = C.hc("l0m0", 512)[:, 0:128]

    P.op("vector", lambda e: e.memset(KC[0:64, :], 0.0), writes=[KCr], tag="l3a_kc0")

    for hp in range(H // 2):
        d = hp % 2
        if hp + 1 < H // 2:
            load_pair(hp + 1, 1 - d)
        h0 = 2 * hp

        for (mi, dst, dstr_) in ((0, fq, fqr), (1, fk, fkr)):
            for n0 in range(0, S, 512):
                pb, pbr = C.pb[0], C.pbr[0]

                def grp(e, mi=mi, n0=n0, pb=pb, d=d):
                    last = None
                    for c in range(NC):
                        last = e.matmul(pb[:, :], wp[d][:, mi * 1024 + c * 128:mi * 1024 + (c + 1) * 128],
                                        C.ln_fm[:, c * S + n0: c * S + n0 + 512],
                                        start=(c == 0), stop=(c == NC - 1))
                    return last
                P.op("tensor", grp, reads=[wpr[d]] + C.ln_fmr, writes=[pbr], tag=f"l3ap{hp}{mi}")
                P.op("vector", lambda e, pb=pb: e.tensor_scalar_max(fs1[:, :], pb[:, :], 0.0),
                     reads=[pbr], writes=[fs1r], tag="l3af1")
                P.op("vector", lambda e, pb=pb: e.tensor_scalar_min(fs2[:, :], pb[:, :], 0.0),
                     reads=[pbr], writes=[fs2r], tag="l3af2")
                P.op("scalar", lambda e: e.activation(fs2[:, :], fs2[:, :], AF.Exp),
                     reads=[fs2r], writes=[fs2r], tag="l3af3")
                P.op("vector", lambda e, dst=dst, n0=n0: e.tensor_tensor(
                    dst[:, n0:n0 + 512], fs1[:, :], fs2[:, :], ALU.add),
                    reads=[fs1r, fs2r], writes=[dstr_], tag="l3af4")

        for i in range(NT):
            pb, pbr = C.pb[1], C.pbr[1]

            def grpv(e, i=i, pb=pb, d=d):
                last = None
                for c in range(NC):
                    last = e.matmul(pb[:, 0:128], C.ln_fm[:, c * S + i * 128: c * S + (i + 1) * 128],
                                    wp[d][:, 2 * 1024 + c * 128: 2 * 1024 + (c + 1) * 128],
                                    start=(c == 0), stop=(c == NC - 1))
                return last
            P.op("tensor", grpv, reads=[wpr[d]] + C.ln_fmr, writes=[pbr], tag=f"l3av{hp}_{i}")
            P.op("scalar", lambda e, pb=pb, i=i: e.copy(vtm[:, i * 128:(i + 1) * 128], pb[:, 0:128]),
                 reads=[pbr], writes=[vtmr], tag=f"l3ave{hp}_{i}")

        for i in range(NT):
            p_ap = C.psbf[:, 0:128]
            P.op("tensor", lambda e, i=i, p_ap=p_ap: e.transpose(
                p_ap, fk[:, i * 128:(i + 1) * 128], C.ident),
                reads=[fkr, C.hcbr], writes=[C.psbfr], tag=f"l3akt{hp}_{i}")
            P.op("vector", lambda e, i=i, p_ap=p_ap: e.tensor_copy(
                fktm[:, i * 128:(i + 1) * 128], p_ap),
                reads=[C.psbfr], writes=[fktmr], tag=f"l3akte{hp}_{i}")

        for j in range(2):
            h = h0 + j
            rows = slice(j * 64, (j + 1) * 64)
            if j == 0:
                P.op("vector", lambda e, rows=rows: e.tensor_tensor(
                    KC[0:64, :], KC[0:64, :], fk[rows, :], ALU.add),
                    reads=[KCr, fkr], writes=[KCr], tag=f"l3akc{h}")
                P.op("vector", lambda e, rows=rows: e.tensor_tensor(
                    qk[0:64, :], fq[rows, :], KC[0:64, :], ALU.mult),
                    reads=[fqr, KCr], writes=[qkr], tag=f"l3aqk{h}")
            else:
                P.op("vector", lambda e: e.tensor_copy(zbc[0:64, :], fk[64:128, :]),
                     reads=[fkr], writes=[zbcr], tag=f"l3akst{h}")
                P.op("vector", lambda e: e.tensor_tensor(
                    KC[0:64, :], KC[0:64, :], zbc[0:64, :], ALU.add),
                    reads=[KCr, zbcr], writes=[KCr], tag=f"l3akc{h}")
                P.op("vector", lambda e: e.tensor_copy(qk[0:64, :], fq[64:128, :]),
                     reads=[fqr], writes=[qkr], tag=f"l3aqst{h}")
                P.op("vector", lambda e: e.tensor_tensor(
                    qk[0:64, :], qk[0:64, :], KC[0:64, :], ALU.mult),
                    reads=[qkr, KCr], writes=[qkr], tag=f"l3aqk{h}")
            P.op("vector", lambda e: e.tensor_copy(zbc[0:64, :], qk[0:64, :]),
                 reads=[qkr], writes=[zbcr], tag=f"l3aqkc{h}")
            for n0 in range(0, S, 512):
                pb, pbr = C.pb[2], C.pbr[2]
                P.op("tensor", lambda e, n0=n0, pb=pb: e.matmul(
                    pb[0:1, :], C.onecol_bf[0:64, 0:1], zbc[0:64, n0:n0 + 512],
                    start=True, stop=True),
                    reads=[zbcr, C.hcbr], writes=[pbr], tag=f"l3az{h}_{n0}")
                P.op("vector", lambda e, pb=pb, n0=n0: e.tensor_scalar(
                    zrow[:, n0:n0 + 512], pb[0:1, :], eps_c[0:1, :], None, ALU.add),
                    reads=[pbr, P.consts_region], writes=[zrowr], tag=f"l3aze{h}_{n0}")
            P.op("vector", lambda e: e.reciprocal(zrow[:, :], zrow[:, :]),
                 reads=[zrowr], writes=[zrowr], tag=f"l3azr{h}")
            P.op("vector", lambda e: e.tensor_copy(zbc[0:1, :], zrow[:, :]),
                 reads=[zrowr], writes=[zbcr], tag=f"l3azc{h}")
            for n0 in range(0, S, 512):
                pb, pbr = C.pb[2], C.pbr[2]
                P.op("tensor", lambda e, n0=n0, pb=pb: e.matmul(
                    pb[0:64, :], C.ones_bf[0:1, 0:64], zbc[0:1, n0:n0 + 512],
                    start=True, stop=True),
                    reads=[zbcr, C.hcbr], writes=[pbr], tag=f"l3azb{h}_{n0}")
                P.op("scalar", lambda e, pb=pb, n0=n0: e.copy(zbc[0:64, n0:n0 + 512], pb[0:64, :]),
                     reads=[pbr], writes=[zbcr], tag=f"l3azbe{h}_{n0}")
            if j == 0:
                P.op("vector", lambda e: e.tensor_tensor(
                    fq[0:64, :], fq[0:64, :], zbc[0:64, :], ALU.mult),
                    reads=[fqr, zbcr], writes=[fqr], tag=f"l3aqz{h}")
            else:
                P.op("vector", lambda e: e.tensor_copy(zbc[64:128, :], zbc[0:64, :]),
                     reads=[zbcr], writes=[zbcr], tag=f"l3aqzs{h}")
                P.op("vector", lambda e: e.tensor_tensor(
                    fq[64:128, :], fq[64:128, :], zbc[64:128, :], ALU.mult),
                    reads=[fqr, zbcr], writes=[fqr], tag=f"l3aqz{h}")

            P.op("vector", lambda e, j=j: e.memset(Ssb[j * 64:(j + 1) * 64, 0:64], 0.0),
                 writes=[Ssbr], tag=f"l3as0{h}")
            P.op("vector", lambda e, j=j: e.memset(Ssb32[j * 64:(j + 1) * 64, :], 0.0),
                 writes=[Ssb32r], tag=f"l3as032{h}")
            pS, pSr = C.pb[3 + j], C.pbr[3 + j]
            for cchunk in range(NT):
                pb, pbr = C.pb[5], C.pbr[5]
                P.op("tensor", lambda e, cchunk=cchunk, rows=rows, pb=pb: e.matmul(
                    pb[:, 0:128], fk[rows, cchunk * 128:(cchunk + 1) * 128],
                    fq[rows, cchunk * 128:(cchunk + 1) * 128], start=True, stop=True),
                    reads=[fkr, fqr], writes=[pbr], tag=f"l3ast{h}_{cchunk}")
                P.op("vector", lambda e, pb=pb: e.tensor_tensor(
                    ptb[:, :], pb[:, 0:128], triu, ALU.mult),
                    reads=[pbr, C.hcbr], writes=[ptbr], tag=f"l3am{h}_{cchunk}")
                po, por = C.pb[2], C.pbr[2]

                def outg(e, cchunk=cchunk, rows=rows, j=j, po=po):
                    e.matmul(po[0:64, 0:128], vtm[:, cchunk * 128 + j * 64: cchunk * 128 + (j + 1) * 64],
                             ptb[:, :], start=True, stop=False)
                    return e.matmul(po[0:64, 0:128], Ssb[j * 64:(j + 1) * 64, 0:64],
                                    fq[rows, cchunk * 128:(cchunk + 1) * 128], start=False, stop=True)
                P.op("tensor", outg, reads=[vtmr, ptbr, Ssbr, fqr], writes=[por],
                     tag=f"l3ao{h}_{cchunk}")
                P.op("tensor", lambda e, cchunk=cchunk, j=j, pS=pS: e.matmul(
                    pS[0:64, 0:64], fktm[:, cchunk * 128 + j * 64: cchunk * 128 + (j + 1) * 64],
                    vtm[:, cchunk * 128 + j * 64: cchunk * 128 + (j + 1) * 64],
                    start=True, stop=True),
                    reads=[fktmr, vtmr], writes=[pSr], tag=f"l3asu{h}_{cchunk}")
                P.op("vector", lambda e, j=j, pS=pS: e.tensor_tensor(
                    Ssb32[j * 64:(j + 1) * 64, :], Ssb32[j * 64:(j + 1) * 64, :], pS[0:64, 0:64], ALU.add),
                    reads=[pSr, Ssb32r], writes=[Ssb32r], tag=f"l3asa{h}_{cchunk}")
                P.op("scalar", lambda e, j=j: e.copy(Ssb[j * 64:(j + 1) * 64, 0:64], Ssb32[j * 64:(j + 1) * 64, :]),
                     reads=[Ssb32r], writes=[Ssbr], tag=f"l3ase{h}_{cchunk}")
                c = h // 2
                arow = slice((h % 2) * 64, (h % 2) * 64 + 64)
                P.op("scalar", lambda e, po=po, c=c, arow=arow, cchunk=cchunk: e.copy(
                    att[arow, c * S + cchunk * 128: c * S + (cchunk + 1) * 128], po[0:64, 0:128]),
                    reads=[por], writes=[attr[c]], tag=f"l3aoe{h}_{cchunk}")

    P.dma("sync", bo_bc[:, :], W["L3_attn_bo"][None].broadcast_to([128, 1024]),
          writes=[bo_bcr], tag="l3a_bobc")
    for n0 in (0, 512):
        src = W["L3_attn_Wo"][:, n0:n0 + 512].rearrange("(c p) n -> p c n", p=128)
        P.dma("gpsimd", wo.rearrange("p (c n) -> p c n", c=NC), src, writes=[wor], tag=f"l3awo{n0}")
        for i in range(NT):
            pb, pbr = C.pb[5], C.pbr[5]

            def grpo(e, i=i, pb=pb):
                last = None
                for c in range(NC):
                    last = e.matmul(pb[:, :], att[:, c * S + i * 128: c * S + (i + 1) * 128],
                                    wo[:, c * 512:(c + 1) * 512],
                                    start=(c == 0), stop=(c == NC - 1))
                return last
            P.op("tensor", grpo, reads=[wor] + attr, writes=[pbr], tag=f"l3aop{i}_{n0}")
            x_ap = C.X[:, i * 1024 + n0: i * 1024 + n0 + 512]
            P.op("vector", lambda e, pb=pb, n0=n0: e.tensor_tensor(
                res[:, :], pb[:, :], bo_bc[:, n0:n0 + 512], ALU.add),
                reads=[pbr, bo_bcr], writes=[resr], tag=f"l3ax1_{i}_{n0}")
            P.op("vector", lambda e, x_ap=x_ap: e.tensor_tensor(
                x_ap, x_ap, res[:, :], ALU.add),
                reads=[resr, C.Xr[i]], writes=[C.Xr[i]], tag=f"l3ax2_{i}_{n0}")
    C.arena.reset()



def emit_l1_ffn(C: Core, W):
    P = C.P
    A = C.arena
    emit_layernorm(C)
    E, I = 8, 512

    gw, gwr = A.take(BF16, 1024, "l1f_gw")
    cwd = P.nc.dram_tensor("l1f_cwd", [8, S], BF16).ap()
    cwdr = Region("l1f_cwd")
    gwT, gwTr = A.take(BF16, 64, "l1f_gwT")
    logi, logir = A.take(F32, 16, "l1f_logi")
    cw, cwr = A.take(F32, NT * 8, "l1f_cw")
    cwbc, cwbcr = A.take(BF16, 512, "l1f_cwbc")
    inter, interr = A.take_tiles(BF16, 512, 36, "l1f_int")
    wg, wgr = A.take(BF16, 8 * 512, "l1f_wg")
    wu, wur = A.take(BF16, 8 * 512, "l1f_wu")
    wd, wdr = A.take(BF16, 4 * 1024, "l1f_wd")
    fsg, fsgr = A.take(F32, 512, "l1f_fsg")

    P.dma("gpsimd", gw[0:8, :], W["L1_ffn_gate_w"], writes=[gwr], tag="l1f_gw")
    for c in range(NC):
        p_ap = C.psbf[:, 0:8]
        P.op("tensor", lambda e, c=c, p_ap=p_ap: e.transpose(
            p_ap, gw[0:8, c * 128:(c + 1) * 128], C.ident[0:8, 0:8]),
            reads=[gwr, C.hcbr], writes=[C.psbfr], tag=f"l1f_gwt{c}")
        P.op("vector", lambda e, c=c, p_ap=p_ap: e.tensor_copy(gwT[:, c * 8:(c + 1) * 8], p_ap),
             reads=[C.psbfr], writes=[gwTr], tag=f"l1f_gwte{c}")

    eps20 = P.const_col(1e-20)
    L = logi
    pcols = lambda i: cw[:, i * 8:(i + 1) * 8]
    for i in range(NT):
        pb, pbr = C.pb[0], C.pbr[0]

        def gl(e, i=i, pb=pb):
            last = None
            for c in range(NC):
                last = e.matmul(pb[:, 0:8], C.ln_fm[:, c * S + i * 128: c * S + (i + 1) * 128],
                                gwT[:, c * 8:(c + 1) * 8], start=(c == 0), stop=(c == NC - 1))
            return last
        P.op("tensor", gl, reads=[gwTr] + C.ln_fmr, writes=[pbr], tag=f"l1f_gl{i}")
        P.op("vector", lambda e, pb=pb: e.reduce_max(L[:, 0:1], pb[:, 0:8], axis=AX.X),
             reads=[pbr], writes=[logir], tag="l1f_m")
        P.op("vector", lambda e: e.tensor_scalar_mul(L[:, 1:2], L[:, 0:1], -1.0),
             reads=[logir], writes=[logir], tag="l1f_nm")
        P.op("scalar", lambda e, pb=pb, i=i: e.activation(
            cw[:, i * 8:(i + 1) * 8], pb[:, 0:8], AF.Exp, bias=L[:, 1:2], accum_out=L[:, 2:3]),
            reads=[pbr, logir], writes=[cwr, logir], tag="l1f_exp")
        pcols = lambda i: cw[:, i * 8:(i + 1) * 8]
        P.op("vector", lambda e, i=i: e.reduce_max(L[:, 3:4], pcols(i), axis=AX.X),
             reads=[cwr], writes=[logir], tag="l1f_m1")
        P.op("vector", lambda e, i=i: e.tensor_scalar(L[:, 8:16], pcols(i), L[:, 3:4], None, ALU.is_lt),
             reads=[cwr, logir], writes=[logir], tag="l1f_lt")
        P.op("vector", lambda e, i=i: e.tensor_tensor(L[:, 8:16], pcols(i), L[:, 8:16], ALU.mult),
             reads=[cwr, logir], writes=[logir], tag="l1f_pm")
        P.op("vector", lambda e: e.reduce_max(L[:, 4:5], L[:, 8:16], axis=AX.X),
             reads=[logir], writes=[logir], tag="l1f_m2")
        P.op("vector", lambda e: e.tensor_tensor(L[:, 5:6], L[:, 3:4], L[:, 4:5], ALU.add),
             reads=[logir], writes=[logir], tag="l1f_den")
        P.op("vector", lambda e: e.tensor_scalar(L[:, 6:7], L[:, 2:3], eps20[:, 0:1], None, ALU.mult),
             reads=[logir, P.consts_region], writes=[logir], tag="l1f_eps")
        P.op("vector", lambda e: e.tensor_tensor(L[:, 5:6], L[:, 5:6], L[:, 6:7], ALU.add),
             reads=[logir], writes=[logir], tag="l1f_den2")
        emit_recip_nr(P, L[:, 6:7], L[:, 5:6], L[:, 7:8], logir, f"l1f_rden{i}")
        P.op("vector", lambda e, i=i: e.tensor_scalar(L[:, 8:16], pcols(i), L[:, 4:5], None, ALU.is_ge),
             reads=[cwr, logir], writes=[logir], tag="l1f_keep")
        P.op("vector", lambda e, i=i: e.tensor_tensor(pcols(i), pcols(i), L[:, 8:16], ALU.mult),
             reads=[cwr, logir], writes=[cwr], tag="l1f_cwm")
        P.op("vector", lambda e, i=i: e.tensor_scalar(pcols(i), pcols(i), L[:, 6:7], None, ALU.mult),
             reads=[cwr, logir], writes=[cwr], tag="l1f_cwn")

    cwbf, cwbfr = A.take(BF16, 128, "l1f_cwbf")
    cwfm, cwfmr = A.take(BF16, S, "l1f_cwfm")
    for i in range(NT):
        P.op("vector", lambda e, i=i: e.tensor_copy(cwbf[:, 0:8], cw[:, i * 8:(i + 1) * 8]),
             reads=[cwr], writes=[cwbfr], tag=f"l1f_cwc{i}")
        p_ap = C.psbf[:, 0:128]
        P.op("tensor", lambda e, p_ap=p_ap: e.transpose(p_ap[0:8, 0:128], cwbf[:, 0:8], C.ident),
             reads=[cwbfr, C.hcbr], writes=[C.psbfr], tag=f"l1f_cwt{i}")
        P.op("vector", lambda e, i=i, p_ap=p_ap: e.tensor_copy(
            cwfm[0:8, i * 128:(i + 1) * 128], p_ap[0:8, 0:128]),
            reads=[C.psbfr], writes=[cwfmr], tag=f"l1f_cwte{i}")
    P.dma("gpsimd", cwd, cwfm[0:8, :], reads=[cwfmr], writes=[cwdr], tag="l1f_cwst")

    for tq in range(4):
        t0 = tq * 512
        for e_ in range(9):
            if e_ < 8:
                srcg = W["L1_ffn_Wg"][e_].rearrange("(c p) n -> p c n", p=128)
                srcu = W["L1_ffn_Wu"][e_].rearrange("(c p) n -> p c n", p=128)
            else:
                srcg = W["L1_ffn_sWg"].rearrange("(c p) n -> p c n", p=128)
                srcu = W["L1_ffn_sWu"].rearrange("(c p) n -> p c n", p=128)
            P.dma("gpsimd", wg.rearrange("p (c n) -> p c n", c=NC), srcg, writes=[wgr], tag=f"l1fwg{tq}_{e_}")
            P.dma("gpsimd", wu.rearrange("p (c n) -> p c n", c=NC), srcu, writes=[wur], tag=f"l1fwu{tq}_{e_}")
            for ic in range(4):
                pg, pgr = C.pb[0], C.pbr[0]
                pu, pur = C.pb[1], C.pbr[1]

                def gg(e, ic=ic, pg=pg, t0=t0):
                    last = None
                    for c in range(NC):
                        last = e.matmul(pg[:, :], wg[:, c * 512 + ic * 128: c * 512 + (ic + 1) * 128],
                                        C.ln_fm[:, c * S + t0: c * S + t0 + 512],
                                        start=(c == 0), stop=(c == NC - 1))
                    return last
                P.op("tensor", gg, reads=[wgr] + C.ln_fmr, writes=[pgr], tag=f"l1fg{tq}_{e_}_{ic}")

                def gu(e, ic=ic, pu=pu, t0=t0):
                    last = None
                    for c in range(NC):
                        last = e.matmul(pu[:, :], wu[:, c * 512 + ic * 128: c * 512 + (ic + 1) * 128],
                                        C.ln_fm[:, c * S + t0: c * S + t0 + 512],
                                        start=(c == 0), stop=(c == NC - 1))
                    return last
                P.op("tensor", gu, reads=[wur] + C.ln_fmr, writes=[pur], tag=f"l1fu{tq}_{e_}_{ic}")
                P.op("scalar", lambda e, pg=pg: e.activation(fsg[:, :], pg[:, :], AF.Gelu),
                     reads=[pgr], writes=[fsgr], tag=f"l1fge{tq}_{e_}_{ic}")
                dsti = inter[:, (e_ * 4 + ic) * 512:(e_ * 4 + ic + 1) * 512]
                P.op("vector", lambda e, pu=pu, dsti=dsti: e.tensor_tensor(
                    dsti, pu[:, :], fsg[:, :], ALU.mult),
                    reads=[pur, fsgr], writes=[interr[e_ * 4 + ic]], tag=f"l1fim{tq}_{e_}_{ic}")
            if e_ < 8:
                P.dma("gpsimd", cwbc[:, :],
                      cwd[e_, t0:t0 + 512][None].broadcast_to([128, 512]),
                      reads=[cwdr], writes=[cwbcr], tag=f"l1fcb{tq}_{e_}")
                for ic in range(4):
                    dsti = inter[:, (e_ * 4 + ic) * 512:(e_ * 4 + ic + 1) * 512]
                    P.op("gpsimd", lambda e, dsti=dsti, ic=ic: e.tensor_tensor(
                        dsti, dsti, cwbc[:, :], ALU.mult),
                        reads=[interr[e_ * 4 + ic], cwbcr], writes=[interr[e_ * 4 + ic]], tag=f"l1fsc{tq}_{e_}_{ic}")
        for tg in range(2):
            for e_ in range(9):
                if e_ < 8:
                    srcd = W["L1_ffn_Wd"][e_].rearrange("(c p) n -> p c n", p=128)
                else:
                    srcd = W["L1_ffn_sWd"].rearrange("(c p) n -> p c n", p=128)
                P.dma("gpsimd", wd.rearrange("p (c n) -> p c n", c=4), srcd, writes=[wdr], tag=f"l1fwd{tq}_{tg}_{e_}")
                for tt in range(2):
                    for hh in range(2):
                        bank = tt * 2 + hh
                        pb, pbr = C.pb[2 + bank], C.pbr[2 + bank]

                        def gd(e, e_=e_, tt=tt, hh=hh, tg=tg, pb=pb):
                            last = None
                            for ic in range(4):
                                last = e.matmul(
                                    pb[:, :],
                                    inter[:, (e_ * 4 + ic) * 512 + (tg * 2 + tt) * 128:
                                          (e_ * 4 + ic) * 512 + (tg * 2 + tt + 1) * 128],
                                    wd[:, ic * 1024 + hh * 512: ic * 1024 + (hh + 1) * 512],
                                    start=(e_ == 0 and ic == 0), stop=(e_ == 8 and ic == 3))
                            return last
                        P.op("tensor", gd, reads=[wdr] + interr, writes=[pbr],
                             tag=f"l1fd{tq}_{tg}_{e_}_{tt}_{hh}")
            for tt in range(2):
                for hh in range(2):
                    bank = tt * 2 + hh
                    pb, pbr = C.pb[2 + bank], C.pbr[2 + bank]
                    i = tq * 4 + tg * 2 + tt
                    x_ap = C.X[:, i * 1024 + hh * 512: i * 1024 + hh * 512 + 512]
                    P.op("vector", lambda e, pb=pb, x_ap=x_ap: e.tensor_tensor(
                        x_ap, x_ap, pb[:, :], ALU.add),
                        reads=[pbr, C.Xr[i]], writes=[C.Xr[i]], tag=f"l1fx{tq}_{tg}_{tt}_{hh}")
    C.arena.reset()



def _emit_round_clip(P, A, C, src_psum_or_sb, srcr, dst, dstr, tmp, tmpr, ti, tir,
                      rs_col, rscol_r, pre_clip, tag):
    ALUv = ALU
    if pre_clip:
        P.op("vector", lambda e: e.tensor_scalar(tmp, src_psum_or_sb, rs_col, -129.0, ALUv.mult, ALUv.max),
             reads=[srcr, rscol_r], writes=[tmpr], tag=tag + "_t1")
        P.op("vector", lambda e: e.tensor_scalar(tmp, tmp, 128.0, 256.5, ALUv.min, ALUv.add),
             reads=[tmpr], writes=[tmpr], tag=tag + "_t2")
    else:
        P.op("vector", lambda e: e.tensor_scalar(tmp, src_psum_or_sb, rs_col, 256.5, ALUv.mult, ALUv.add),
             reads=[srcr, rscol_r], writes=[tmpr], tag=tag + "_t1")
    P.op("scalar", lambda e: e.copy(ti, tmp), reads=[tmpr], writes=[tir], tag=tag + "_c")
    P.op("scalar", lambda e: e.copy(src_psum_or_sb, ti), reads=[tir], writes=[srcr], tag=tag + "_cf")
    P.op("vector", lambda e: e.tensor_tensor(ti, src_psum_or_sb, tmp, ALUv.is_gt),
         reads=[srcr, tmpr], writes=[tir], tag=tag + "_gt")
    P.op("vector", lambda e: e.scalar_tensor_tensor(tmp, src_psum_or_sb, -256.0, ti, ALUv.add, ALUv.subtract),
         reads=[srcr, tir], writes=[tmpr], tag=tag + "_fl")
    P.op("vector", lambda e: e.tensor_scalar(dst, tmp, -128.0, 127.0, ALUv.max, ALUv.min),
         reads=[tmpr], writes=[dstr], tag=tag + "_cl")


def emit_l0_ffn(C: Core, W):
    P = C.P
    A = C.arena
    nc = P.nc
    I = 2816
    NI = 22
    emit_layernorm(C, fm=False)

    interd = nc.dram_tensor("l0f_inter", [NI, 128, S], F32).ap()
    interdr = Region("l0f_interd")
    interd2r = [Region("l0f_interdA"), Region("l0f_interdB")]
    rhd = nc.dram_tensor("l0f_rh", [NI, 128, S], BF16).ap()
    rhdr = Region("l0f_rhd")
    ccin = nc.dram_tensor("l0f_ccin", [1, 1], F32).ap()
    ccout = nc.dram_tensor("l0f_ccout", [1, 1], F32).ap()
    ccr = Region("l0f_cc")

    rx, rxr = A.take_tiles(BF16, S, NC, "l0f_rx")
    rxtm, rxtmr = A.take(BF16, 1024, "l0f_rxtm")
    q32, q32r = A.take(F32, 1024, "l0f_q32")
    qi, qir = A.take(I32, 1024, "l0f_qi")
    qb, qbr = A.take(BF16, 1024, "l0f_qb")
    ln32, ln32r = A.take(F32, 1024, "l0f_ln32")
    mx, mxr = A.take(F32, 16, "l0f_mx")
    wsta2 = [A.take(F32, 1024, f"l0f_wsta{d}") for d in range(2)]
    wgq2 = [A.take(BF16, 1024, f"l0f_wgq{d}") for d in range(2)]
    wscl, wsclr = A.take(F32, 2 * NI, "l0f_wscl")
    wgT2 = [A.take(BF16, 8 * 128, f"l0f_wgT{d}") for d in range(2)]
    wmx2 = [A.take(F32, 4, f"l0f_wmx{d}") for d in range(2)]
    hmx, hmxr = A.take(F32, 4, "l0f_hmx")
    gsb2_ = [A.take(F32, 512, f"l0f_gsb{d}") for d in range(2)]
    itmp2 = [A.take(F32, 512, f"l0f_itmp{d}") for d in range(2)]
    wdT, wdTr = A.take_tiles(BF16, 1024, NI, "l0f_wdT")
    res, resr = gsb2_[0]
    sc, scr = A.take(F32, 8, "l0f_sc")

    P.op("vector", lambda e: e.memset(mx[:, 0:1], 0.0), writes=[mxr], tag="l0f_mx0")
    for i in range(NT):
        emit_ln32_tile(C, i, ln32[:, :], ln32r)
        P.op("vector", lambda e: e.tensor_reduce(mx[:, 1:2], ln32[:, :], axis=AX.X,
                                                 op=ALU.max, apply_absolute_value=True),
             reads=[ln32r], writes=[mxr], tag=f"l0f_mxr{i}")
        P.op("vector", lambda e: e.tensor_tensor(mx[:, 0:1], mx[:, 0:1], mx[:, 1:2], ALU.max),
             reads=[mxr], writes=[mxr], tag=f"l0f_mxa{i}")
    mrowd = nc.dram_tensor("l0f_mrow", [1, 128], F32).ap()
    mrowdr = Region("l0f_mrow")
    P.dma("gpsimd", mrowd, mx[:, 0:1], reads=[mxr], writes=[mrowdr], tag="l0f_mst")
    P.dma("gpsimd", q32[0:1, 0:128], mrowd, reads=[mrowdr], writes=[q32r], tag="l0f_mld")
    P.op("vector", lambda e: e.reduce_max(mx[0:1, 2:3], q32[0:1, 0:128], axis=AX.X),
         reads=[q32r], writes=[mxr], tag="l0f_gmx")
    P.dma("gpsimd", ccin[0:1, 0:1], mx[0:1, 2:3], reads=[mxr], writes=[ccr], tag="l0f_ccs")
    P.op("gpsimd", lambda e: e.collective_compute(
        "AllReduce", ALU.max, replica_groups=[CC_GROUP],
        ins=[ccin.opt()], outs=[ccout.opt()]),
        reads=[ccr], writes=[ccr], sem="coll", inc=1, tag="l0f_cc")
    P.dma("gpsimd", mx[0:1, 3:4], ccout[0:1, 0:1], reads=[ccr], writes=[mxr], tag="l0f_ccl")
    P.dma("sync", mx[0:1, 4:5], W["L0_ffn_clip_x"][None], writes=[mxr], tag="l0f_clipx")
    P.op("scalar", lambda e: e.activation(mx[0:1, 5:6], mx[0:1, 4:5], AF.Sigmoid),
         reads=[mxr], writes=[mxr], tag="l0f_sig")
    P.op("vector", lambda e: e.tensor_tensor(mx[0:1, 6:7], mx[0:1, 5:6], mx[0:1, 3:4], ALU.mult),
         reads=[mxr], writes=[mxr], tag="l0f_sm")
    inv127 = P.const_col(1.0 / 127.0)
    P.op("vector", lambda e: e.tensor_scalar(mx[0:1, 6:7], mx[0:1, 6:7], inv127[0:1, :], None, ALU.mult),
         reads=[mxr, P.consts_region], writes=[mxr], tag="l0f_s127")
    emit_recip_nr(P, mx[0:1, 7:8], mx[0:1, 6:7], mx[0:1, 9:10], mxr, "l0f_rs")
    sxd = nc.dram_tensor("l0f_sx", [1, 2], F32).ap()
    sxdr = Region("l0f_sx")
    P.dma("gpsimd", sxd, mx[0:1, 6:8], reads=[mxr], writes=[sxdr], tag="l0f_sxs")
    P.dma("gpsimd", sc[:, 0:1], sxd[0, 1:2][None].broadcast_to([128, 1]), reads=[sxdr],
          writes=[scr], tag="l0f_rsb")
    P.dma("gpsimd", sc[:, 1:2], sxd[0, 0:1][None].broadcast_to([128, 1]), reads=[sxdr],
          writes=[scr], tag="l0f_sxb")

    for i in range(NT):
        emit_ln32_tile(C, i, ln32[:, :], ln32r)
        dtile = rxtm[:, 0:1024]
        dreg = rxtmr
        _emit_round_clip(P, A, C, ln32[:, :], ln32r, dtile, dreg, q32[:, :], q32r,
                         qi[:, :], qir, sc[:, 0:1], scr, True, f"l0f_qx{i}")
        for c in range(NC):
            p_ap = C.psbf[:, (c % 8) * 128:(c % 8) * 128 + 128]
            P.op("tensor", lambda e, dtile=dtile, c=c, p_ap=p_ap: e.transpose(
                p_ap, dtile[:, c * 128:(c + 1) * 128], C.ident),
                reads=[dreg, C.hcbr], writes=[C.psbfr], tag=f"l0f_qt{i}_{c}")
            if c % 2 == 0:
                P.op("vector", lambda e, c=c, i=i, p_ap=p_ap: e.tensor_copy(
                    rx[:, c * S + i * 128: c * S + (i + 1) * 128], p_ap),
                    reads=[C.psbfr], writes=[rxr[c]], tag=f"l0f_qte{i}_{c}")
            else:
                P.op("scalar", lambda e, c=c, i=i, p_ap=p_ap: e.copy(
                    rx[:, c * S + i * 128: c * S + (i + 1) * 128], p_ap),
                    reads=[C.psbfr], writes=[rxr[c]], tag=f"l0f_qte{i}_{c}")

    P.op("vector", lambda e: e.memset(hmx[:, 0:1], 0.0), writes=[hmxr], tag="l0f_hm0")
    for gi, wname in ((0, "L0_ffn_w_gate"), (1, "L0_ffn_w_up")):
        for ichunk in range(NI):
            d = (gi * NI + ichunk) % 2
            wsta, wstar = wsta2[d]
            wgq, wgqr = wgq2[d]
            wgT, wgTr = wgT2[d]
            wmx, wmxr = wmx2[d]
            P.dma("sync" if d == 0 else "scalar",
                  wsta[:, :], W[wname][ichunk * 128:(ichunk + 1) * 128, :],
                  writes=[wstar], tag=f"l0f_wl{gi}_{ichunk}")
            P.op("vector", lambda e, wsta=wsta, wmx=wmx: e.tensor_reduce(
                wmx[:, 0:1], wsta[:, :], axis=AX.X,
                op=ALU.max, apply_absolute_value=True),
                reads=[wstar], writes=[wmxr], tag=f"l0f_wm{gi}_{ichunk}")
            P.op("vector", lambda e, gi=gi, ichunk=ichunk, wmx=wmx: e.tensor_scalar(
                wscl[:, gi * NI + ichunk: gi * NI + ichunk + 1], wmx[:, 0:1], inv127[:, :], None, ALU.mult),
                reads=[wmxr, P.consts_region], writes=[wsclr], tag=f"l0f_ws{gi}_{ichunk}")
            emit_recip_nr(P, wmx[:, 1:2], wmx[:, 0:1], wmx[:, 2:3], wmxr,
                          f"l0f_wr{gi}_{ichunk}")
            P.op("vector", lambda e, wmx=wmx: e.tensor_scalar(wmx[:, 1:2], wmx[:, 1:2], 127.0, None, ALU.mult),
                 reads=[wmxr], writes=[wmxr], tag=f"l0f_wr2{gi}_{ichunk}")
            P.op("vector", lambda e, gi=gi, ichunk=ichunk, wmx=wmx: e.tensor_tensor(
                wmx[:, 3:4], wscl[:, gi * NI + ichunk: gi * NI + ichunk + 1], sc[:, 1:2], ALU.mult),
                reads=[wsclr, scr], writes=[wmxr], tag=f"l0f_se{gi}_{ichunk}")
            _emit_round_clip(P, A, C, wsta[:, :], wstar, wgq[:, 0:1024], wgqr,
                             q32[:, :], q32r, qi[:, :], qir, wmx[:, 1:2], wmxr, False,
                             f"l0f_wq{gi}_{ichunk}")
            for c in range(NC):
                p_ap = C.psbf[:, (c % 8) * 128:(c % 8) * 128 + 128]
                P.op("tensor", lambda e, c=c, p_ap=p_ap, wgq=wgq: e.transpose(
                    p_ap, wgq[:, c * 128:(c + 1) * 128], C.ident),
                    reads=[wgqr, C.hcbr], writes=[C.psbfr], tag=f"l0f_wt{gi}_{ichunk}_{c}")
                P.op("scalar", lambda e, c=c, p_ap=p_ap, wgT=wgT: e.copy(
                    wgT[:, c * 128:(c + 1) * 128], p_ap),
                    reads=[C.psbfr], writes=[wgTr], tag=f"l0f_wte{gi}_{ichunk}_{c}")
            for n0 in range(0, S, 512):
                pb, pbr = C.pb[2 * gi + d], C.pbr[2 * gi + d]
                d2 = (ichunk * 4 + n0 // 512) % 2
                gsb, gsbr = gsb2_[d2]
                itmp, itmpr = itmp2[d2]
                idr = interd2r[(n0 // 512) % 2]

                def gg(e, n0=n0, pb=pb, wgT=wgT):
                    last = None
                    for c in range(NC):
                        last = e.matmul(pb[:, :], wgT[:, c * 128:(c + 1) * 128],
                                        rx[:, c * S + n0: c * S + n0 + 512],
                                        start=(c == 0), stop=(c == NC - 1))
                    return last
                P.op("tensor", gg, reads=[wgTr] + rxr, writes=[pbr], tag=f"l0f_mm{gi}_{ichunk}_{n0}")
                if gi == 0:
                    P.op("scalar", lambda e, pb=pb, gsb=gsb, wmx=wmx: e.activation(
                        gsb[:, :], pb[:, :], AF.Silu, scale=wmx[:, 3:4]),
                        reads=[pbr, wmxr], writes=[gsbr], tag=f"l0f_sil{ichunk}_{n0}")
                    P.dma("gpsimd", interd[ichunk, :, n0:n0 + 512], gsb[:, :],
                          reads=[gsbr], writes=[idr], tag=f"l0f_ist{ichunk}_{n0}")
                else:
                    P.dma("sync", gsb[:, :], interd[ichunk, :, n0:n0 + 512],
                          reads=[idr], writes=[gsbr], tag=f"l0f_ild{ichunk}_{n0}")
                    P.op("scalar", lambda e, pb=pb, itmp=itmp, wmx=wmx: e.activation(
                        itmp[:, :], pb[:, :], AF.Copy, scale=wmx[:, 3:4]),
                        reads=[pbr, wmxr], writes=[itmpr], tag=f"l0f_up{ichunk}_{n0}")
                    P.op("vector", lambda e, itmp=itmp, gsb=gsb: e.tensor_tensor(
                        itmp[:, :], itmp[:, :], gsb[:, :], ALU.mult),
                        reads=[itmpr, gsbr], writes=[itmpr], tag=f"l0f_im{ichunk}_{n0}")
                    P.op("vector", lambda e, itmp=itmp: e.tensor_reduce(
                        hmx[:, 1:2], itmp[:, :], axis=AX.X,
                        op=ALU.max, apply_absolute_value=True),
                        reads=[itmpr], writes=[hmxr], tag=f"l0f_hm{ichunk}_{n0}")
                    P.op("vector", lambda e: e.tensor_tensor(hmx[:, 0:1], hmx[:, 0:1], hmx[:, 1:2], ALU.max),
                         reads=[hmxr], writes=[hmxr], tag=f"l0f_hma{ichunk}_{n0}")
                    P.dma("gpsimd", interd[ichunk, :, n0:n0 + 512], itmp[:, :],
                          reads=[itmpr], writes=[idr], tag=f"l0f_ist2{ichunk}_{n0}")

    P.dma("gpsimd", mrowd, hmx[:, 0:1], reads=[hmxr], writes=[mrowdr], tag="l0f_hst")
    P.dma("gpsimd", q32[0:1, 0:128], mrowd, reads=[mrowdr], writes=[q32r], tag="l0f_hld")
    P.op("vector", lambda e: e.reduce_max(mx[0:1, 13:14], q32[0:1, 0:128], axis=AX.X),
         reads=[q32r], writes=[mxr], tag="l0f_ghm")
    P.dma("gpsimd", ccin[0:1, 0:1], mx[0:1, 13:14], reads=[mxr, ccr], writes=[ccr], tag="l0f_ccs2")
    P.op("gpsimd", lambda e: e.collective_compute(
        "AllReduce", ALU.max, replica_groups=[CC_GROUP],
        ins=[ccin.opt()], outs=[ccout.opt()]),
        reads=[ccr], writes=[ccr], sem="coll", inc=1, tag="l0f_cc2")
    P.dma("gpsimd", mx[0:1, 14:15], ccout[0:1, 0:1], reads=[ccr], writes=[mxr], tag="l0f_ccl2")
    P.dma("sync", mx[0:1, 4:5], W["L0_ffn_clip_down"][None], writes=[mxr], tag="l0f_clipd")
    P.op("scalar", lambda e: e.activation(mx[0:1, 5:6], mx[0:1, 4:5], AF.Sigmoid),
         reads=[mxr], writes=[mxr], tag="l0f_sig2")
    P.op("vector", lambda e: e.tensor_tensor(mx[0:1, 6:7], mx[0:1, 5:6], mx[0:1, 14:15], ALU.mult),
         reads=[mxr], writes=[mxr], tag="l0f_sm2")
    P.op("vector", lambda e: e.tensor_scalar(mx[0:1, 6:7], mx[0:1, 6:7], inv127[0:1, :], None, ALU.mult),
         reads=[mxr, P.consts_region], writes=[mxr], tag="l0f_s1272")
    emit_recip_nr(P, mx[0:1, 7:8], mx[0:1, 6:7], mx[0:1, 9:10], mxr, "l0f_rs2")
    P.dma("gpsimd", sxd, mx[0:1, 6:8], reads=[mxr], writes=[sxdr], tag="l0f_sxs2")
    P.dma("gpsimd", sc[:, 2:3], sxd[0, 1:2][None].broadcast_to([128, 1]), reads=[sxdr],
          writes=[scr], tag="l0f_rsb2")
    P.dma("gpsimd", sc[:, 3:4], sxd[0, 0:1][None].broadcast_to([128, 1]), reads=[sxdr],
          writes=[scr], tag="l0f_sxb2")

    for ichunk in range(NI):
        for half in range(2):
            cols = slice(half * 1024, half * 1024 + 1024)
            P.dma("sync", q32[:, :] if False else ln32[:, :], interd[ichunk, :, cols],
                  reads=[interd2r[0], interd2r[1]], writes=[ln32r], tag=f"l0f_il{ichunk}_{half}")
            _emit_round_clip(P, A, C, ln32[:, :], ln32r, qb[:, :], qbr, q32[:, :], q32r,
                             qi[:, :], qir, sc[:, 2:3], scr, True, f"l0f_qh{ichunk}_{half}")
            P.dma("gpsimd", rhd[ichunk, :, cols], qb[:, :], reads=[qbr], writes=[rhdr],
                  tag=f"l0f_rhs{ichunk}_{half}")
    wds, wdsr = A.take(F32, 2816, "l0f_wds")
    for hc in range(NC):
        P.dma("sync", wds[:, :], W["L0_ffn_w_down"][hc * 128:(hc + 1) * 128, :],
              writes=[wdsr], tag=f"l0f_dl{hc}")
        P.op("vector", lambda e: e.tensor_reduce(mx[:, 9:10], wds[:, :], axis=AX.X,
                                                 op=ALU.max, apply_absolute_value=True),
             reads=[wdsr], writes=[mxr], tag=f"l0f_dm{hc}")
        P.op("vector", lambda e: e.tensor_scalar(mx[:, 15:16], mx[:, 9:10], inv127[:, :], None, ALU.mult),
             reads=[mxr, P.consts_region], writes=[mxr], tag=f"l0f_dsc{hc}")
        emit_recip_nr(P, mx[:, 10:11], mx[:, 9:10], mx[:, 12:13], mxr, f"l0f_dr{hc}")
        P.op("vector", lambda e: e.tensor_scalar(mx[:, 10:11], mx[:, 10:11], 127.0, None, ALU.mult),
             reads=[mxr], writes=[mxr], tag=f"l0f_dr2{hc}")
        for half in range(3):
            cols = slice(half * 1024, min((half + 1) * 1024, 2816))
            w = cols.stop - cols.start
            _emit_round_clip(P, A, C, wds[:, cols], wdsr, qb[:, 0:w], qbr, q32[:, 0:w], q32r,
                             qi[:, 0:w], qir, mx[:, 10:11], mxr, False, f"l0f_dq{hc}_{half}")
            P.op("vector", lambda e, w=w: e.tensor_scalar(qb[:, 0:w], qb[:, 0:w], mx[:, 15:16], None, ALU.mult),
                 reads=[qbr, mxr], writes=[qbr], tag=f"l0f_dv{hc}_{half}")
            nb = (w + 127) // 128
            for c in range(nb):
                ic = half * 8 + c
                if ic >= NI:
                    break
                p_ap = C.psbf[:, (c % 8) * 128:(c % 8) * 128 + 128]
                P.op("tensor", lambda e, c=c, p_ap=p_ap: e.transpose(
                    p_ap, qb[:, c * 128:(c + 1) * 128], C.ident),
                    reads=[qbr, C.hcbr], writes=[C.psbfr], tag=f"l0f_dt{hc}_{half}_{c}")
                P.op("scalar", lambda e, ic=ic, hc=hc, p_ap=p_ap: e.copy(
                    wdT[:, ic * 1024 + hc * 128: ic * 1024 + (hc + 1) * 128], p_ap),
                    reads=[C.psbfr], writes=[wdTr[ic]], tag=f"l0f_dte{hc}_{half}_{c}")

    rhs2 = [A.take(BF16, 256, f"l0f_rhl{d}") for d in range(2)]
    for tg in range(8):
        for ichunk in range(NI):
            rhs_, rhs_r = rhs2[ichunk % 2]
            P.dma("sync" if ichunk % 2 == 0 else "scalar",
                  rhs_[:, 0:256], rhd[ichunk, :, tg * 256:(tg + 1) * 256],
                  reads=[rhdr], writes=[rhs_r], tag=f"l0f_rl{tg}_{ichunk}")
            for tt in range(2):
                for hh in range(2):
                    bank = tt * 2 + hh
                    pb, pbr = C.pb[2 + bank], C.pbr[2 + bank]
                    P.op("tensor", lambda e, ichunk=ichunk, tt=tt, hh=hh, pb=pb, rhs_=rhs_: e.matmul(
                        pb[:, :], rhs_[:, tt * 128:(tt + 1) * 128],
                        wdT[:, ichunk * 1024 + hh * 512: ichunk * 1024 + (hh + 1) * 512],
                        start=(ichunk == 0), stop=(ichunk == NI - 1)),
                        reads=[rhs_r, wdTr[ichunk]], writes=[pbr], tag=f"l0f_dn{tg}_{ichunk}_{tt}_{hh}")
        for tt in range(2):
            for hh in range(2):
                bank = tt * 2 + hh
                pb, pbr = C.pb[2 + bank], C.pbr[2 + bank]
                xtile = (tg * 256 + tt * 128) // 128
                x_ap = C.X[:, xtile * 1024 + hh * 512: xtile * 1024 + hh * 512 + 512]
                P.op("scalar", lambda e, pb=pb: e.activation(
                    res[:, :], pb[:, :], AF.Copy, scale=sc[:, 3:4]),
                    reads=[pbr, scr], writes=[resr], tag=f"l0f_ev{tg}_{tt}_{hh}")
                P.op("vector", lambda e, x_ap=x_ap: e.tensor_tensor(x_ap, x_ap, res[:, :], ALU.add),
                     reads=[resr, C.Xr[xtile]], writes=[C.Xr[xtile]], tag=f"l0f_x{tg}_{tt}_{hh}")
    C.arena.reset()



def emit_l2_attn(C: Core, W):
    P = C.P
    A = C.arena
    nc = P.nc
    HEADS, STATE, INNER, HD = 8, 64, 2048, 256
    NO = 16
    m0 = A.mark()
    P.op("vector", lambda e: e.memset(C.stat[:, 16:48], 0.0), writes=[C.statr], tag="l2a_gz")
    emit_layernorm(C)

    xind = nc.dram_tensor("l2a_xin", [NO, 128, S], BF16).ap()
    xindr = Region("l2a_xind")
    zd = nc.dram_tensor("l2a_z", [NO, 128, S], BF16).ap()
    zdr = Region("l2a_zd")
    yhd = nc.dram_tensor("l2a_y", [NO, 128, S], F32).ap()
    yhdr = Region("l2a_yhd")
    misc = nc.dram_tensor("l2a_misc", [4, 128], F32).ap()
    miscr = Region("l2a_misc")

    wst, wstr = A.take(F32, 2048, "l2a_wst")
    wbfst, wbfstr = A.take(BF16, 2048, "l2a_wbfst")
    wT, wTr = A.take(BF16, 8 * 128, "l2a_wT")
    xpc, xpcr = A.take(F32, S, "l2a_xpc")
    cvacc, cvaccr = A.take(F32, S, "l2a_cva")
    xinc, xincr = A.take(BF16, S, "l2a_xinc")
    convw, convwr = A.take(F32, 4 * NO, "l2a_cw")
    convb, convbr = A.take(F32, NO, "l2a_cb")

    P.dma("sync", convw.rearrange("p (c k) -> p c k", k=4),
          W["L2_attn_conv_w"].rearrange("(c p) one k -> p (c one) k", p=128),
          writes=[convwr], tag="l2a_cwl")
    P.dma("sync", wst[0:NO, 0:128], W["L2_attn_conv_b"].rearrange("(c p) -> c p", p=128),
          writes=[wstr], tag="l2a_cbl")
    idf0, idf0r = A.take(F32, 128, "l2a_idf0")
    P.op("vector", lambda e: e.tensor_copy(idf0[:, :], C.ident),
         reads=[C.hcbr], writes=[idf0r], tag="l2a_idf0")
    pb0, pb0r = C.pb[0], C.pbr[0]
    P.op("tensor", lambda e: e.transpose(pb0[:, 0:NO], wst[0:NO, 0:128], idf0[0:NO, 0:NO]),
         reads=[wstr, idf0r], writes=[pb0r], tag="l2a_cbt")
    P.op("vector", lambda e: e.tensor_copy(convb[:, :], pb0[:, 0:NO]),
         reads=[pb0r], writes=[convbr], tag="l2a_cbe")

    for mat, isz in (("Wz", True), ("Wx", False)):
        for oc in range(NO):
            P.dma("sync", wst[:, 0:1024], W[f"L2_attn_{mat}"][oc * 128:(oc + 1) * 128, :],
                  writes=[wstr], tag=f"l2a_{mat}l{oc}")
            P.op("vector", lambda e: e.tensor_copy(wbfst[:, 0:1024], wst[:, 0:1024]),
                 reads=[wstr], writes=[wbfstr], tag=f"l2a_{mat}c{oc}")
            for c in range(NC):
                p_ap = C.psbf[:, (c % 8) * 128:(c % 8) * 128 + 128]
                P.op("tensor", lambda e, c=c, p_ap=p_ap: e.transpose(
                    p_ap, wbfst[:, c * 128:(c + 1) * 128], C.ident),
                    reads=[wbfstr, C.hcbr], writes=[C.psbfr], tag=f"l2a_{mat}t{oc}_{c}")
                P.op("scalar", lambda e, c=c, p_ap=p_ap: e.copy(wT[:, c * 128:(c + 1) * 128], p_ap),
                     reads=[C.psbfr], writes=[wTr], tag=f"l2a_{mat}te{oc}_{c}")
            for n0 in range(0, S, 512):
                pb, pbr = C.pb[1], C.pbr[1]

                def gp(e, n0=n0, pb=pb):
                    last = None
                    for c in range(NC):
                        last = e.matmul(pb[:, :], wT[:, c * 128:(c + 1) * 128],
                                        C.ln_fm[:, c * S + n0: c * S + n0 + 512],
                                        start=(c == 0), stop=(c == NC - 1))
                    return last
                P.op("tensor", gp, reads=[wTr] + C.ln_fmr, writes=[pbr], tag=f"l2a_{mat}m{oc}_{n0}")
                if isz:
                    P.op("scalar", lambda e, pb=pb, n0=n0: e.activation(
                        xinc[:, n0:n0 + 512], pb[:, :], AF.Silu),
                        reads=[pbr], writes=[xincr], tag=f"l2a_ze{oc}_{n0}")
                else:
                    P.op("scalar", lambda e, pb=pb, n0=n0: e.copy(xpc[:, n0:n0 + 512], pb[:, :]),
                         reads=[pbr], writes=[xpcr], tag=f"l2a_xe{oc}_{n0}")
            if isz:
                P.dma("gpsimd", zd[oc], xinc[:, :], reads=[xincr], writes=[zdr], tag=f"l2a_zs{oc}")
            else:
                P.op("vector", lambda e, oc=oc: e.tensor_scalar(
                    cvacc[:, :], xpc[:, :], convw[:, oc * 4 + 3:oc * 4 + 4], None, ALU.mult),
                    reads=[xpcr, convwr], writes=[cvaccr], tag=f"l2a_cv3_{oc}")
                for k in range(1, 4):
                    P.op("vector", lambda e, k=k, oc=oc: e.scalar_tensor_tensor(
                        cvacc[:, k:S], xpc[:, 0:S - k], convw[:, oc * 4 + 3 - k:oc * 4 + 4 - k],
                        cvacc[:, k:S], ALU.mult, ALU.add),
                        reads=[xpcr, convwr, cvaccr], writes=[cvaccr], tag=f"l2a_cv{k}_{oc}")
                P.op("vector", lambda e, oc=oc: e.tensor_scalar(
                    cvacc[:, :], cvacc[:, :], convb[:, oc:oc + 1], None, ALU.add),
                    reads=[cvaccr, convbr], writes=[cvaccr], tag=f"l2a_cvb{oc}")
                P.op("scalar", lambda e: e.activation(xinc[:, :], cvacc[:, :], AF.Silu),
                     reads=[cvaccr], writes=[xincr], tag=f"l2a_cvs{oc}")
                P.dma("gpsimd", xind[oc], xinc[:, :], reads=[xincr], writes=[xindr], tag=f"l2a_xs{oc}")

    A.rewind(m0)
    xin, xinr = A.take_tiles(BF16, S, NO, "l2a_xr")
    for oc in range(NO):
        P.dma("sync", xin[:, oc * S:(oc + 1) * S], xind[oc], reads=[xindr],
              writes=[xinr[oc]], tag=f"l2a_xrl{oc}")
    bmat, bmatr = A.take_tiles(BF16, S, 4, "l2a_b")
    cmat, cmatr = A.take_tiles(BF16, S, 4, "l2a_c")
    dtt, dttr = A.take(F32, 8 * NT, "l2a_dt")
    dta32, dta32r = A.take(F32, 8, "l2a_dta")
    cA, cAr = A.take(F32, 8 * NT, "l2a_cA")
    abc, abcr = A.take(F32, 8, "l2a_abc")
    bdtb, bdtbr = A.take(F32, 8, "l2a_bdtb")
    dpar, dparr = A.take(F32, 8, "l2a_dp")
    carow, carowr = A.take(F32, 144, "l2a_car")
    ones32, ones32r = A.take(F32, 128, "l2a_o32")
    trif, trifr = A.take(F32, 128, "l2a_trif")
    idf, idfr = A.take(F32, 128, "l2a_idf")
    P.op("vector", lambda e: e.tensor_copy(idf[:, :], C.ident),
         reads=[C.hcbr], writes=[idfr], tag="l2a_idf")
    st32, st32r = A.take(F32, 8 * 256, "l2a_st32")
    stbf, stbfr = A.take(BF16, 8 * 256, "l2a_stbf")
    m1 = A.mark()
    wst2, wst2r = A.take(F32, 1024, "l2a_wst2")
    wbfst2, wbfst2r = A.take(BF16, 1024, "l2a_wbfst2")
    wbT, wbTr = A.take_tiles(BF16, 512, NO, "l2a_wbT")
    wdtT, wdtTr = A.take(BF16, 8 * NO, "l2a_wdtT")

    P.op("vector", lambda e: e.memset(ones32[0:1, :], 1.0), writes=[ones32r], tag="l2a_o32")
    P.op("vector", lambda e: e.tensor_copy(trif[:, :], C.hc("l0m0", 512)[:, 0:128]),
         reads=[C.hcbr], writes=[trifr], tag="l2a_trif")
    P.dma("sync", carow[0:1, 128:136], W["L2_attn_log_a"][None], writes=[carowr], tag="l2a_lal")
    P.op("scalar", lambda e: e.activation(carow[0:1, 128:136], carow[0:1, 128:136], AF.Exp),
         reads=[carowr], writes=[carowr], tag="l2a_la1")
    P.op("vector", lambda e: e.tensor_scalar_mul(carow[0:1, 128:136], carow[0:1, 128:136], -1.0),
         reads=[carowr], writes=[carowr], tag="l2a_la2")
    P.dma("gpsimd", misc[0, 0:8][None], carow[0:1, 128:136], reads=[carowr], writes=[miscr], tag="l2a_las")
    P.dma("gpsimd", abc[:, 0:8], misc[0, 0:8][None].broadcast_to([128, 8]), reads=[miscr],
          writes=[abcr], tag="l2a_lab")
    P.dma("gpsimd", bdtb[:, 0:8], W["L2_attn_b_dt"][None].broadcast_to([128, 8]),
          writes=[bdtbr], tag="l2a_bdl")
    P.dma("gpsimd", dpar[:, 0:8], W["L2_attn_d_param"][None].broadcast_to([128, 8]),
          writes=[dparr], tag="l2a_dpl")

    for mat, dstt, dstr_ in (("Wb", bmat, bmatr), ("Wc", cmat, cmatr)):
        for rc in range(4):
          for hf in range(2):
            P.dma("sync", wst2[:, :], W[f"L2_attn_{mat}"][rc * 128:(rc + 1) * 128, hf * 1024:(hf + 1) * 1024],
                  writes=[wst2r], tag=f"l2a_{mat}l{rc}_{hf}")
            P.op("vector", lambda e: e.tensor_copy(wbfst2[:, :], wst2[:, :]),
                 reads=[wst2r], writes=[wbfst2r], tag=f"l2a_{mat}c{rc}_{hf}")
            for oc8 in range(8):
                oc = hf * 8 + oc8
                p_ap = C.psbf[:, (oc % 8) * 128:(oc % 8) * 128 + 128]
                P.op("tensor", lambda e, oc8=oc8, p_ap=p_ap: e.transpose(
                    p_ap, wbfst2[:, oc8 * 128:(oc8 + 1) * 128], C.ident),
                    reads=[wbfst2r, C.hcbr], writes=[C.psbfr], tag=f"l2a_{mat}t{rc}_{oc}")
                P.op("scalar", lambda e, oc=oc, rc=rc, p_ap=p_ap: e.copy(
                    wbT[:, oc * 512 + rc * 128: oc * 512 + (rc + 1) * 128], p_ap),
                    reads=[C.psbfr], writes=[wbTr[oc]], tag=f"l2a_{mat}te{rc}_{oc}")
        for mc in range(4):
            for n0 in range(0, S, 512):
                pb, pbr = C.pb[2 + (mc % 2)], C.pbr[2 + (mc % 2)]

                def gb(e, n0=n0, mc=mc, pb=pb):
                    last = None
                    for oc in range(NO):
                        last = e.matmul(pb[:, :], wbT[:, oc * 512 + mc * 128: oc * 512 + (mc + 1) * 128],
                                        xin[:, oc * S + n0: oc * S + n0 + 512],
                                        start=(oc == 0), stop=(oc == NO - 1))
                    return last
                P.op("tensor", gb, reads=wbTr + xinr, writes=[pbr], tag=f"l2a_{mat}m{mc}_{n0}")
                P.op("scalar", lambda e, pb=pb, mc=mc, n0=n0, dstt=dstt: e.copy(
                    dstt[:, mc * S + n0: mc * S + n0 + 512], pb[:, :]),
                    reads=[pbr], writes=[dstr_[mc]], tag=f"l2a_{mat}e{mc}_{n0}")
    for hf in range(2):
        P.dma("sync", wst2[0:8, 0:1024], W["L2_attn_Wdt"][:, hf * 1024:(hf + 1) * 1024],
              writes=[wst2r], tag=f"l2a_dtl{hf}")
        P.op("vector", lambda e: e.tensor_copy(wbfst2[0:8, 0:1024], wst2[0:8, 0:1024]),
             reads=[wst2r], writes=[wbfst2r], tag=f"l2a_dtc{hf}")
        for oc8 in range(8):
            oc = hf * 8 + oc8
            p_ap = C.psbf[:, 0:8]
            P.op("tensor", lambda e, oc8=oc8, p_ap=p_ap: e.transpose(
                p_ap, wbfst2[0:8, oc8 * 128:(oc8 + 1) * 128], C.ident[0:8, 0:8]),
                reads=[wbfst2r, C.hcbr], writes=[C.psbfr], tag=f"l2a_dtt{oc}")
            P.op("vector", lambda e, oc=oc, p_ap=p_ap: e.tensor_copy(wdtT[:, oc * 8:(oc + 1) * 8], p_ap),
                 reads=[C.psbfr], writes=[wdtTr], tag=f"l2a_dtte{oc}")
    for i in range(NT):
        pb, pbr = C.pb[4], C.pbr[4]

        def gdt(e, i=i, pb=pb):
            last = None
            for oc in range(NO):
                last = e.matmul(pb[:, 0:8], xin[:, oc * S + i * 128: oc * S + (i + 1) * 128],
                                wdtT[:, oc * 8:(oc + 1) * 8], start=(oc == 0), stop=(oc == NO - 1))
            return last
        P.op("tensor", gdt, reads=[wdtTr] + xinr, writes=[pbr], tag=f"l2a_dtm{i}")
        dcol = lambda i: dtt[:, i * 8:(i + 1) * 8]
        P.op("vector", lambda e, i=i, pb=pb: e.tensor_tensor(dcol(i), pb[:, 0:8], bdtb[:, 0:8], ALU.add),
             reads=[pbr, bdtbr], writes=[dttr], tag=f"l2a_dtb{i}")
        P.op("vector", lambda e, i=i: e.tensor_scalar_mul(dta32[:, 0:8], dcol(i), -1.0),
             reads=[dttr], writes=[dta32r], tag=f"l2a_sp1{i}")
        P.op("vector", lambda e, i=i: e.tensor_tensor(dta32[:, 0:8], dta32[:, 0:8], dcol(i), ALU.min),
             reads=[dta32r, dttr], writes=[dta32r], tag=f"l2a_sp1b{i}")
        cp1 = P.const_col(1.0)
        P.op("scalar", lambda e: e.activation(dta32[:, 0:8], dta32[:, 0:8], AF.Exp),
             reads=[dta32r], writes=[dta32r], tag=f"l2a_sp2{i}")
        P.op("scalar", lambda e, cp1=cp1: e.activation(dta32[:, 0:8], dta32[:, 0:8], AF.Ln, bias=cp1),
             reads=[dta32r, P.consts_region], writes=[dta32r], tag=f"l2a_sp3{i}")
        P.op("vector", lambda e, i=i: e.tensor_scalar(dcol(i), dcol(i), 0.0, None, ALU.max),
             reads=[dttr], writes=[dttr], tag=f"l2a_sp4{i}")
        P.op("vector", lambda e, i=i: e.tensor_tensor(dcol(i), dcol(i), dta32[:, 0:8], ALU.add),
             reads=[dttr, dta32r], writes=[dttr], tag=f"l2a_sp5{i}")
        P.op("vector", lambda e, i=i: e.tensor_tensor(dta32[:, 0:8], dcol(i), abc[:, 0:8], ALU.mult),
             reads=[dttr, abcr], writes=[dta32r], tag=f"l2a_da{i}")
        pb2, pb2r = C.pb[5], C.pbr[5]
        P.op("tensor", lambda e, pb2=pb2: e.matmul(pb2[:, 0:8], trif[:, :], dta32[:, 0:8],
                                                   start=True, stop=True),
             reads=[trifr, dta32r], writes=[pb2r], tag=f"l2a_cum{i}")
        P.op("scalar", lambda e, i=i, pb2=pb2: e.copy(cA[:, i * 8:(i + 1) * 8], pb2[:, 0:8]),
             reads=[pb2r], writes=[cAr], tag=f"l2a_cae{i}")

    A.rewind(m1)
    xtm, xtmr = A.take(BF16, S, "l2a_xtm")
    LT32, LT32r = A.take(F32, 128, "l2a_LT32")
    LTe, LTer = A.take(F32, 128, "l2a_LTe")
    LTb, LTbr = A.take(BF16, 128, "l2a_LTb")
    Mbf, Mbfr = A.take(BF16, 128, "l2a_M")
    bsc, bscr = A.take(BF16, 128, "l2a_bsc")
    btm, btmr = A.take(BF16, 64, "l2a_btm")
    ecol, ecolr = A.take(F32, 8, "l2a_ec")
    scr1, scr1r = A.take(F32, 256, "l2a_s1")
    scr2, scr2r = A.take(F32, 256, "l2a_s2")
    ytp, ytpr = A.take(F32, 256, "l2a_ytp")
    P.op("vector", lambda e: e.memset(st32[0:64, :], 0.0), writes=[st32r], tag="l2a_st0")
    P.op("vector", lambda e: e.memset(stbf[:, :], 0.0), writes=[stbfr], tag="l2a_stb0")

    masklow = C.hc("l2neg", 128)
    for cc in range(NT):
        for oc in range(NO):
            p_ap = C.psbf[:, (oc % 8) * 128:(oc % 8) * 128 + 128]
            P.op("tensor", lambda e, oc=oc, cc=cc, p_ap=p_ap: e.transpose(
                p_ap, xin[:, oc * S + cc * 128: oc * S + (cc + 1) * 128], C.ident),
                reads=[xinr[oc], C.hcbr], writes=[C.psbfr], tag=f"l2a_xt{cc}_{oc}")
            P.op("vector", lambda e, oc=oc, p_ap=p_ap: e.tensor_copy(
                xtm[:, oc * 128:(oc + 1) * 128], p_ap),
                reads=[C.psbfr], writes=[xtmr], tag=f"l2a_xte{cc}_{oc}")
        for h in range(HEADS):
            hb = (h % 2) * 64
            hrow = slice(hb, hb + 64)
            hchunk = h // 2
            cacol = cA[:, cc * 8 + h: cc * 8 + h + 1]
            dtcol = dtt[:, cc * 8 + h: cc * 8 + h + 1]
            pb, pbr = C.pb[0], C.pbr[0]
            P.op("tensor", lambda e, cacol=cacol, pb=pb: e.transpose(pb[0:1, 0:128], cacol, idf[:, :]),
                 reads=[cAr, idfr], writes=[pbr], tag=f"l2a_rt{cc}_{h}")
            P.op("vector", lambda e, pb=pb: e.tensor_copy(carow[0:1, 0:128], pb[0:1, 0:128]),
                 reads=[pbr], writes=[carowr], tag=f"l2a_re{cc}_{h}")
            P.op("tensor", lambda e, pb=pb: e.matmul(pb[:, 0:128], ones32[0:1, :], carow[0:1, 0:128],
                                                     start=True, stop=True),
                 reads=[ones32r, carowr], writes=[pbr], tag=f"l2a_R{cc}_{h}")
            P.op("vector", lambda e, cacol=cacol, pb=pb: e.tensor_scalar(
                LT32[:, :], pb[:, 0:128], cacol, None, ALU.subtract),
                reads=[pbr, cAr], writes=[LT32r], tag=f"l2a_D{cc}_{h}")
            P.op("vector", lambda e: e.tensor_tensor(LT32[:, :], LT32[:, :], masklow, ALU.add),
                 reads=[LT32r, C.hcbr], writes=[LT32r], tag=f"l2a_Dm{cc}_{h}")
            P.op("scalar", lambda e: e.activation(LTe[:, :], LT32[:, :], AF.Exp),
                 reads=[LT32r], writes=[LTer], tag=f"l2a_De{cc}_{h}")
            P.op("vector", lambda e, dtcol=dtcol: e.tensor_scalar(LTb[:, :], LTe[:, :], dtcol, None, ALU.mult),
                 reads=[LTer, dttr], writes=[LTbr], tag=f"l2a_Ds{cc}_{h}")
            pg, pgr = C.pb[1], C.pbr[1]
            P.op("tensor", lambda e, hchunk=hchunk, hrow=hrow, cc=cc, pg=pg: e.matmul(
                pg[:, 0:128], bmat[hrow, hchunk * S + cc * 128: hchunk * S + (cc + 1) * 128],
                cmat[hrow, hchunk * S + cc * 128: hchunk * S + (cc + 1) * 128],
                start=True, stop=True),
                reads=[bmatr[hchunk], cmatr[hchunk]], writes=[pgr], tag=f"l2a_G{cc}_{h}")
            P.op("vector", lambda e, pg=pg: e.tensor_tensor(Mbf[:, :], pg[:, 0:128], LTb[:, :], ALU.mult),
                 reads=[pgr, LTbr], writes=[Mbfr], tag=f"l2a_M{cc}_{h}")
            P.op("scalar", lambda e, cacol=cacol: e.activation(ecol[:, 0:1], cacol, AF.Exp),
                 reads=[cAr], writes=[ecolr], tag=f"l2a_et{cc}_{h}")
            pb2, pb2r = C.pb[0], C.pbr[0]
            P.op("tensor", lambda e, pb2=pb2: e.matmul(pb2[:, 0:1], ones32[0:1, :], carow[0:1, 127:128],
                                                       start=True, stop=True),
                 reads=[ones32r, carowr], writes=[pb2r], tag=f"l2a_cl{cc}_{h}")
            P.op("vector", lambda e, pb2=pb2: e.tensor_copy(ecol[:, 1:2], pb2[:, 0:1]),
                 reads=[pb2r], writes=[ecolr], tag=f"l2a_cle{cc}_{h}")
            P.op("vector", lambda e, cacol=cacol: e.tensor_tensor(ecol[:, 2:3], ecol[:, 1:2], cacol, ALU.subtract),
                 reads=[ecolr, cAr], writes=[ecolr], tag=f"l2a_e2a{cc}_{h}")
            P.op("scalar", lambda e: e.activation(ecol[:, 2:3], ecol[:, 2:3], AF.Exp),
                 reads=[ecolr], writes=[ecolr], tag=f"l2a_e2{cc}_{h}")
            P.op("vector", lambda e, dtcol=dtcol: e.tensor_tensor(ecol[:, 3:4], ecol[:, 2:3], dtcol, ALU.mult),
                 reads=[ecolr, dttr], writes=[ecolr], tag=f"l2a_w{cc}_{h}")
            P.op("scalar", lambda e: e.activation(ecol[:, 4:5], ecol[:, 1:2], AF.Exp),
                 reads=[ecolr], writes=[ecolr], tag=f"l2a_lam{cc}_{h}")
            pA, pAr = C.pb[2], C.pbr[2]
            P.op("tensor", lambda e, h=h, pA=pA: e.matmul(
                pA[:, 0:256], Mbf[:, :], xtm[:, h * 256:(h + 1) * 256], start=True, stop=True),
                reads=[Mbfr, xtmr], writes=[pAr], tag=f"l2a_Yi{cc}_{h}")
            pB, pBr = C.pb[3], C.pbr[3]
            P.op("tensor", lambda e, h=h, hrow=hrow, hchunk=hchunk, cc=cc, pB=pB: e.matmul(
                pB[:, 0:256], cmat[hrow, hchunk * S + cc * 128: hchunk * S + (cc + 1) * 128],
                stbf[hrow, h * 256:(h + 1) * 256], start=True, stop=True),
                reads=[cmatr[hchunk], stbfr], writes=[pBr], tag=f"l2a_Ys{cc}_{h}")
            P.op("scalar", lambda e, pB=pB: e.activation(scr1[:, :], pB[:, 0:256], AF.Copy, scale=ecol[:, 0:1]),
                 reads=[pBr, ecolr], writes=[scr1r], tag=f"l2a_Ye{cc}_{h}")
            P.op("vector", lambda e, h=h: e.scalar_tensor_tensor(
                scr2[:, :], xtm[:, h * 256:(h + 1) * 256], dpar[:, h:h + 1], scr1[:, :], ALU.mult, ALU.add),
                reads=[xtmr, dparr, scr1r], writes=[scr2r], tag=f"l2a_Yd{cc}_{h}")
            P.op("vector", lambda e, pA=pA: e.tensor_tensor(ytp[:, :], pA[:, 0:256], scr2[:, :], ALU.add),
                 reads=[pAr, scr2r], writes=[ytpr], tag=f"l2a_Ya{cc}_{h}")
            P.op("vector", lambda e: e.reduce_sum(ecol[:, 5:6], ytp[:, :], axis=AX.X),
                 reads=[ytpr], writes=[ecolr], tag=f"l2a_gs{cc}_{h}")
            P.op("vector", lambda e, h=h: e.tensor_tensor(C.stat[:, 16 + h:17 + h], C.stat[:, 16 + h:17 + h],
                                                          ecol[:, 5:6], ALU.add),
                 reads=[ecolr, C.statr], writes=[C.statr], tag=f"l2a_gsa{cc}_{h}")
            P.op("scalar", lambda e: e.activation(scr1[:, :], ytp[:, :], AF.Square, accum_out=ecol[:, 6:7]),
                 reads=[ytpr], writes=[scr1r, ecolr], tag=f"l2a_gq{cc}_{h}")
            P.op("vector", lambda e, h=h: e.tensor_tensor(C.stat[:, 24 + h:25 + h], C.stat[:, 24 + h:25 + h],
                                                          ecol[:, 6:7], ALU.add),
                 reads=[ecolr, C.statr], writes=[C.statr], tag=f"l2a_gqa{cc}_{h}")
            for db in range(2):
                pT, pTr = C.pb[4], C.pbr[4]
                P.op("tensor", lambda e, db=db, pT=pT: e.transpose(
                    pT[:, 0:128], ytp[:, db * 128:(db + 1) * 128], idf[:, :]),
                    reads=[ytpr, idfr], writes=[pTr], tag=f"l2a_yt{cc}_{h}_{db}")
                P.op("vector", lambda e, pT=pT: e.tensor_copy(scr2[:, 0:128], pT[:, 0:128]),
                     reads=[pTr], writes=[scr2r], tag=f"l2a_yte{cc}_{h}_{db}")
                P.dma("gpsimd", yhd[2 * h + db, :, cc * 128:(cc + 1) * 128], scr2[:, 0:128],
                      reads=[scr2r], writes=[yhdr], tag=f"l2a_yst{cc}_{h}_{db}")
            P.op("vector", lambda e, hrow=hrow, hchunk=hchunk, cc=cc: e.tensor_copy(
                bsc[0:64, :], bmat[hrow, hchunk * S + cc * 128: hchunk * S + (cc + 1) * 128]),
                reads=[bmatr[hchunk]], writes=[bscr], tag=f"l2a_bc{cc}_{h}")
            pT, pTr = C.pb[4], C.pbr[4]
            P.op("tensor", lambda e, pT=pT: e.transpose(pT.bitcast(BF16)[:, 0:128][:, 0:64] if False else C.psbf[:, 0:64], bsc[0:64, :], C.ident[0:64, 0:64]),
                 reads=[bscr, C.hcbr], writes=[C.psbfr], tag=f"l2a_bt{cc}_{h}")
            P.op("vector", lambda e, cc=cc: e.tensor_scalar(btm[:, 0:64], C.psbf[:, 0:64], ecol[:, 3:4], None, ALU.mult),
                 reads=[C.psbfr, ecolr], writes=[btmr], tag=f"l2a_bw{cc}_{h}")
            pS, pSr = C.pb[5], C.pbr[5]
            P.op("tensor", lambda e, h=h, pS=pS: e.matmul(
                pS[0:64, 0:256], btm[:, 0:64], xtm[:, h * 256:(h + 1) * 256], start=True, stop=True),
                reads=[btmr, xtmr], writes=[pSr], tag=f"l2a_su{cc}_{h}")
            P.op("vector", lambda e, h=h: e.tensor_scalar(
                st32[0:64, h * 256:(h + 1) * 256], st32[0:64, h * 256:(h + 1) * 256], ecol[0:64, 4:5], None, ALU.mult),
                reads=[st32r, ecolr], writes=[st32r], tag=f"l2a_sl{cc}_{h}")
            P.op("vector", lambda e, h=h, pS=pS: e.tensor_tensor(
                st32[0:64, h * 256:(h + 1) * 256], st32[0:64, h * 256:(h + 1) * 256], pS[0:64, 0:256], ALU.add),
                reads=[st32r, pSr], writes=[st32r], tag=f"l2a_sa{cc}_{h}")
            P.op("scalar", lambda e, h=h, hb=hb: e.copy(
                stbf[hb:hb + 64, h * 256:(h + 1) * 256], st32[0:64, h * 256:(h + 1) * 256]),
                reads=[st32r], writes=[stbfr], tag=f"l2a_se{cc}_{h}")

    A.rewind(m0)
    outfm, outfmr = A.take_tiles(BF16, S, NO, "l2a_of")
    woT, woTr = A.take_tiles(BF16, 1024, NO, "l2a_woT")
    ych, ychr = A.take(F32, S, "l2a_ych")
    zch, zchr = A.take(BF16, S, "l2a_zch")
    gnp, gnpr = A.take(F32, 2 * NO, "l2a_gnp")
    grow, growr = A.take(F32, 144, "l2a_grow")
    gcol, gcolr = A.take(F32, 8, "l2a_gcol")
    wst3, wst3r = A.take(F32, 2048, "l2a_wst3")
    wbf3, wbf3r = A.take(BF16, 2048, "l2a_wbf3")
    res, resr = A.take(F32, 512, "l2a_res")
    idf2, idf2r = A.take(F32, 128, "l2a_idf2")
    P.op("vector", lambda e: e.tensor_copy(idf2[:, :], C.ident),
         reads=[C.hcbr], writes=[idf2r], tag="l2a_idf2")

    for which, nm in ((0, "gn_w"), (1, "gn_b")):
        P.dma("sync", wst3[0:16, 0:128], W[f"L2_attn_{nm}"].rearrange("(c p) -> c p", p=128),
              writes=[wst3r], tag=f"l2a_{nm}l")
        pb, pbr = C.pb[0], C.pbr[0]
        P.op("tensor", lambda e, pb=pb: e.transpose(pb[:, 0:16], wst3[0:16, 0:128], idf2[0:16, 0:16]),
             reads=[wst3r, idf2r], writes=[pbr], tag=f"l2a_{nm}t")
        P.op("vector", lambda e, which=which, pb=pb: e.tensor_copy(
            gnp[:, which * NO:(which + 1) * NO], pb[:, 0:16]),
            reads=[pbr], writes=[gnpr], tag=f"l2a_{nm}e")
    gnd = nc.dram_tensor("l2a_gnd", [16, 128], F32).ap()
    gndr = Region("l2a_gnd")
    pbg, pbgr = C.pb[1], C.pbr[1]
    P.op("tensor", lambda e: e.transpose(pbg[0:16, 0:128], C.stat[:, 16:32], idf2[:, :]),
         reads=[C.statr, idf2r], writes=[pbgr], tag="l2a_gstT")
    P.op("vector", lambda e: e.tensor_copy(grow[0:16, 0:128], pbg[0:16, 0:128]),
         reads=[pbgr], writes=[growr], tag="l2a_gstE")
    P.dma("gpsimd", gnd, grow[0:16, 0:128], reads=[growr], writes=[gndr], tag="l2a_gst")
    for h in range(HEADS):
        P.dma("sync", grow[0:1, 0:128], gnd[h][None], reads=[gndr], writes=[growr], tag=f"l2a_gl{h}")
        P.op("vector", lambda e: e.reduce_sum(gcol[0:1, 0:1], grow[0:1, 0:128], axis=AX.X),
             reads=[growr], writes=[gcolr], tag=f"l2a_gr{h}")
        P.dma("sync", grow[0:1, 0:128], gnd[8 + h][None], reads=[gndr], writes=[growr], tag=f"l2a_gl2{h}")
        P.op("vector", lambda e: e.reduce_sum(gcol[0:1, 1:2], grow[0:1, 0:128], axis=AX.X),
             reads=[growr], writes=[gcolr], tag=f"l2a_gr2{h}")
        NE = float(S * 256)
        P.op("vector", lambda e: e.tensor_scalar(gcol[0:1, 2:3], gcol[0:1, 0:1], 1.0 / NE, None, ALU.mult),
             reads=[gcolr], writes=[gcolr], tag=f"l2a_gm{h}")
        P.op("vector", lambda e: e.tensor_scalar(gcol[0:1, 3:4], gcol[0:1, 1:2], 1.0 / NE, None, ALU.mult),
             reads=[gcolr], writes=[gcolr], tag=f"l2a_gq{h}")
        P.op("vector", lambda e: e.tensor_tensor(gcol[0:1, 4:5], gcol[0:1, 2:3], gcol[0:1, 2:3], ALU.mult),
             reads=[gcolr], writes=[gcolr], tag=f"l2a_gm2{h}")
        P.op("vector", lambda e: e.tensor_tensor(gcol[0:1, 3:4], gcol[0:1, 3:4], gcol[0:1, 4:5], ALU.subtract),
             reads=[gcolr], writes=[gcolr], tag=f"l2a_gv{h}")
        ceps = P.const_col(1e-5)
        cnh = P.const_col(-0.5)
        P.op("scalar", lambda e, ceps=ceps: e.activation(gcol[0:1, 3:4], gcol[0:1, 3:4], AF.Ln, bias=ceps[0:1, :]),
             reads=[gcolr, P.consts_region], writes=[gcolr], tag=f"l2a_gvl{h}")
        P.op("scalar", lambda e, cnh=cnh: e.activation(gcol[0:1, 3:4], gcol[0:1, 3:4], AF.Exp, scale=cnh[0:1, :]),
             reads=[gcolr, P.consts_region], writes=[gcolr], tag=f"l2a_gvr{h}")
        P.dma("gpsimd", misc[2, 2 * h:2 * h + 2][None], gcol[0:1, 2:4], reads=[gcolr],
              writes=[miscr], tag=f"l2a_gms{h}")
    for oc in range(NO):
        h = oc // 2
        P.dma("sync", ych[:, :], yhd[oc], reads=[yhdr], writes=[ychr], tag=f"l2a_yl{oc}")
        P.dma("gpsimd", gcol[:, 5:6], misc[2, 2 * h:2 * h + 1][None].broadcast_to([128, 1]),
              reads=[miscr], writes=[gcolr], tag=f"l2a_mb{oc}")
        P.dma("gpsimd", gcol[:, 6:7], misc[2, 2 * h + 1:2 * h + 2][None].broadcast_to([128, 1]),
              reads=[miscr], writes=[gcolr], tag=f"l2a_rb{oc}")
        P.op("vector", lambda e, oc=oc: e.tensor_scalar(
            ych[:, :], ych[:, :], gcol[:, 5:6], gcol[:, 6:7], ALU.subtract, ALU.mult),
            reads=[ychr, gcolr], writes=[ychr], tag=f"l2a_gn1{oc}")
        P.op("vector", lambda e, oc=oc: e.tensor_scalar(
            ych[:, :], ych[:, :], gnp[:, oc:oc + 1], gnp[:, NO + oc:NO + oc + 1], ALU.mult, ALU.add),
            reads=[ychr, gnpr], writes=[ychr], tag=f"l2a_gn2{oc}")
        P.dma("sync", zch[:, :], zd[oc], reads=[zdr], writes=[zchr], tag=f"l2a_zl{oc}")
        P.op("vector", lambda e, oc=oc: e.tensor_tensor(
            outfm[:, oc * S:(oc + 1) * S], ych[:, :], zch[:, :], ALU.mult),
            reads=[ychr, zchr], writes=[outfmr[oc]], tag=f"l2a_og{oc}")
    for dc in range(NC):
        P.dma("sync", wst3[:, :], W["L2_attn_Wout"][dc * 128:(dc + 1) * 128, :],
              writes=[wst3r], tag=f"l2a_wol{dc}")
        P.op("vector", lambda e: e.tensor_copy(wbf3[:, :], wst3[:, :]),
             reads=[wst3r], writes=[wbf3r], tag=f"l2a_woc{dc}")
        for oc in range(NO):
            p_ap = C.psbf[:, (oc % 8) * 128:(oc % 8) * 128 + 128]
            P.op("tensor", lambda e, oc=oc, p_ap=p_ap: e.transpose(
                p_ap, wbf3[:, oc * 128:(oc + 1) * 128], C.ident),
                reads=[wbf3r, C.hcbr], writes=[C.psbfr], tag=f"l2a_wot{dc}_{oc}")
            P.op("scalar", lambda e, oc=oc, dc=dc, p_ap=p_ap: e.copy(
                woT[:, oc * 1024 + dc * 128: oc * 1024 + (dc + 1) * 128], p_ap),
                reads=[C.psbfr], writes=[woTr[oc]], tag=f"l2a_wote{dc}_{oc}")
    for tg in range(8):
        for ichunk in range(NO):
            pass
        for tt in range(2):
            for hh in range(2):
                bank = tt * 2 + hh
                pb, pbr = C.pb[2 + bank], C.pbr[2 + bank]

                def go(e, tg=tg, tt=tt, hh=hh, pb=pb):
                    last = None
                    for oc in range(NO):
                        last = e.matmul(
                            pb[:, :], outfm[:, oc * S + tg * 256 + tt * 128: oc * S + tg * 256 + (tt + 1) * 128],
                            woT[:, oc * 1024 + hh * 512: oc * 1024 + (hh + 1) * 512],
                            start=(oc == 0), stop=(oc == NO - 1))
                    return last
                P.op("tensor", go, reads=woTr + outfmr, writes=[pbr], tag=f"l2a_om{tg}_{tt}_{hh}")
                xtile = (tg * 256 + tt * 128) // 128
                x_ap = C.X[:, xtile * 1024 + hh * 512: xtile * 1024 + hh * 512 + 512]
                P.op("vector", lambda e, x_ap=x_ap, pb=pb, xtile=xtile: e.tensor_tensor(
                    x_ap, x_ap, pb[:, :], ALU.add),
                    reads=[pbr, C.Xr[xtile]], writes=[C.Xr[xtile]], tag=f"l2a_x{tg}_{tt}_{hh}")
    P.op("vector", lambda e: e.memset(C.stat[:, 16:48], 0.0), writes=[C.statr], tag="l2a_statz")
    C.arena.reset()



def emit_l1_attn(C: Core, W):
    P = C.P
    A = C.arena
    nc = P.nc
    DSTATE, DTRANK = 16, 64
    m0 = A.mark()
    emit_layernorm(C)

    bcd = nc.dram_tensor("l1a_bc", [32, S], F32).ap()
    bcdr = Region("l1a_bcd")
    drd = nc.dram_tensor("l1a_dr", [64, S], BF16).ap()
    drdr = Region("l1a_drd")

    wst, wstr = A.take(F32, 1024, "l1a_wst")
    wbf, wbfr = A.take(BF16, 1024, "l1a_wbf")
    xpwT, xpwTr = A.take(BF16, 8 * 96, "l1a_xpwT")
    drank, drankr = A.take(BF16, S, "l1a_drank")
    bc32, bc32r = A.take(F32, S, "l1a_bc32")
    P.dma("sync", wst[0:96, :], W["L1_attn_x_proj_w"], writes=[wstr], tag="l1a_xpl")
    P.op("vector", lambda e: e.tensor_copy(wbf[0:96, :], wst[0:96, :]),
         reads=[wstr], writes=[wbfr], tag="l1a_xpc")
    for c in range(NC):
        p_ap = C.psbf[:, (c % 8) * 128:(c % 8) * 128 + 128]
        P.op("tensor", lambda e, c=c, p_ap=p_ap: e.transpose(
            p_ap[:, 0:96], wbf[0:96, c * 128:(c + 1) * 128], C.ident[0:96, 0:96]),
            reads=[wbfr, C.hcbr], writes=[C.psbfr], tag=f"l1a_xpt{c}")
        P.op("scalar", lambda e, c=c, p_ap=p_ap: e.copy(xpwT[:, c * 96:(c + 1) * 96], p_ap[:, 0:96]),
             reads=[C.psbfr], writes=[xpwTr], tag=f"l1a_xpte{c}")
    for n0 in range(0, S, 512):
        pb, pbr = C.pb[0], C.pbr[0]

        def gxp(e, n0=n0, pb=pb):
            last = None
            for c in range(NC):
                last = e.matmul(pb[0:96, :], xpwT[:, c * 96:(c + 1) * 96],
                                C.ln_fm[:, c * S + n0: c * S + n0 + 512],
                                start=(c == 0), stop=(c == NC - 1))
            return last
        P.op("tensor", gxp, reads=[xpwTr] + C.ln_fmr, writes=[pbr], tag=f"l1a_xpm{n0}")
        P.op("scalar", lambda e, n0=n0, pb=pb: e.copy(drank[0:64, n0:n0 + 512], pb[0:64, :]),
             reads=[pbr], writes=[drankr], tag=f"l1a_dre{n0}")
        P.op("vector", lambda e, n0=n0, pb=pb: e.tensor_copy(bc32[0:32, n0:n0 + 512], pb[64:96, :]),
             reads=[pbr], writes=[bc32r], tag=f"l1a_bce{n0}")
    P.dma("gpsimd", bcd, bc32[0:32, :], reads=[bc32r], writes=[bcdr], tag="l1a_bcs")
    P.dma("gpsimd", drd, drank[0:64, :], reads=[drankr], writes=[drdr], tag="l1a_drs")
    A.rewind(m0)
    drank2, drank2r = A.take(BF16, S, "l1a_drank2")
    P.dma("sync", drank2[0:64, :], drd, reads=[drdr], writes=[drank2r], tag="l1a_drl")
    wst2, wst2r = A.take(F32, 1024, "l1a_wst2")
    wbf2, wbf2r = A.take(BF16, 1024, "l1a_wbf2")

    dtwT, dtwTr = A.take(BF16, 8 * 128, "l1a_dtwT")
    dtb, dtbr = A.take(F32, 8, "l1a_dtb")
    for c in range(NC):
        P.dma("sync", wst2[:, 0:64], W["L1_attn_dt_proj_w"][c * 128:(c + 1) * 128, :],
              writes=[wst2r], tag=f"l1a_dtl{c}")
        P.op("vector", lambda e: e.tensor_copy(wbf2[:, 0:64], wst2[:, 0:64]),
             reads=[wst2r], writes=[wbf2r], tag=f"l1a_dtc{c}")
        p_ap = C.psbf[:, 0:128]
        P.op("tensor", lambda e, p_ap=p_ap: e.transpose(p_ap[0:64, 0:128], wbf2[:, 0:64], C.ident),
             reads=[wbf2r, C.hcbr], writes=[C.psbfr], tag=f"l1a_dtt{c}")
        P.op("scalar", lambda e, c=c, p_ap=p_ap: e.copy(dtwT[0:64, c * 128:(c + 1) * 128], p_ap[0:64, 0:128]),
             reads=[C.psbfr], writes=[dtwTr], tag=f"l1a_dtte{c}")
    P.dma("sync", wst2[0:8, 0:128], W["L1_attn_dt_proj_b"].rearrange("(c p) -> c p", p=128),
          writes=[wst2r], tag="l1a_dtbl")
    idf, idfr = A.take(F32, 128, "l1a_idf")
    P.op("vector", lambda e: e.tensor_copy(idf[:, :], C.ident),
         reads=[C.hcbr], writes=[idfr], tag="l1a_idf")
    pb, pbr = C.pb[1], C.pbr[1]
    P.op("tensor", lambda e, pb=pb: e.transpose(pb[:, 0:8], wst2[0:8, 0:128], idf[0:8, 0:8]),
         reads=[wst2r, idfr], writes=[pbr], tag="l1a_dtbt")
    P.op("vector", lambda e, pb=pb: e.tensor_copy(dtb[:, :], pb[:, 0:8]),
         reads=[pbr], writes=[dtbr], tag="l1a_dtbe")
    acol, acolr = A.take(F32, 128, "l1a_acol")
    P.dma("sync", wst2[:, 0:128], W["L1_attn_A_log"].rearrange("(g p) n -> g (p n)", p=8),
          writes=[wst2r], tag="l1a_al")
    pb, pbr = C.pb[1], C.pbr[1]
    P.op("tensor", lambda e, pb=pb: e.transpose(pb[:, 0:128], wst2[:, 0:128], idf[:, :]),
         reads=[wst2r, idfr], writes=[pbr], tag="l1a_at")
    P.op("scalar", lambda e, pb=pb: e.activation(acol[:, :], pb[:, 0:128], AF.Exp),
         reads=[pbr], writes=[acolr], tag="l1a_ae")
    P.op("vector", lambda e: e.tensor_scalar_mul(acol[:, :], acol[:, :], -1.0),
         reads=[acolr], writes=[acolr], tag="l1a_an")
    dcolv, dcolvr = A.take(F32, 8, "l1a_dcol")
    P.dma("sync", wst2[0:8, 0:128], W["L1_attn_D"].rearrange("(c p) -> c p", p=128),
          writes=[wst2r], tag="l1a_dl")
    pb, pbr = C.pb[1], C.pbr[1]
    P.op("tensor", lambda e, pb=pb: e.transpose(pb[:, 0:8], wst2[0:8, 0:128], idf[0:8, 0:8]),
         reads=[wst2r, idfr], writes=[pbr], tag="l1a_dt2")
    P.op("vector", lambda e, pb=pb: e.tensor_copy(dcolv[:, :], pb[:, 0:8]),
         reads=[pbr], writes=[dcolvr], tag="l1a_de")

    btile, btiler = A.take(F32, S, "l1a_bt")
    ctile, ctiler = A.take(F32, S, "l1a_ct")
    P.dma("gpsimd", btile[:, :], bcd[0:16][None].broadcast_to([8, 16, S]),
          reads=[bcdr], writes=[btiler], tag="l1a_btl")
    P.dma("gpsimd", ctile[:, :], bcd[16:32][None].broadcast_to([8, 16, S]),
          reads=[bcdr], writes=[ctiler], tag="l1a_ctl")

    x32c, x32cr = A.take(F32, S, "l1a_x32")
    lnst, lnstr = A.take(F32, 1024, "l1a_lnst")
    delta, deltar = A.take(F32, S, "l1a_delta")
    deltab, deltabr = A.take(BF16, S, "l1a_deltab")
    ubf, ubfr = A.take(BF16, S, "l1a_ubf")
    dA, dAr = A.take(F32, S, "l1a_dA")
    dBu, dBur = A.take(F32, S, "l1a_dBu")
    hsc, hscr = A.take(F32, S, "l1a_h")
    hcb_, hcb_r = A.take(BF16, S, "l1a_hc")
    ysc, yscr = A.take(F32, S, "l1a_y")
    ytt, yttr = A.take(F32, 128, "l1a_ytt")
    sp1, sp1r = A.take(F32, S, "l1a_sp1")
    cm1 = P.const_col(-1.0)
    cp1 = P.const_col(1.0)

    for c in range(NC):
        for i in range(NT):
            emit_ln32_tile(C, i, lnst[:, :], lnstr)
            pb, pbr = C.pb[1], C.pbr[1]
            P.op("tensor", lambda e, c=c, pb=pb: e.transpose(
                pb[:, 0:128], lnst[:, c * 128:(c + 1) * 128], idf[:, :]),
                reads=[lnstr, idfr], writes=[pbr], tag=f"l1a_x32t{c}_{i}")
            P.op("vector", lambda e, i=i, pb=pb: e.tensor_copy(
                x32c[:, i * 128:(i + 1) * 128], pb[:, 0:128]),
                reads=[pbr], writes=[x32cr], tag=f"l1a_x32e{c}_{i}")
        for n0 in range(0, S, 512):
            pb, pbr = C.pb[2], C.pbr[2]
            P.op("tensor", lambda e, c=c, n0=n0, pb=pb: e.matmul(
                pb[:, :], dtwT[0:64, c * 128:(c + 1) * 128], drank2[0:64, n0:n0 + 512],
                start=True, stop=True),
                reads=[dtwTr, drank2r], writes=[pbr], tag=f"l1a_dm{c}_{n0}")
            P.op("vector", lambda e, n0=n0, pb=pb, c=c: e.tensor_scalar(
                delta[:, n0:n0 + 512], pb[:, :], dtb[:, c:c + 1], None, ALU.add),
                reads=[pbr, dtbr], writes=[deltar], tag=f"l1a_db{c}_{n0}")
        P.op("vector", lambda e: e.tensor_scalar_mul(sp1[:, :], delta[:, :], -1.0),
             reads=[deltar], writes=[sp1r], tag=f"l1a_sp1{c}")
        P.op("vector", lambda e: e.tensor_tensor(sp1[:, :], sp1[:, :], delta[:, :], ALU.min),
             reads=[sp1r, deltar], writes=[sp1r], tag=f"l1a_sp1b{c}")
        P.op("scalar", lambda e: e.activation(sp1[:, :], sp1[:, :], AF.Exp),
             reads=[sp1r], writes=[sp1r], tag=f"l1a_sp2{c}")
        P.op("scalar", lambda e: e.activation(sp1[:, :], sp1[:, :], AF.Ln, bias=cp1),
             reads=[sp1r, P.consts_region], writes=[sp1r], tag=f"l1a_sp3{c}")
        P.op("vector", lambda e: e.tensor_scalar(delta[:, :], delta[:, :], 0.0, None, ALU.max),
             reads=[deltar], writes=[deltar], tag=f"l1a_sp4{c}")
        P.op("vector", lambda e: e.tensor_tensor(delta[:, :], delta[:, :], sp1[:, :], ALU.add),
             reads=[deltar, sp1r], writes=[deltar], tag=f"l1a_sp5{c}")
        P.op("vector", lambda e: e.tensor_copy(deltab[:, :], delta[:, :]),
             reads=[deltar], writes=[deltabr], tag=f"l1a_dc{c}")
        P.op("gpsimd", lambda e: e.tensor_tensor(ubf[:, :], delta[:, :], x32c[:, :], ALU.mult),
             reads=[deltar, x32cr], writes=[ubfr], tag=f"l1a_u{c}")
        P.op("vector", lambda e: e.memset(ysc[:, :], 0.0), writes=[yscr], tag=f"l1a_y0{c}")
        for g16 in range(16):
            g = c * 16 + g16
            sel = C.hc(f"l1sel{g16}", 128)
            prd, prdr = C.pb[3], C.pbr[3]
            pru, prur = C.pb[4], C.pbr[4]
            for q in range(4):
                P.op("tensor", lambda e, q=q, prd=prd, sel=sel: e.matmul(
                    prd[:, :], sel, deltab[:, q * 512:(q + 1) * 512], start=True, stop=True),
                    reads=[deltabr, C.hcbr], writes=[prdr], tag=f"l1a_rd{g}_{q}")
                P.op("scalar", lambda e, q=q, g=g, prd=prd: e.activation(
                    dA[:, q * 512:(q + 1) * 512], prd[:, :], AF.Exp, scale=acol[:, g:g + 1]),
                    reads=[prdr, acolr], writes=[dAr], tag=f"l1a_dA{g}_{q}")
                P.op("tensor", lambda e, q=q, pru=pru, sel=sel: e.matmul(
                    pru[:, :], sel, ubf[:, q * 512:(q + 1) * 512], start=True, stop=True),
                    reads=[ubfr, C.hcbr], writes=[prur], tag=f"l1a_ru{g}_{q}")
                P.op("vector", lambda e, q=q, pru=pru: e.tensor_tensor(
                    dBu[:, q * 512:(q + 1) * 512], pru[:, :], btile[:, q * 512:(q + 1) * 512], ALU.mult),
                    reads=[prur, btiler], writes=[dBur], tag=f"l1a_dBu{g}_{q}")
            P.op("vector", lambda e: e.tensor_tensor_scan(
                hsc[:, :], dA[:, :], dBu[:, :], 0.0, ALU.mult, ALU.add),
                reads=[dAr, dBur], writes=[hscr], tag=f"l1a_scan{g}")
            P.op("gpsimd", lambda e: e.tensor_tensor(hcb_[:, :], hsc[:, :], ctile[:, :], ALU.mult),
                 reads=[hscr, ctiler], writes=[hcb_r], tag=f"l1a_hc{g}")
            m2 = C.hc(f"l1m2_{g16 % 4}", 32)
            for q in range(4):
                pyb, pybr = C.pb[5], C.pbr[5]
                P.op("tensor", lambda e, q=q, m2=m2, pyb=pyb, g16=g16: e.matmul(
                    pyb[32 * q:32 * q + 32, :], m2, hcb_[:, q * 512:(q + 1) * 512],
                    start=(g16 % 4 == 0), stop=(g16 % 4 == 3),
                    tile_position=(0, 32 * q), skip_group_check=True),
                    reads=[hcb_r, C.hcbr], writes=[pybr], tag=f"l1a_yr{g}_{q}")
                if g16 % 4 == 3:
                    rw = 32 * (g16 // 4)
                    srcap = (lambda qq: C.pb[5].rearrange("p n -> p n"))(q) if False else None
                    P.op("vector", lambda e, q=q, rw=rw: e.tensor_copy(
                        ysc[rw:rw + 32, q * 512:(q + 1) * 512],
                        C.pb[5].rearrange("(a b) n -> a b n", a=4)[q] if False else C.pb[5][32 * q:32 * q + 32, :]),
                        reads=[C.pbr[5]], writes=[yscr], tag=f"l1a_ye{g}_{q}")
        P.op("vector", lambda e, c=c: e.scalar_tensor_tensor(
            ysc[:, :], x32c[:, :], dcolv[:, c:c + 1], ysc[:, :], ALU.mult, ALU.add),
            reads=[x32cr, dcolvr, yscr], writes=[yscr], tag=f"l1a_yd{c}")
        for i in range(NT):
            pb, pbr = C.pb[1], C.pbr[1]
            P.op("tensor", lambda e, i=i, pb=pb: e.transpose(
                pb[:, 0:128], ysc[:, i * 128:(i + 1) * 128], idf[:, :]),
                reads=[yscr, idfr], writes=[pbr], tag=f"l1a_yt{c}_{i}")
            P.op("vector", lambda e, pb=pb: e.tensor_copy(ytt[:, :], pb[:, 0:128]),
                 reads=[pbr], writes=[yttr], tag=f"l1a_yte{c}_{i}")
            x_ap = C.X[:, i * 1024 + c * 128: i * 1024 + (c + 1) * 128]
            P.op("vector", lambda e, x_ap=x_ap: e.tensor_tensor(x_ap, x_ap, ytt[:, :], ALU.add),
                 reads=[yttr, C.Xr[i]], writes=[C.Xr[i]], tag=f"l1a_xa{c}_{i}")
    C.arena.reset()



def emit_l3_ffn(C: Core, W):
    P = C.P
    A = C.arena
    nc = P.nc
    H, DH, HID = 8, 64, 2730
    NAC = 22
    emit_layernorm(C, fm=False)

    yd = nc.dram_tensor("l3f_y", [NT, 128, 1024], F32).ap()
    ydr = [Region(f"l3f_yd{i}") for i in range(NT)]
    rot_ap = nc.dram_tensor("l3f_rot", [128, 2 * S], F32, kind="ExternalInput").ap()
    m0 = A.mark()
    lnst0, lnst0r = A.take(F32, 1024, "l3f_lnst")
    for i in range(NT):
        emit_ln32_tile(C, i, lnst0[:, :], lnst0r)
        P.dma("gpsimd", yd[i], lnst0[:, :], reads=[lnst0r], writes=[ydr[i]], tag=f"l3f_y0{i}")
    A.rewind(m0)

    rott, rottr = A.take(BF16, 2 * S, "l3f_rot")
    P.dma("gpsimd", rott[:, :], rot_ap, writes=[rottr], tag="l3f_rotl")
    w1d = nc.dram_tensor("l3f_w1d", [2, 1024, 5460], BF16).ap()
    w1dr = Region("l3f_w1d")
    w2d = nc.dram_tensor("l3f_w2d", [2, HID, 1024], BF16).ap()
    w2dr = Region("l3f_w2d")
    wqd = nc.dram_tensor("l3f_wqd", [2, 1024, 1536], BF16).ap()
    wqdr = Region("l3f_wqd")
    wod = nc.dram_tensor("l3f_wod", [2, 512, 1024], BF16).ap()
    wodr = Region("l3f_wod")
    for lc in range(2):
        P.dma("gpsimd", wqd[lc], W["L3_ffn_wqkv"][lc], writes=[wqdr], tag=f"l3f_cq{lc}")
        P.dma("gpsimd", wod[lc], W["L3_ffn_wo"][lc], writes=[wodr], tag=f"l3f_co{lc}")
        for j0 in range(0, 5460, 1820):
            P.dma("gpsimd", w1d[lc][:, j0:j0 + 1820], W["L3_ffn_w1"][lc][:, j0:j0 + 1820],
                  writes=[w1dr], tag=f"l3f_c1{lc}_{j0}")
        P.dma("gpsimd", w2d[lc], W["L3_ffn_w2"][lc], writes=[w2dr], tag=f"l3f_c2{lc}")
    fcell, fcellr = A.take(F32, 1, "l3f_fence")
    idf, idfr = A.take(F32, 8, "l3f_idf")
    P.op("vector", lambda e: e.tensor_copy(idf[:, :], C.ident[:, 0:8]),
         reads=[C.hcbr], writes=[idfr], tag="l3f_idf")
    nh = P.const_col(-0.5)
    cpe = P.const_col(1e-5)

    def build_shift_fm(l, hfm, hfmr, ybufs, tag):
        for c in range(4, NC):
            P.op("vector", lambda e, c=c: e.memset(hfm[:, c * S:c * S + 1], 0.0),
                 writes=[hfmr[c]], tag=f"{tag}z{l}_{c}")
        nb = len(ybufs)
        for i in range(NT):
            ytile, ytiler, ybf, ybfr = ybufs[i % nb]
            P.dma("sync", ytile[:, :], yd[i], reads=[ydr[i]], writes=[ytiler], tag=f"{tag}l{l}_{i}")
            P.op("vector", lambda e, ytile=ytile, ybf=ybf: e.tensor_copy(ybf[:, :], ytile[:, :]),
                 reads=[ytiler], writes=[ybfr], tag=f"{tag}c{l}_{i}")
            for c in range(NC):
                p_ap = C.psbf[:, (c % 8) * 128:(c % 8) * 128 + 128]
                P.op("tensor", lambda e, c=c, p_ap=p_ap, ybf=ybf: e.transpose(
                    p_ap, ybf[:, c * 128:(c + 1) * 128], C.ident),
                    reads=[ybfr, C.hcbr], writes=[C.psbfr], tag=f"{tag}t{l}_{i}_{c}")
                if c < 4:
                    P.op("vector", lambda e, c=c, i=i, p_ap=p_ap: e.tensor_copy(
                        hfm[:, c * S + i * 128: c * S + (i + 1) * 128], p_ap),
                        reads=[C.psbfr], writes=[hfmr[c]], tag=f"{tag}e{l}_{i}_{c}")
                elif i < NT - 1:
                    P.op("vector", lambda e, c=c, i=i, p_ap=p_ap: e.tensor_copy(
                        hfm[:, c * S + i * 128 + 1: c * S + (i + 1) * 128 + 1], p_ap),
                        reads=[C.psbfr], writes=[hfmr[c]], tag=f"{tag}e{l}_{i}_{c}")
                else:
                    P.op("vector", lambda e, c=c, i=i, p_ap=p_ap: e.tensor_copy(
                        hfm[:, c * S + i * 128 + 1: (c + 1) * S], p_ap[:, 0:127]),
                        reads=[C.psbfr], writes=[hfmr[c]], tag=f"{tag}e{l}_{i}_{c}")

    for l in range(2):
        m1 = A.mark()
        qkv, qkvr = A.take_tiles(BF16, S, 12, "l3f_qkv")
        vtm, vtmr = A.take(BF16, NT * 512, "l3f_vtm")
        att, attr = A.take(BF16, NT * 512, "l3f_att")
        wqs2 = [A.take(BF16, NC * 128, "l3f_wqs")]
        mh = A.mark()
        hfm, hfmr = A.take_tiles(BF16, S, NC, "l3f_hfm")
        ybufs1 = [A.take(F32, 1024, "l3f_yt") + A.take(BF16, 1024, "l3f_ybf")]
        build_shift_fm(l, hfm, hfmr, ybufs1, "l3f_h")
        for mc in range(12):
            wqs, wqsr = wqs2[0]
            P.dma("sync", wqs.rearrange("p (c n) -> p c n", c=NC),
                  wqd[l][:, mc * 128:(mc + 1) * 128].rearrange("(c p) n -> p c n", p=128),
                  reads=[wqdr], writes=[wqsr], tag=f"l3f_wq{l}_{mc}")
            for n0 in range(0, S, 512):
                bank = 2 * (mc % 2) + (n0 // 512) % 2
                pb, pbr = C.pb[bank], C.pbr[bank]

                def gq(e, mc=mc, n0=n0, pb=pb, hfm=hfm, wqs=wqs):
                    last = None
                    for c in range(NC):
                        last = e.matmul(pb[:, :], wqs[:, c * 128:(c + 1) * 128],
                                        hfm[:, c * S + n0: c * S + n0 + 512],
                                        start=(c == 0), stop=(c == NC - 1))
                    return last
                P.op("tensor", gq, reads=[wqsr] + hfmr, writes=[pbr], tag=f"l3f_qm{l}_{mc}_{n0}")
                P.op("scalar", lambda e, mc=mc, n0=n0, pb=pb: e.copy(
                    qkv[:, mc * S + n0: mc * S + n0 + 512], pb[:, :]),
                    reads=[pbr], writes=[qkvr[mc]], tag=f"l3f_qe{l}_{mc}_{n0}")
        for i in range(NT):
            for vc in range(4):
                p_ap = C.psbf[:, (vc % 8) * 128:(vc % 8) * 128 + 128]
                P.op("tensor", lambda e, i=i, vc=vc, p_ap=p_ap: e.transpose(
                    p_ap, qkv[:, (8 + vc) * S + i * 128: (8 + vc) * S + (i + 1) * 128], C.ident),
                    reads=[qkvr[8 + vc], C.hcbr], writes=[C.psbfr], tag=f"l3f_vt{l}_{i}_{vc}")
                P.op("vector", lambda e, i=i, vc=vc, p_ap=p_ap: e.tensor_copy(
                    vtm[:, i * 512 + vc * 128: i * 512 + (vc + 1) * 128], p_ap),
                    reads=[C.psbfr], writes=[vtmr], tag=f"l3f_ve{l}_{i}_{vc}")
        A.rewind(mh)
        mh2 = A.mark()
        qsc, qscr = A.take(F32, 4, "l3f_qs")
        P.dma("sync", qsc[0:64, 0:1], W["L3_ffn_q_scale"][l].rearrange("(n o) -> n o", o=1),
              writes=[qscr], tag=f"l3f_qsl{l}")
        P.dma("sync", qsc[0:64, 1:2], W["L3_ffn_k_scale"][l].rearrange("(n o) -> n o", o=1),
              writes=[qscr], tag=f"l3f_ksl{l}")
        qh, qhr = A.take(BF16, S, "l3f_qh")
        kh, khr = A.take(BF16, S, "l3f_kh")
        sq2, sq2r = A.take(BF16, S, "l3f_sq2")
        nrm, nrmr = A.take(F32, S, "l3f_nrm")
        rnr, rnrr = A.take(BF16, S, "l3f_rnr")
        rotb, rotbr = A.take(BF16, S, "l3f_rotb")
        pbuf2 = [A.take(BF16, 5 * 128, f"l3f_pbuf{d}") for d in range(2)]
        pT2 = [A.take(BF16, 128, f"l3f_pT{d}") for d in range(2)]
        mcol2 = [A.take(F32, 8, f"l3f_mcol{d}") for d in range(2)]
        dnegm, dnegmr = A.take(BF16, 128, "l3f_dneg")
        onegm, onegmr = A.take(BF16, 128, "l3f_oneg")
        triu01 = C.hc("l0m0", 512)[:, 0:128]
        P.op("vector", lambda e: e.tensor_tensor(dnegm[:, :], C.ident, triu01, ALU.subtract),
             reads=[C.hcbr], writes=[dnegmr], tag=f"l3f_dn1{l}")
        P.op("vector", lambda e: e.tensor_scalar_mul(dnegm[:, :], dnegm[:, :], 1e30),
             reads=[dnegmr], writes=[dnegmr], tag=f"l3f_dn2{l}")
        P.op("vector", lambda e: e.tensor_scalar(onegm[:, :], triu01, -1.0, 1e30, ALU.add, ALU.mult),
             reads=[C.hcbr], writes=[onegmr], tag=f"l3f_on1{l}")
        psch = [Region(f"l3f_psA{l}"), Region(f"l3f_psB{l}")]
        P.op("vector", lambda e: e.memset(fcell[0:1, 0:1], 0.0),
             reads=[C.psbfr], writes=[psch[0], fcellr], tag=f"l3f_fiA{l}")
        P.op("vector", lambda e: e.memset(fcell[0:1, 0:1], 0.0),
             reads=[C.psbfr], writes=[psch[1], fcellr], tag=f"l3f_fiB{l}")

        for h in range(H):
            hb = (h % 2) * 64
            hc_ = h // 2
            krows = slice(hb, hb + 64)
            for which in range(2):
                ch = hc_ + (0 if which == 0 else 4)
                src_rows = qkv[krows, ch * S:(ch + 1) * S]
                base = 0 if which == 0 else 64
                bs = slice(base, base + 64)
                cost = rott[bs, 0:S]
                sint = rott[bs, S:2 * S]
                rhT = C.hc("l3frh", 64)[bs, :]
                P.op("vector", lambda e, src_rows=src_rows, bs=bs: e.tensor_copy(
                    rotb[bs, :], src_rows),
                    reads=[qkvr[ch]], writes=[rotbr], tag=f"l3f_st{l}_{h}_{which}")
                P.op("scalar", lambda e, bs=bs: e.activation(
                    sq2[bs, :], rotb[bs, :], AF.Square),
                    reads=[rotbr], writes=[sq2r], tag=f"l3f_sq{l}_{h}_{which}")
                for n0 in range(0, S, 512):
                    pb, pbr = C.pb[1], C.pbr[1]
                    P.op("tensor", lambda e, n0=n0, pb=pb, bs=bs, base=base: e.matmul(
                        pb[0:1, :], C.onecol_bf[base:base + 64, 0:1], sq2[bs, n0:n0 + 512],
                        start=True, stop=True),
                        reads=[sq2r, C.hcbr], writes=[pbr], tag=f"l3f_ss{l}_{h}_{which}_{n0}")
                    P.op("vector", lambda e, n0=n0, pb=pb: e.tensor_scalar(
                        nrm[0:1, n0:n0 + 512], pb[0:1, :], 1e-24, None, ALU.max),
                        reads=[pbr], writes=[nrmr], tag=f"l3f_sm{l}_{h}_{which}_{n0}")
                P.op("scalar", lambda e: e.activation(nrm[0:1, :], nrm[0:1, :], AF.Ln),
                     reads=[nrmr], writes=[nrmr], tag=f"l3f_ln{l}_{h}_{which}")
                P.op("scalar", lambda e: e.activation(nrm[0:1, :], nrm[0:1, :], AF.Exp, scale=nh[0:1, :]),
                     reads=[nrmr, P.consts_region], writes=[nrmr], tag=f"l3f_ex{l}_{h}_{which}")
                P.op("vector", lambda e: e.tensor_copy(rnr[0:1, :], nrm[0:1, :]),
                     reads=[nrmr], writes=[rnrr], tag=f"l3f_rc{l}_{h}_{which}")
                for n0 in range(0, S, 512):
                    pb, pbr = C.pb[1], C.pbr[1]
                    P.op("tensor", lambda e, n0=n0, pb=pb: e.matmul(
                        pb[0:64, :], C.ones_bf[0:1, 0:64], rnr[0:1, n0:n0 + 512],
                        start=True, stop=True),
                        reads=[rnrr, C.hcbr], writes=[pbr], tag=f"l3f_rb{l}_{h}_{which}_{n0}")
                    P.op("scalar", lambda e, n0=n0, pb=pb, bs=bs: e.copy(
                        sq2[bs, n0:n0 + 512], pb[0:64, :]),
                        reads=[pbr], writes=[sq2r], tag=f"l3f_rbe{l}_{h}_{which}_{n0}")
                qs_col = qsc[0:64, which:which + 1]
                P.op("vector", lambda e, bs=bs: e.tensor_tensor(
                    rotb[bs, :], rotb[bs, :], sq2[bs, :], ALU.mult),
                    reads=[rotbr, sq2r], writes=[rotbr], tag=f"l3f_n1{l}_{h}_{which}")
                P.op("vector", lambda e, qs_col=qs_col, bs=bs: e.tensor_scalar(
                    rotb[bs, :], rotb[bs, :], qs_col, None, ALU.mult),
                    reads=[rotbr, qscr], writes=[rotbr], tag=f"l3f_n2{l}_{h}_{which}")
                for n0 in range(0, S, 512):
                    pb, pbr = C.pb[1], C.pbr[1]
                    P.op("tensor", lambda e, n0=n0, pb=pb, bs=bs, rhT=rhT: e.matmul(
                        pb[bs, :], rhT, rotb[bs, n0:n0 + 512], start=True, stop=True),
                        reads=[rotbr, C.hcbr], writes=[pbr], tag=f"l3f_rh{l}_{h}_{which}_{n0}")
                    P.op("vector", lambda e, n0=n0, pb=pb, bs=bs, sint=sint: e.tensor_tensor(
                        sq2[bs, n0:n0 + 512], pb[bs, :], sint[:, n0:n0 + 512], ALU.mult),
                        reads=[pbr, rottr], writes=[sq2r], tag=f"l3f_rs{l}_{h}_{which}_{n0}")
                P.op("gpsimd", lambda e, bs=bs, cost=cost: e.tensor_tensor(
                    rotb[bs, :], rotb[bs, :], cost, ALU.mult),
                    reads=[rotbr, rottr], writes=[rotbr], tag=f"l3f_r1{l}_{h}_{which}")
                if which == 0:
                    P.op("vector", lambda e: e.tensor_tensor(
                        qh[0:64, :], rotb[0:64, :], sq2[0:64, :], ALU.add),
                        reads=[rotbr, sq2r], writes=[qhr], tag=f"l3f_r4{l}_{h}_{which}")
                else:
                    P.op("vector", lambda e: e.tensor_tensor(
                        kh[64:128, :], rotb[64:128, :], sq2[64:128, :], ALU.add),
                        reads=[rotbr, sq2r], writes=[khr], tag=f"l3f_r4{l}_{h}_{which}")
                    P.op("vector", lambda e: e.tensor_copy(kh[0:64, :], kh[64:128, :]),
                         reads=[khr], writes=[khr], tag=f"l3f_r5{l}_{h}_{which}")
            P.op("vector", lambda e: e.tensor_scalar_mul(qh[0:64, :], qh[0:64, :], 8.0),
                 reads=[qhr], writes=[qhr], tag=f"l3f_q8{l}_{h}")
            for qt in range(NT):
                d2 = qt % 2
                pA, pAr = (C.pb[2], C.pbr[2]) if d2 == 0 else (C.pb[5], C.pbr[5])
                pB, pBr = (C.pb[3], C.pbr[3]) if d2 == 0 else (C.pb[6], C.pbr[6])
                pV, pVr = (C.pb[4], C.pbr[4]) if d2 == 0 else (C.pb[0], C.pbr[0])
                pbuf, pbufr = pbuf2[d2]
                pT_, pT_r = pT2[d2]
                mcol, mcolr = mcol2[d2]
                chan = psch[d2]
                r0 = max(0, 4 - qt)
                for r in range(r0, 5):
                    kt = qt + r - 4
                    tgt, tgtr = ((pA, pAr) if r < 4 else (pB, pBr))
                    cols = slice(r * 128, (r + 1) * 128) if r < 4 else slice(0, 128)
                    P.op("tensor", lambda e, kt=kt, qt=qt, tgt=tgt, cols=cols: e.matmul(
                        tgt[:, cols], qh[0:64, qt * 128:(qt + 1) * 128],
                        kh[0:64, kt * 128:(kt + 1) * 128], start=True, stop=True,
                        skip_group_check=True),
                        reads=[qhr, khr], writes=[tgtr], tag=f"l3f_s{l}_{h}_{qt}_{r}")
                if r0 == 0:
                    P.op("vector", lambda e, pA=pA: e.tensor_tensor(
                        pA[:, 0:128], pA[:, 0:128], onegm[:, :], ALU.add),
                        reads=[pAr, onegmr], writes=[pAr], tag=f"l3f_am0{l}_{h}_{qt}")
                P.op("vector", lambda e, pB=pB: e.tensor_tensor(
                    pB[:, 0:128], pB[:, 0:128], dnegm[:, :], ALU.add),
                    reads=[pBr, dnegmr], writes=[pBr], tag=f"l3f_am4{l}_{h}_{qt}")
                if r0 < 4:
                    P.op("vector", lambda e, r0=r0, pA=pA, mcol=mcol: e.reduce_max(
                        mcol[:, 0:1], pA[:, r0 * 128:512], axis=AX.X),
                        reads=[pAr], writes=[mcolr], tag=f"l3f_m1{l}_{h}_{qt}")
                    P.op("vector", lambda e, pB=pB, mcol=mcol: e.reduce_max(
                        mcol[:, 1:2], pB[:, 0:128], axis=AX.X),
                        reads=[pBr], writes=[mcolr], tag=f"l3f_m2{l}_{h}_{qt}")
                    P.op("vector", lambda e, mcol=mcol: e.tensor_tensor(
                        mcol[:, 0:1], mcol[:, 0:1], mcol[:, 1:2], ALU.max),
                        reads=[mcolr], writes=[mcolr], tag=f"l3f_m3{l}_{h}_{qt}")
                else:
                    P.op("vector", lambda e, pB=pB, mcol=mcol: e.reduce_max(
                        mcol[:, 0:1], pB[:, 0:128], axis=AX.X),
                        reads=[pBr], writes=[mcolr], tag=f"l3f_m2{l}_{h}_{qt}")
                P.op("vector", lambda e, mcol=mcol: e.tensor_scalar_mul(mcol[:, 2:3], mcol[:, 0:1], -1.0),
                     reads=[mcolr], writes=[mcolr], tag=f"l3f_m4{l}_{h}_{qt}")
                if r0 < 4:
                    P.op("scalar", lambda e, r0=r0, pA=pA, pbuf=pbuf, mcol=mcol: e.activation(
                        pbuf[:, r0 * 128:512], pA[:, r0 * 128:512], AF.Exp, bias=mcol[:, 2:3]),
                        reads=[pAr, mcolr], writes=[pbufr], tag=f"l3f_e1{l}_{h}_{qt}")
                P.op("scalar", lambda e, pB=pB, pbuf=pbuf, mcol=mcol: e.activation(
                    pbuf[:, 512:640], pB[:, 0:128], AF.Exp, bias=mcol[:, 2:3]),
                    reads=[pBr, mcolr], writes=[pbufr], tag=f"l3f_e2{l}_{h}_{qt}")
                P.op("vector", lambda e, r0=r0, pbuf=pbuf, mcol=mcol: e.reduce_sum(
                    mcol[:, 3:4], pbuf[:, r0 * 128:640], axis=AX.X),
                    reads=[pbufr], writes=[mcolr], tag=f"l3f_rs{l}_{h}_{qt}")
                P.op("vector", lambda e, mcol=mcol: e.reciprocal(mcol[:, 4:5], mcol[:, 3:4]),
                     reads=[mcolr], writes=[mcolr], tag=f"l3f_rr{l}_{h}_{qt}")
                for r in range(r0, 5):
                    kt = qt + r - 4
                    p_ap = C.psbf[:, d2 * 512 + (r % 4) * 128: d2 * 512 + (r % 4) * 128 + 128]
                    P.op("tensor", lambda e, r=r, p_ap=p_ap, pbuf=pbuf: e.transpose(
                        p_ap, pbuf[:, r * 128:(r + 1) * 128], C.ident),
                        reads=[pbufr, C.hcbr], writes=[chan], tag=f"l3f_pt{l}_{h}_{qt}_{r}")
                    P.op("vector", lambda e, p_ap=p_ap, pT_=pT_: e.tensor_copy(pT_[:, :], p_ap),
                         reads=[chan], writes=[pT_r], tag=f"l3f_pte{l}_{h}_{qt}_{r}")
                    P.op("tensor", lambda e, r=r, kt=kt, h=h, pV=pV, r0=r0, pT_=pT_: e.matmul(
                        pV[:, 0:64], pT_[:, :], vtm[:, kt * 512 + h * 64: kt * 512 + (h + 1) * 64],
                        start=(r == r0), stop=(r == 4)),
                        reads=[pT_r, vtmr], writes=[pVr], tag=f"l3f_av{l}_{h}_{qt}_{r}")
                P.op("vector", lambda e, h=h, qt=qt, pV=pV, mcol=mcol: e.tensor_scalar(
                    att[:, qt * 512 + h * 64: qt * 512 + (h + 1) * 64], pV[:, 0:64],
                    mcol[:, 4:5], None, ALU.mult),
                    reads=[pVr, mcolr], writes=[attr], tag=f"l3f_ae{l}_{h}_{qt}")
        P.op("vector", lambda e: e.memset(fcell[0:1, 0:1], 0.0),
             reads=psch, writes=[C.psbfr, fcellr], tag=f"l3f_fo{l}")
        A.rewind(mh2)
        wo, wor = A.take(BF16, 4 * 1024, "l3f_wo")
        P.dma("sync", wo.rearrange("p (c n) -> p c n", c=4),
              wod[l].rearrange("(c p) n -> p c n", p=128),
              reads=[wodr], writes=[wor], tag=f"l3f_wol{l}")
        attfm, attfmr = A.take(BF16, 4 * S, "l3f_attfm")
        yt22 = [A.take(F32, 1024, f"l3f_yt2{d}") for d in range(2)]
        for i in range(NT):
            for vc in range(4):
                p_ap = C.psbf[:, (vc % 8) * 128:(vc % 8) * 128 + 128]
                P.op("tensor", lambda e, i=i, vc=vc, p_ap=p_ap: e.transpose(
                    p_ap, att[:, i * 512 + vc * 128: i * 512 + (vc + 1) * 128], C.ident),
                    reads=[attr, C.hcbr], writes=[C.psbfr], tag=f"l3f_at{l}_{i}_{vc}")
                P.op("vector", lambda e, i=i, vc=vc, p_ap=p_ap: e.tensor_copy(
                    attfm[:, vc * S + i * 128: vc * S + (i + 1) * 128], p_ap),
                    reads=[C.psbfr], writes=[attfmr], tag=f"l3f_ate{l}_{i}_{vc}")
        for i in range(NT):
            yt2, yt2r = yt22[i % 2]
            P.dma("sync", yt2[:, :], yd[i], reads=[ydr[i]], writes=[yt2r], tag=f"l3f_yrl{l}_{i}")
            for hh in range(2):
                pb, pbr = C.pb[5 + hh], C.pbr[5 + hh]

                def go(e, i=i, hh=hh, pb=pb):
                    last = None
                    for vc in range(4):
                        last = e.matmul(pb[:, :], attfm[:, vc * S + i * 128: vc * S + (i + 1) * 128],
                                        wo[:, vc * 1024 + hh * 512: vc * 1024 + (hh + 1) * 512],
                                        start=(vc == 0), stop=(vc == 3))
                    return last
                P.op("tensor", go, reads=[attfmr, wor], writes=[pbr], tag=f"l3f_om{l}_{i}_{hh}")
                P.op("vector", lambda e, hh=hh, pb=pb, yt2=yt2: e.tensor_tensor(
                    yt2[:, hh * 512:(hh + 1) * 512], yt2[:, hh * 512:(hh + 1) * 512],
                    pb[:, :], ALU.add),
                    reads=[pbr, yt2r], writes=[yt2r], tag=f"l3f_oa{l}_{i}_{hh}")
            P.dma("gpsimd", yd[i], yt2[:, :], reads=[yt2r], writes=[ydr[i]], tag=f"l3f_ys{l}_{i}")
        A.rewind(m1)

        m2_ = A.mark()
        hfm, hfmr = A.take_tiles(BF16, S, NC, "l3f_h2fm")
        ybufs2 = [A.take(F32, 1024, f"l3f_yt3{d}") + A.take(BF16, 1024, f"l3f_ybf2{d}")
                  for d in range(2)]
        build_shift_fm(l, hfm, hfmr, ybufs2, "l3f_g")
        sqf, sqfr = A.take(F32, S, "l3f_sqf")
        vrf, vrfr = A.take(F32, S, "l3f_vrf")
        scf, scfr = A.take(F32, S, "l3f_scf")
        mrow, mrowr2 = A.take(BF16, S, "l3f_mrow")
        rrow, rrowr2 = A.take(BF16, S, "l3f_rrow")
        mbc, mbcr = A.take(BF16, S, "l3f_mbc")
        rbc, rbcr = A.take(BF16, S, "l3f_rbc")
        gcols, gcolsr = A.take(F32, 16, "l3f_gc")
        for which, nm in ((0, "ln_g"), (1, "ln_b")):
            P.dma("sync", sqf[0:8, 0:128], W[f"L3_ffn_{nm}"][l].rearrange("(c p) -> c p", p=128),
                  writes=[sqfr], tag=f"l3f_{nm}{l}")
            pb, pbr = C.pb[1], C.pbr[1]
            P.op("tensor", lambda e, pb=pb: e.transpose(pb[:, 0:8], sqf[0:8, 0:128], idf[0:8, 0:8]),
                 reads=[sqfr, idfr], writes=[pbr], tag=f"l3f_{nm}t{l}")
            P.op("vector", lambda e, which=which, pb=pb: e.tensor_copy(
                gcols[:, which * 8:(which + 1) * 8], pb[:, 0:8]),
                reads=[pbr], writes=[gcolsr], tag=f"l3f_{nm}e{l}")
        for n0 in range(0, S, 512):
            pb, pbr = C.pb[1], C.pbr[1]

            def gsum(e, n0=n0, pb=pb):
                last = None
                for c in range(NC):
                    last = e.matmul(pb[0:1, :], C.onecol_bf[:, 0:1],
                                    hfm[:, c * S + n0: c * S + n0 + 512],
                                    start=(c == 0), stop=(c == NC - 1))
                return last
            P.op("tensor", gsum, reads=hfmr + [C.hcbr], writes=[pbr], tag=f"l3f_su{l}_{n0}")
            P.op("vector", lambda e, n0=n0, pb=pb: e.tensor_scalar_mul(
                sqf[0:1, n0:n0 + 512], pb[0:1, :], 1.0 / 1024),
                reads=[pbr], writes=[sqfr], tag=f"l3f_sue{l}_{n0}")
        sq512, sq512r = A.take(F32, 512, "l3f_sq512")
        onef, onefr = A.take(F32, 1, "l3f_onef")
        P.op("vector", lambda e: e.memset(onef[:, :], 1.0), writes=[onefr], tag=f"l3f_of{l}")
        for n0 in range(0, S, 512):
            pb, pbr = C.pb[1], C.pbr[1]
            for c in range(NC):
                P.op("scalar", lambda e, c=c, n0=n0: e.activation(
                    sq512[:, :], hfm[:, c * S + n0: c * S + n0 + 512], AF.Square),
                    reads=[hfmr[c]], writes=[sq512r], tag=f"l3f_q2{l}_{c}_{n0}")
                P.op("tensor", lambda e, c=c, pb=pb: e.matmul(
                    pb[0:1, :], onef[:, 0:1], sq512[:, :],
                    start=(c == 0), stop=(c == NC - 1), skip_group_check=True),
                    reads=[sq512r, onefr], writes=[pbr], tag=f"l3f_q2m{l}_{c}_{n0}")
            P.op("vector", lambda e, n0=n0, pb=pb: e.tensor_scalar_mul(
                vrf[0:1, n0:n0 + 512], pb[0:1, :], 1.0 / 1024),
                reads=[pbr], writes=[vrfr], tag=f"l3f_q2e{l}_{n0}")
        P.op("vector", lambda e: e.tensor_tensor(scf[0:1, 0:S], sqf[0:1, 0:S], sqf[0:1, 0:S], ALU.mult),
             reads=[sqfr], writes=[scfr], tag=f"l3f_mm{l}")
        P.op("vector", lambda e: e.tensor_tensor(vrf[0:1, 0:S], vrf[0:1, 0:S], scf[0:1, 0:S], ALU.subtract),
             reads=[vrfr, scfr], writes=[vrfr], tag=f"l3f_vv{l}")
        P.op("scalar", lambda e: e.activation(vrf[0:1, 0:S], vrf[0:1, 0:S], AF.Ln, bias=cpe[0:1, :]),
             reads=[vrfr, P.consts_region], writes=[vrfr], tag=f"l3f_vl{l}")
        P.op("scalar", lambda e: e.activation(vrf[0:1, 0:S], vrf[0:1, 0:S], AF.Exp, scale=nh[0:1, :]),
             reads=[vrfr, P.consts_region], writes=[vrfr], tag=f"l3f_vr{l}")
        P.op("vector", lambda e: e.tensor_copy(mrow[0:1, :], sqf[0:1, 0:S]),
             reads=[sqfr], writes=[mrowr2], tag=f"l3f_mc2{l}")
        P.op("vector", lambda e: e.tensor_copy(rrow[0:1, :], vrf[0:1, 0:S]),
             reads=[vrfr], writes=[rrowr2], tag=f"l3f_rc3{l}")
        for n0 in range(0, S, 512):
            pb, pbr = C.pb[1], C.pbr[1]
            P.op("tensor", lambda e, n0=n0, pb=pb: e.matmul(
                pb[:, :], C.ones_bf[0:1, 0:128], mrow[0:1, n0:n0 + 512], start=True, stop=True),
                reads=[mrowr2, C.hcbr], writes=[pbr], tag=f"l3f_mb{l}_{n0}")
            P.op("scalar", lambda e, n0=n0, pb=pb: e.copy(mbc[:, n0:n0 + 512], pb[:, :]),
                 reads=[pbr], writes=[mbcr], tag=f"l3f_mbe{l}_{n0}")
            P.op("tensor", lambda e, n0=n0, pb=pb: e.matmul(
                pb[:, :], C.ones_bf[0:1, 0:128], rrow[0:1, n0:n0 + 512], start=True, stop=True),
                reads=[rrowr2, C.hcbr], writes=[pbr], tag=f"l3f_rb2{l}_{n0}")
            P.op("scalar", lambda e, n0=n0, pb=pb: e.copy(rbc[:, n0:n0 + 512], pb[:, :]),
                 reads=[pbr], writes=[rbcr], tag=f"l3f_rbe2{l}_{n0}")
        for c in range(NC):
            P.op("vector", lambda e, c=c: e.tensor_tensor(
                hfm[:, c * S:(c + 1) * S], hfm[:, c * S:(c + 1) * S], mbc[:, :], ALU.subtract),
                reads=[hfmr[c], mbcr], writes=[hfmr[c]], tag=f"l3f_a1{l}_{c}")
            P.op("gpsimd", lambda e, c=c: e.tensor_tensor(
                hfm[:, c * S:(c + 1) * S], hfm[:, c * S:(c + 1) * S], rbc[:, :], ALU.mult),
                reads=[hfmr[c], rbcr], writes=[hfmr[c]], tag=f"l3f_a2{l}_{c}")
            P.op("vector", lambda e, c=c: e.tensor_scalar(
                hfm[:, c * S:(c + 1) * S], hfm[:, c * S:(c + 1) * S],
                gcols[:, c:c + 1], gcols[:, 8 + c:8 + c + 1], ALU.mult, ALU.add),
                reads=[hfmr[c], gcolsr], writes=[hfmr[c]], tag=f"l3f_a3{l}_{c}")
        wstg2 = [A.take(BF16, NC * 128, f"l3f_wstg{d}") for d in range(2)]
        b1c2 = [A.take(F32, 1, f"l3f_b1c{d}") for d in range(2)]
        inter, interr = A.take_tiles(BF16, 512, NAC, "l3f_int")
        gsb2, gsb2r = A.take(F32, 512, "l3f_gsb")
        b2row, b2rowr = A.take(BF16, 1024, "l3f_b2")
        w2st2 = [A.take(BF16, 512, f"l3f_w2st{d}") for d in range(2)]
        P.dma("gpsimd", b2row[0:1, :], W["L3_ffn_b2"][l][None], writes=[b2rowr], tag=f"l3f_b2l{l}")
        for tq in range(4):
            t0 = tq * 512
            for part in range(2):
                for kc in range(NAC):
                    mrows = 128 if kc < 21 else 42
                    j0 = part * 2730 + kc * 128
                    d = (part * NAC + kc) % 2
                    wstg, wstgr = wstg2[d]
                    b1c, b1cr = b1c2[d]
                    P.dma("sync" if d == 0 else "scalar",
                          wstg.rearrange("p (c n) -> p c n", c=NC)[:, :, 0:mrows],
                          w1d[l][:, j0:j0 + mrows].rearrange("(c p) n -> p c n", p=128),
                          reads=[w1dr], writes=[wstgr], tag=f"l3f_w1l{l}_{tq}_{part}_{kc}")
                    P.dma("sync", b1c[0:mrows, 0:1],
                          W["L3_ffn_b1"][l][j0:j0 + mrows].rearrange("(n o) -> n o", o=1),
                          writes=[b1cr], tag=f"l3f_b1l{l}_{tq}_{part}_{kc}")
                    bank = 2 + 2 * part + (kc % 2)
                    pb, pbr = C.pb[bank], C.pbr[bank]

                    def g1(e, kc=kc, pb=pb, mrows=mrows, t0=t0, hfm=hfm, wstg=wstg):
                        last = None
                        for c in range(NC):
                            last = e.matmul(pb[0:mrows, :], wstg[:, c * 128:c * 128 + mrows],
                                            hfm[:, c * S + t0: c * S + t0 + 512],
                                            start=(c == 0), stop=(c == NC - 1))
                        return last
                    P.op("tensor", g1, reads=[wstgr] + hfmr,
                         writes=[pbr], tag=f"l3f_w1m{l}_{tq}_{part}_{kc}")
                    if part == 0:
                        P.op("vector", lambda e, kc=kc, pb=pb, mrows=mrows, b1c=b1c: e.tensor_scalar(
                            inter[0:mrows, kc * 512:(kc + 1) * 512], pb[0:mrows, :],
                            b1c[0:mrows, 0:1], None, ALU.add),
                            reads=[pbr, b1cr], writes=[interr[kc]], tag=f"l3f_ae1{l}_{tq}_{kc}")
                    else:
                        P.op("scalar", lambda e, pb=pb, mrows=mrows, b1c=b1c: e.activation(
                            gsb2[0:mrows, :], pb[0:mrows, :], AF.Gelu, bias=b1c[0:mrows, 0:1]),
                            reads=[pbr, b1cr], writes=[gsb2r], tag=f"l3f_ge{l}_{tq}_{kc}")
                        P.op("vector", lambda e, kc=kc, mrows=mrows: e.tensor_tensor(
                            inter[0:mrows, kc * 512:(kc + 1) * 512],
                            inter[0:mrows, kc * 512:(kc + 1) * 512],
                            gsb2[0:mrows, :], ALU.mult),
                            reads=[gsb2r, interr[kc]], writes=[interr[kc]], tag=f"l3f_im{l}_{tq}_{kc}")
            for hh in range(2):
                for kc in range(NAC):
                    mrows = 128 if kc < 21 else 42
                    w2st, w2str = w2st2[kc % 2]
                    P.dma("sync" if kc % 2 == 0 else "scalar",
                          w2st[0:mrows, :], w2d[l][kc * 128:kc * 128 + mrows,
                                                   hh * 512:(hh + 1) * 512],
                          reads=[w2dr], writes=[w2str], tag=f"l3f_w2l{l}_{tq}_{hh}_{kc}")
                    for ti in range(4):
                        pb, pbr = C.pb[2 + ti], C.pbr[2 + ti]
                        P.op("tensor", lambda e, kc=kc, ti=ti, pb=pb, mrows=mrows, w2st=w2st: e.matmul(
                            pb[:, :], inter[0:mrows, kc * 512 + ti * 128: kc * 512 + ti * 128 + 128],
                            w2st[0:mrows, :], start=(kc == 0), stop=False),
                            reads=[w2str] + interr, writes=[pbr], tag=f"l3f_w2m{l}_{tq}_{hh}_{kc}_{ti}")
                for ti in range(4):
                    pb, pbr = C.pb[2 + ti], C.pbr[2 + ti]
                    P.op("tensor", lambda e, ti=ti, pb=pb, hh=hh: e.matmul(
                        pb[:, :], C.ones_bf[0:1, 0:128], b2row[0:1, hh * 512:(hh + 1) * 512],
                        start=False, stop=True),
                        reads=[b2rowr, C.hcbr], writes=[pbr], tag=f"l3f_w2b{l}_{tq}_{hh}_{ti}")
                for ti in range(4):
                    i = tq * 4 + ti
                    pb, pbr = C.pb[2 + ti], C.pbr[2 + ti]
                    ytile, ytiler = ybufs2[ti % 2][0], ybufs2[ti % 2][1]
                    P.dma("sync", ytile[:, 0:512],
                          yd[i][:, hh * 512:(hh + 1) * 512], reads=[ydr[i]], writes=[ytiler],
                          tag=f"l3f_y4l{l}_{tq}_{hh}_{ti}")
                    P.op("vector", lambda e, pb=pb, ytile=ytile: e.tensor_tensor(
                        ytile[:, 0:512], ytile[:, 0:512], pb[:, :], ALU.add),
                        reads=[pbr, ytiler], writes=[ytiler], tag=f"l3f_y4a{l}_{tq}_{hh}_{ti}")
                    P.dma("gpsimd", yd[i][:, hh * 512:(hh + 1) * 512], ytile[:, 0:512],
                          reads=[ytiler], writes=[ydr[i]], tag=f"l3f_y4s{l}_{tq}_{hh}_{ti}")
        A.rewind(m2_)

    ytf, ytfr = A.take(F32, 1024, "l3f_ytf")
    for i in range(NT):
        P.dma("sync", ytf[:, :], yd[i], reads=[ydr[i]], writes=[ytfr], tag=f"l3f_yfl{i}")
        x_ap = C.X[:, i * 1024:(i + 1) * 1024]
        P.op("vector", lambda e, x_ap=x_ap: e.tensor_tensor(x_ap, x_ap, ytf[:, :], ALU.add),
             reads=[ytfr, C.Xr[i]], writes=[C.Xr[i]], tag=f"l3f_xf{i}")
    C.arena.reset()



def build_hconsts():
    cols = {}
    parts = []
    ofs = 0

    def add(name, arr):
        nonlocal ofs
        cols[name] = ofs
        parts.append(arr.astype(ml_dtypes.bfloat16))
        ofs += arr.shape[1]

    add("ident", np.eye(128, dtype=np.float32))
    ones = np.zeros((128, 512), np.float32)
    ones[0, :] = 1.0
    add("ones", ones)
    add("onecol", np.ones((128, 1), np.float32))
    s_idx = np.arange(128)[:, None]
    t_idx = np.arange(128)[None, :]
    add("l2neg", np.where(s_idx <= t_idx, 0.0, -1e30).astype(np.float32))
    for v in range(16):
        M = np.zeros((128, 128), np.float32)
        m_ = np.arange(128)
        M[8 * v + m_ // 16, m_] = 1.0
        add(f"l1sel{v}", M)
    for v in range(4):
        M = np.zeros((128, 32), np.float32)
        k_ = np.arange(128)
        M[k_, 8 * v + k_ // 16] = 1.0
        add(f"l1m2_{v}", M)
    for off in (-384, -256, -128, 0):
        m = np.zeros((128, 512), np.float32)
        s_idx = np.arange(128)[:, None]
        jrel = np.arange(512)[None, :]
        m[:] = (s_idx <= off + jrel)
        add(f"l0m{off}", m)
    rh = np.zeros((128, 64), np.float32)
    for b0 in (0, 64):
        for k_ in range(32):
            rh[b0 + k_, k_ + 32] = 1.0
            rh[b0 + 32 + k_, k_] = -1.0
    add("l3frh", rh)
    cols["_total"] = ofs
    return cols, np.concatenate(parts, axis=1)



BLOCK_EMITTERS = {}
LAST_PROG = None
CC_GROUP = list(range(NCORES))


def build_program(block_names, weight_names, hc_layout, hc_total):
    nc = bass.Bass("TRN2", target_bir_lowering=False)
    x_in = nc.dram_tensor("x_in", [S, D], F32, kind="ExternalInput").ap()
    out = nc.dram_tensor("out", [S, D], F32, kind="ExternalOutput").ap()
    hcb = nc.dram_tensor("hcb", [128, hc_total], BF16, kind="ExternalInput").ap()
    W = {}
    DT = {"float32": F32, "uint8": mybir.dt.uint8}
    for name, shape, dt in weight_names:
        W[name] = nc.dram_tensor(name, list(shape), DT[dt], kind="ExternalInput").ap()
    global LAST_PROG
    with ExitStack() as ctx:
        P = Prog(nc, ctx)
        LAST_PROG = P
        C = Core(P, hcb, hc_layout)
        emit_load_x(C, x_in)
        for b in block_names:
            BLOCK_EMITTERS[b](C, W)
        emit_store_x(C, out)
        tot = P.total["dgp"]
        P.ops["gpsimd"].append(dict(
            fn=lambda e: e.wait_ge(P.sems["dgp"], tot) and None,
            waits={}, sem="pool", inc=0, cnt=P.total["pool"], tag="final"))
        P.finalize()
    return nc


BLOCK_EMITTERS["l0_attn"] = emit_l0_attn
BLOCK_EMITTERS["l2_ffn"] = emit_l2_ffn
BLOCK_EMITTERS["l3_attn"] = emit_l3_attn
BLOCK_EMITTERS["l1_ffn"] = emit_l1_ffn
BLOCK_EMITTERS["l0_ffn"] = emit_l0_ffn
BLOCK_EMITTERS["l2_attn"] = emit_l2_attn
BLOCK_EMITTERS["l1_attn"] = emit_l1_attn
BLOCK_EMITTERS["l3_ffn"] = emit_l3_ffn

WEIGHTS_BY_BLOCK = {
    "l0_attn": [(n, sh, "float32") for n, sh in
                [("L0_attn_Wq", (1024, 1024)), ("L0_attn_bq", (1024,)),
                 ("L0_attn_Wk", (1024, 1024)), ("L0_attn_bk", (1024,)),
                 ("L0_attn_Wv", (1024, 1024)), ("L0_attn_bv", (1024,)),
                 ("L0_attn_Wm", (1024, 1024)), ("L0_attn_bm", (1024,))]],
    "l3_attn": [(n, sh, "float32") for n, sh in
                [("L3_attn_Wq", (1024, 1024)), ("L3_attn_Wk", (1024, 1024)),
                 ("L3_attn_Wv", (1024, 1024)), ("L3_attn_Wo", (1024, 1024)),
                 ("L3_attn_bo", (1024,))]],
    "l0_ffn": [(n, sh, "float32") for n, sh in
               [("L0_ffn_clip_x", (1,)), ("L0_ffn_clip_down", (1,)),
                ("L0_ffn_w_gate", (2816, 1024)), ("L0_ffn_w_up", (2816, 1024)),
                ("L0_ffn_w_down", (1024, 2816))]],
    "l1_ffn": [(n, sh, "float32") for n, sh in
               [("L1_ffn_gate_w", (8, 1024)), ("L1_ffn_Wg", (8, 1024, 512)),
                ("L1_ffn_Wu", (8, 1024, 512)), ("L1_ffn_Wd", (8, 512, 1024)),
                ("L1_ffn_sWg", (1024, 512)), ("L1_ffn_sWu", (1024, 512)),
                ("L1_ffn_sWd", (512, 1024))]],
    "l3_ffn": [(n, sh, "float32") for n, sh in
               [("L3_ffn_wqkv", (2, 1024, 1536)), ("L3_ffn_q_scale", (2, 64)),
                ("L3_ffn_k_scale", (2, 64)), ("L3_ffn_wo", (2, 512, 1024)),
                ("L3_ffn_ln_g", (2, 1024)), ("L3_ffn_ln_b", (2, 1024)),
                ("L3_ffn_w1", (2, 1024, 5460)), ("L3_ffn_b1", (2, 5460)),
                ("L3_ffn_w2", (2, 2730, 1024)), ("L3_ffn_b2", (2, 1024))]],
    "l1_attn": [(n, sh, "float32") for n, sh in
                [("L1_attn_A_log", (1024, 16)), ("L1_attn_D", (1024,)),
                 ("L1_attn_x_proj_w", (96, 1024)), ("L1_attn_dt_proj_w", (1024, 64)),
                 ("L1_attn_dt_proj_b", (1024,))]],
    "l2_attn": [(n, sh, "float32") for n, sh in
                [("L2_attn_Wx", (2048, 1024)), ("L2_attn_Wz", (2048, 1024)),
                 ("L2_attn_conv_w", (2048, 1, 4)), ("L2_attn_conv_b", (2048,)),
                 ("L2_attn_Wb", (512, 2048)), ("L2_attn_Wc", (512, 2048)),
                 ("L2_attn_Wdt", (8, 2048)), ("L2_attn_b_dt", (8,)),
                 ("L2_attn_log_a", (8,)), ("L2_attn_d_param", (8,)),
                 ("L2_attn_gn_w", (2048,)), ("L2_attn_gn_b", (2048,)),
                 ("L2_attn_Wout", (1024, 2048))]],
    "l2_ffn": [("L2_ffn_k0", (2048, 4096), "uint8"), ("L2_ffn_k1", (4096, 4096), "uint8"),
               ("L2_ffn_k2", (4096, 1024), "uint8"), ("L2_ffn_s0", (1,), "float32"),
               ("L2_ffn_s1", (1,), "float32"), ("L2_ffn_s2", (1,), "float32")],
}

LAUNCHES = [["l0_attn", "l0_ffn", "l1_attn", "l1_ffn",
             "l2_attn", "l2_ffn", "l3_attn", "l3_ffn"]]


def build_l3f_rot():
    DH, WIN, n = 64, 512, S
    half = np.arange(0, DH, 2, dtype=np.float64)
    inv_freq = 1.0 / 10000.0 ** (half / DH)
    t = np.arange(n, dtype=np.float64)
    freqs = t[:, None] * inv_freq[None, :]
    freqs = np.concatenate([freqs, freqs], axis=1)
    base = (half + 0.4 * DH) / (1.4 * DH)
    power = (t - n // 2) / float(WIN // 2)
    scale = base[None, :] ** power[:, None]
    scale = np.concatenate([scale, scale], axis=1)
    cos, sin = np.cos(freqs), np.sin(freqs)
    tab = np.zeros((128, 2 * n), np.float32)
    tab[0:64, 0:n] = (cos * scale).T
    tab[0:64, n:2 * n] = (sin * scale).T
    tab[64:128, 0:n] = (cos / scale).T
    tab[64:128, n:2 * n] = (sin / scale).T
    return tab


def kernel(**inputs):
    hc_layout, hc_data = build_hconsts()
    x = np.ascontiguousarray(inputs["L0_attn_x"], dtype=np.float32)
    cur = [x[i] for i in range(NCORES)]
    for blocks in LAUNCHES:
        wnames = []
        for b in blocks:
            wnames += WEIGHTS_BY_BLOCK[b]
        nc = build_program(blocks, wnames, hc_layout, hc_layout["_total"])
        rot = build_l3f_rot() if "l3_ffn" in blocks else None
        in_maps = []
        for i in range(NCORES):
            m = {"x_in": cur[i], "hcb": hc_data}
            if rot is not None:
                m["l3f_rot"] = rot
            for name, shape, dt in wnames:
                v = inputs[name]
                m[name] = np.ascontiguousarray(np.reshape(v, shape),
                                               np.uint8 if dt == "uint8" else np.float32)
            in_maps.append(m)
        res = run_bass_kernel_spmd(nc, in_maps, core_ids=list(range(NCORES)))
        cur = [res.results[i]["out"] for i in range(NCORES)]
    return np.stack(cur, axis=0)
